# version 21
# baseline (speedup 1.0000x reference)
import contextlib

import numpy as np

import concourse.bass as bass
import concourse.mybir as mybir
from concourse.bass_utils import run_bass_kernel_spmd

F32 = mybir.dt.float32
F16 = mybir.dt.float16
U8 = mybir.dt.uint8
ALU = mybir.AluOpType
AF = mybir.ActivationFunctionType
AX = mybir.AxisListType

NCORE = 8
D = 1024
S = 2048
NSEQ = 2
T = NSEQ * S
NT = T // 128
NTS = S // 128
NH = 16
HD = 64
NCH = 8
NE = 8
NQG = 4
EPS16 = float(np.float16(1e-5))
VW = 66
VF = NH * VW


class Prog:

    def __init__(self, nc):
        self.nc = nc
        self.ops = []
        self.sems = {}
        self.cnt = {}
        self.wm = {}

    def decl(self, ctx, *names):
        for n in names:
            self.sems[n] = ctx.enter_context(self.nc.semaphore(n))
            self.cnt[n] = 0

    def emit(self, eng, fn, waits=(), inc=None, amt=1):
        ws = []
        for (sn, v) in waits:
            if v is None or v <= 0:
                continue
            k = (eng, sn)
            if self.wm.get(k, 0) >= v:
                continue
            self.wm[k] = v
            ws.append((sn, v))
        incs = []
        if inc is not None:
            self.cnt[inc] += amt
            incs.append((inc, amt))
        self.ops.append((eng, ws, fn, incs))
        return self.cnt[inc] if inc is not None else None

    def barrier(self):
        engs = ("vector", "scalar", "tensor", "gpsimd")
        marks = {}
        for e in engs:
            sn = "bar_" + e
            self.cnt[sn] += 1
            self.ops.append((e, [], None, [(sn, 1)]))
            marks[e] = self.cnt[sn]
        for e in engs:
            self.emit(e, None,
                      waits=[("bar_" + o, marks[o]) for o in engs if o != e])
        self.last_bar = [("bar_" + o, marks[o]) for o in engs]

    def bw(self):
        return list(getattr(self, "last_bar", []))

    def replay(self, blk):
        sems, ops = self.sems, self.ops

        def runner(name):
            def body(eng):
                for (e, ws, fn, incs) in ops:
                    if e != name:
                        continue
                    for (sn, v) in ws:
                        eng.wait_ge(sems[sn], v)
                    if fn is None:
                        if incs:
                            ins = eng.drain()
                            for (sn, a) in incs:
                                ins.then_inc(sems[sn], a)
                        continue
                    ins = fn(eng)
                    for (sn, a) in incs:
                        ins.then_inc(sems[sn], a)
            return body

        blk.vector(runner("vector"))
        blk.scalar(runner("scalar"))
        blk.tensor(runner("tensor"))
        blk.gpsimd(runner("gpsimd"))
        blk.sync(runner("sync"))


def build_nc():
    nc = bass.Bass("TRN2", target_bir_lowering=False)

    x_h = nc.dram_tensor("x_c", [T, D], F16, kind="ExternalInput").ap()
    qw_h = nc.dram_tensor("qw", [D, D], F16, kind="ExternalInput").ap()
    kw_h = nc.dram_tensor("kw", [D, D], F16, kind="ExternalInput").ap()
    vw_h = nc.dram_tensor("vw", [D, D], F16, kind="ExternalInput").ap()
    ow_h = nc.dram_tensor("ow", [D, D], F16, kind="ExternalInput").ap()
    wg_h = nc.dram_tensor("wg", [NE, D], F32, kind="ExternalInput").ap()
    we_h = nc.dram_tensor("we", [NE, D, D], F32, kind="ExternalInput").ap()
    be_h = nc.dram_tensor("be", [NE, D], F32, kind="ExternalInput").ap()
    cos_h = nc.dram_tensor("c_cos", [S, 512], F16, kind="ExternalInput").ap()
    sin_h = nc.dram_tensor("c_sin", [S, 512], F16, kind="ExternalInput").ap()
    tri_h = nc.dram_tensor("c_tri", [128, 128], F16, kind="ExternalInput").ap()
    idn_h = nc.dram_tensor("c_idn", [128, 128], F16, kind="ExternalInput").ap()
    out_h = nc.dram_tensor("out_c", [T, D], F32, kind="ExternalOutput").ap()

    xn_sp = nc.dram_tensor("xn_sp", [T, D], F16, kind="Internal").ap()
    q_sp = nc.dram_tensor("q_sp", [NSEQ, S, D], F16, kind="Internal").ap()
    k_sp = nc.dram_tensor("k_sp", [NSEQ, S, D], F16, kind="Internal").ap()
    v_sp = nc.dram_tensor("v_sp", [NSEQ, S, VF], F16, kind="Internal").ap()
    g_sp = nc.dram_tensor("g_sp", [T, D], F16, kind="Internal").ap()
    we16 = nc.dram_tensor("we16", [NE, D, D], F16, kind="Internal").ap()
    xnTs_h = nc.dram_tensor("xnTs", [128, NCH * T], F16, kind="Internal").ap()
    weTsp_h = nc.dram_tensor("weTsp", [NE, 128, NCH * D], F16,
                             kind="Internal").ap()

    ctx = contextlib.ExitStack()
    with ctx:
        sbt = lambda name, shape, dt: ctx.enter_context(
            nc.sbuf_tensor(name, shape, dt))[:]
        AR1 = sbt("AR1", [128, 65536], U8)
        AR2 = sbt("AR2", [128, 125952], U8)

        def c16(ar, off, cols):
            return ar[:, off:off + 2 * cols].bitcast(F16)

        def c32(ar, off, cols):
            return ar[:, off:off + 4 * cols].bitcast(F32)

        xnT = c16(AR1, 0, 32768)
        kT = c16(AR1, 0, 16384)
        qTb = c16(AR1, 32768, 4096)
        attnT = c16(AR1, 40960, 4096)
        ow128 = c16(AR1, 49152, 8192)

        xst = [c16(AR2, 92672 + 2048 * i, 1024) for i in range(4)]
        xcn16 = [c16(AR2, 100864 + 2048 * i, 1024) for i in range(4)]
        sqd16 = c16(AR2, 109056, 1024)
        lgall = c32(AR2, 111104, 256)
        l2a = c32(AR2, 112128, 256)
        m32 = [c32(AR2, 111104 + 32 * i, 1) for i in range(4)]
        m16 = [c16(AR2, 111232 + 32 * i, 1) for i in range(4)]
        mneg = [c32(AR2, 111360 + 32 * i, 1) for i in range(4)]
        ssum = [c32(AR2, 111488 + 32 * i, 1) for i in range(4)]
        v16t = [c16(AR2, 111616 + 32 * i, 1) for i in range(4)]
        s16t = [c16(AR2, 111744 + 32 * i, 1) for i in range(4)]
        r16t = [c16(AR2, 111872 + 32 * i, 1) for i in range(4)]
        rc32 = [c32(AR2, 112000 + 32 * i, 1) for i in range(4)]
        qwS = c16(AR2, 0, 8192)
        kwS = c16(AR2, 16384, 8192)
        vwS = c16(AR2, 32768, 8192)
        cosb = [c16(AR2, 49152 + 2048 * i, 1024) for i in range(4)]
        qt_st = [c16(AR2, 57344 + 2048 * i, 1024) for i in range(4)]
        kt_st = [c16(AR2, 65536 + 2048 * i, 1024) for i in range(4)]
        vt_st = [c16(AR2, 73728, VF), c16(AR2, 76032, VF)]
        qr_st = [c16(AR2, 78336, 1024), c16(AR2, 80384, 1024)]
        kr_st = [c16(AR2, 82432, 1024), c16(AR2, 84480, 1024)]
        rt = [c16(AR2, 86528 + 1024 * i, 512) for i in range(4)]
        pt = [c16(AR2, 90624 + 1024 * i, 512) for i in range(2)]
        wgf = c32(AR2, 78336, 1024)
        wgr = c32(AR2, 82432, 1024)
        dst = c32(AR2, 113152, 32)
        edt = c32(AR2, 113280, 32)
        dent = c32(AR2, 113408, 32)
        w1t = c32(AR2, 113536, 32)
        w2t = c32(AR2, 113664, 32)
        w132 = c32(AR2, 113792, 32)
        w232 = c32(AR2, 113920, 32)
        v1a = c32(AR2, 114048, 32)
        v2a = c32(AR2, 114176, 32)
        w116 = c16(AR2, 114304, 32)
        w216 = c16(AR2, 114368, 32)
        tmp8 = c32(AR2, 114432, 8)
        eq16 = c16(AR2, 114464, 512)
        bnc = [c16(AR2, 115712 + 4096 * i, 2048) for i in range(2)]
        v_sb = c16(AR2, 0, 16896)
        weT = [c16(AR2, 33792 + 16384 * i, 8192) for i in range(2)]
        xnq = [c16(AR2, 66560 + 8192 * i, 4096) for i in range(2)]
        acc32 = [c32(AR2, 82944 + 4096 * i, 1024) for i in range(4)]
        gsl = [c16(AR2, 99328 + 8192 * i, 4096) for i in range(2)]
        expb3 = [c16(AR2, 115712 + 2048 * i, 1024) for i in range(3)]
        at2 = c32(AR2, 121856, 1024)
        recL = c32(AR2, 121856, 1024)
        recE = c16(AR2, 121856, 2048)

        tri = sbt("tri", [128, 128], F16)
        idn = sbt("idn", [128, 128], F16)
        ones16 = sbt("ones16", [128, 64], F16)

        wm32 = sbt("wm32", [128, NT * 8], F32)
        wm16 = sbt("wm16", [128, NT * 8], F16)
        wmT = sbt("wmT", [8, NT * 128], F16)
        wgT = sbt("wgT", [128, NCH * 24], F16)
        wgs = sbt("wgs", [8, 3 * D], F16)
        be16s = sbt("be16s", [8, D], F16)
        xq2 = [sbt("xq2_%d" % i, [128, 512], F16) for i in range(2)]

        PA = ctx.enter_context(nc.psum_tensor("PA", [128, 1024], F32))[:]
        PB = ctx.enter_context(nc.psum_tensor("PB", [128, 1024], F32))[:]
        PV2 = ctx.enter_context(nc.psum_tensor("PV2", [128, 1024], F32))[:]
        PC6 = ctx.enter_context(nc.psum_tensor("PC6", [128, 512], F32))[:]
        PC7 = ctx.enter_context(nc.psum_tensor("PC7", [128, 512], F32))[:]
        pb7f16 = PC7.bitcast(F16)

        P = Prog(nc)
        P.decl(ctx, "eV", "eA", "eT", "eP",
               "bar_vector", "bar_scalar", "bar_tensor", "bar_gpsimd",
               "sxA", "sxB", "sxC", "sxD", "sspA", "sspB", "sspC", "sspD",
               "sxnT", "scst",
               "scsA", "scsB", "scsC", "scsD", "sw3", "sqsA", "sqsB", "sksA", "sksB",
               "svsA", "svsB", "swe", "sweA", "sweB", "sbe", "sgsA", "sgsB",
               "skT", "svld", "sqTA", "sqTB", "sgq", "sxq", "soA", "soB",
               "sow", "swb", "swc")
        W = P.emit
        C = P.cnt

        def dma(q, out, in_, sem, transpose=False, waits=()):
            def fn(e, out=out, in_=in_, tr=transpose):
                if tr:
                    return e.dma_start(out, in_, transpose=True)
                return e.dma_start(out, in_)
            return W(q, fn, waits=waits, inc=sem, amt=16)

        tok = {}

        dma("sync", qwS[:, :].rearrange("k (c n) -> k c n", c=NCH),
            qw_h.rearrange("(c k) n -> k c n", k=128), "sw3")
        dma("sync", kwS[:, :].rearrange("k (c n) -> k c n", c=NCH),
            kw_h.rearrange("(c k) n -> k c n", k=128), "sw3")
        dma("sync", vwS[:, :].rearrange("k (c n) -> k c n", c=NCH),
            vw_h.rearrange("(c k) n -> k c n", k=128), "sw3")
        dma("sync", tri[:, :], tri_h[:, :], "scst")
        dma("sync", idn[:, :], idn_h[:, :], "scst")
        dma("sync", wgf[0:8, :], wg_h[:, :], "scst")
        cst = C["scst"]
        W("vector", lambda e: e.memset(ones16[:, :], 1.0), inc="eV")

        W("vector", lambda e: e.tensor_copy(wgs[0:8, 0:D], wgf[0:8, :]),
          waits=[("scst", cst)], inc="eV")
        W("vector", lambda e: e.tensor_sub(wgr[0:8, :], wgf[0:8, :],
                                           wgs[0:8, 0:D]),
          waits=[("eV", C["eV"])], inc="eV")
        W("vector", lambda e: e.tensor_copy(wgs[0:8, D:2 * D], wgr[0:8, :]),
          waits=[("eV", C["eV"])], inc="eV")
        W("vector", lambda e: e.tensor_sub(wgr[0:8, :], wgr[0:8, :],
                                           wgs[0:8, D:2 * D]),
          waits=[("eV", C["eV"])], inc="eV")
        W("vector", lambda e: e.tensor_copy(wgs[0:8, 2 * D:3 * D], wgr[0:8, :]),
          waits=[("eV", C["eV"])], inc="eV")
        wg_split = C["eV"]
        for c in range(NCH):
            for s3 in range(3):
                W("tensor",
                  lambda e, c=c, s3=s3: e.transpose(
                      pb7f16[0:128, 0:8],
                      wgs[0:8, s3 * D + c * 128:s3 * D + (c + 1) * 128],
                      idn[0:8, 0:8]),
                  waits=[("eV", wg_split), ("scst", cst), ("eA", C["eA"])],
                  inc="eT")
                W("scalar",
                  lambda e, c=c, s3=s3: e.copy(
                      wgT[:, c * 24 + s3 * 8:c * 24 + s3 * 8 + 8],
                      pb7f16[0:128, 0:8]),
                  waits=[("eT", C["eT"])], inc="eA")
        wgT_done = C["eA"]
        wgT_tp = C["eT"]
        w3 = C["sw3"]

        SSP = ("sspA", "sspB", "sspC", "sspD")
        SX = ("sxA", "sxB", "sxC", "sxD")

        def ln_s0(t):
            q = t % 4
            dma("sync", xst[q][:, :], x_h[t * 128:(t + 1) * 128, :], SX[q],
                waits=[("eA", tok.get(("xcd", q), 0))])

        def ln_s1(t):
            q = t % 4
            W("vector", lambda e, q=q: e.reduce_sum(m32[q][:, :],
                                                    xst[q][:, :], axis=AX.X),
              waits=[(SX[q], C[SX[q]]), ("eV", C["eV"])],
              inc="eV")
            W("vector", lambda e, q=q: e.tensor_scalar(
                m16[q][:, :], m32[q][:, :], 1.0 / D, None, op0=ALU.mult),
              waits=[("eV", C["eV"])], inc="eV")
            W("vector", lambda e, q=q: e.tensor_scalar(
                mneg[q][:, :], m16[q][:, :], -1.0, None, op0=ALU.mult),
              waits=[("eV", C["eV"])], inc="eV")
            tok[("mnegd", q)] = C["eV"]

        def ln_s2(t):
            q = t % 4
            W("scalar", lambda e, q=q: e.activation(
                xcn16[q][:, :], xst[q][:, :], AF.Identity,
                bias=mneg[q][:, :], scale=1.0),
              waits=[("eV", tok[("mnegd", q)]),
                     ("eT", tok.get(("xnrd", q), 0))], inc="eA")
            tok[("xcd", q)] = C["eA"]
            W("scalar", lambda e, q=q: e.activation(
                sqd16[:, :], xcn16[q][:, :], AF.Square,
                accum_out=ssum[q][:, :]),
              waits=[("eA", C["eA"]), ("eV", tok.get(("vrd", q), 0))],
              inc="eA")
            tok[("sqdone", q)] = C["eA"]

        def ln_s3(t):
            q = t % 4
            W("vector", lambda e, q=q: e.tensor_scalar(
                v16t[q][:, :], ssum[q][:, :], 1.0 / D, None, op0=ALU.mult),
              waits=[("eA", tok[("sqdone", q)]), ("eV", C["eV"])], inc="eV")
            tok[("vrd", q)] = C["eV"]
            W("vector", lambda e, q=q: e.tensor_scalar(
                s16t[q][:, :], v16t[q][:, :], EPS16, None, op0=ALU.add),
              waits=[("eV", C["eV"])], inc="eV")
            tok[("s16d", q)] = C["eV"]

        def ln_s4(t):
            q = t % 4
            W("scalar", lambda e, q=q: e.activation(r16t[q][:, :],
                                                    s16t[q][:, :], AF.Sqrt),
              waits=[("eV", tok[("s16d", q)]), ("eA", C["eA"])], inc="eA")
            tok[("sqrtd", q)] = C["eA"]

        def ln_s5(t):
            q = t % 4
            W("vector", lambda e, q=q: e.reciprocal(rc32[q][:, :],
                                                    r16t[q][:, :]),
              waits=[("eA", tok[("sqrtd", q)]), ("eV", C["eV"])], inc="eV")
            W("vector", lambda e, q=q: e.tensor_scalar(
                xcn16[q][:, :], xcn16[q][:, :], rc32[q][:, :], None,
                op0=ALU.mult),
              waits=[("eV", C["eV"]), ("eA", tok[("sqdone", q)])], inc="eV")
            tok[("xc16", q)] = C["eV"]

        PV2f16 = PV2.bitcast(F16)
        tp16 = [PV2f16[:, 0:128], PV2f16[:, 1024:1152]]

        def ln_s6(t):
            q = t % 4
            for c in range(NCH):
                bp = c % 2
                W("tensor", lambda e, q=q, c=c, bp=bp: e.transpose(
                    tp16[bp][:, :], xcn16[q][:, c * 128:(c + 1) * 128],
                    idn[:, :]),
                  waits=[("eV", tok[("xc16", q)]), ("scst", cst),
                         ("eV", tok.get(("tpev", bp), 0)),
                         ("eA", tok.get(("tpea", bp), 0))], inc="eT")
                eng = "vector" if c % 4 < 2 else "scalar"
                def ef(e, q=q, c=c, bp=bp, t=t):
                    dst = xnT[:, c * T + t * 128:c * T + (t + 1) * 128]
                    if hasattr(e, "activation"):
                        return e.copy(dst, tp16[bp][:, :])
                    return e.tensor_copy(dst, tp16[bp][:, :])
                W(eng, ef, waits=[("eT", C["eT"])],
                  inc="eV" if eng == "vector" else "eA")
                tok[("tpev" if eng == "vector" else "tpea", bp)] = \
                    C["eV" if eng == "vector" else "eA"]
            tok[("xnrd", q)] = C["eT"]
            tok[("xnTv", t)] = C["eV"]
            tok[("xnTa", t)] = C["eA"]


        SCS = ("scsA", "scsB", "scsC", "scsD")

        def p1a(gt):
            s, t = gt // NTS, gt % NTS
            p = gt % 2
            q4 = gt % 4
            scs = SCS[q4]
            dma("scalar", cosb[q4][:, 0:512], cos_h[t * 128:(t + 1) * 128, :],
                scs, waits=[("eV", tok.get(("ropeqt", gt - 4), 0)),
                            ("eP", tok.get(("ropekt", gt - 4), 0))])
            dma("scalar", cosb[q4][:, 512:1024],
                sin_h[t * 128:(t + 1) * 128, :], scs)
            tok[("csr", gt)] = C[scs]
            sv = "svsA" if p == 0 else "svsB"
            vv = vt_st[p].rearrange("q (h w) -> q h w", h=NH)
            for grp in range(2):
                for c in range(NCH):
                    qv = qwS[:, c * D:(c + 1) * D].rearrange(
                        "k (h i two) -> k h two i", two=2, i=32)
                    kv = kwS[:, c * D:(c + 1) * D].rearrange(
                        "k (h i two) -> k h two i", two=2, i=32)
                    lhs = xnT[:, c * T + gt * 128:c * T + (gt + 1) * 128]
                    hs = slice(8 * grp, 8 * grp + 8)
                    PG = PA if grp == 0 else PB
                    PVg = PC6 if grp == 0 else PC7
                    rhss = [qv[:, hs, :, :], kv[:, hs, :, :],
                            vwS[:, c * D + 512 * grp:
                                c * D + 512 * grp + 512]]
                    outs_ = [PG[:, 0:512], PG[:, 512:1024], PVg[:, :]]
                    for bi3 in range(3):
                        wts = [("sw3", w3), ("eV", tok[("xnTv", gt)]),
                               ("eA", tok[("xnTa", gt)])]
                        if c == 0:
                            wts.append(("eA", tok.get(("pjev", grp), 0)))
                            if grp == 1 and bi3 == 2:
                                wts.append(("eA", wgT_done))
                        W("tensor",
                          lambda e, o_=outs_[bi3], lhs=lhs,
                          rhs=rhss[bi3], c=c:
                          e.matmul(o_, lhs, rhs,
                                   start=(c == 0), stop=(c == NCH - 1)),
                          waits=wts, inc="eT")
                mm = C["eT"]
                PG = PA if grp == 0 else PB
                PVg = PC6 if grp == 0 else PC7
                W("scalar", lambda e, q4=q4, grp=grp, PG=PG: e.copy(
                    qt_st[q4][:, 512 * grp:512 * grp + 512],
                    PG[:, 0:512]),
                  waits=[("eT", mm),
                         ("eV", tok.get(("ropeqt", gt - 4), 0)),
                         ("eP", tok.get(("ropekt", gt - 4), 0))], inc="eA")
                W("scalar", lambda e, q4=q4, grp=grp, PG=PG: e.copy(
                    kt_st[q4][:, 512 * grp:512 * grp + 512],
                    PG[:, 512:1024]), inc="eA")
                W("scalar", lambda e, vv=vv, grp=grp, PVg=PVg: e.copy(
                    vv[:, 8 * grp:8 * grp + 8, 0:64],
                    PVg[:, :].rearrange("q (h d) -> q h d", h=8)),
                  waits=[(sv, 16 * (gt // 2))], inc="eA")
                tok[("pjev", grp)] = C["eA"]
            if gt < 2:
                W("vector", lambda e, vv=vv: e.memset(vv[:, :, 64:65],
                                                      1.0), inc="eV")
                W("vector", lambda e, vv=vv: e.memset(vv[:, :, 65:66],
                                                      0.0), inc="eV")
                tok[("vones", p)] = C["eV"]
            tok[("pjevt", gt)] = C["eA"]
            dma("gpsimd", v_sp[s, t * 128:(t + 1) * 128, :],
                vt_st[p][:, :], sv,
                waits=[("eA", C["eA"]), ("eV", tok.get(("vones", p), 0))])

        def p1b(gt):
            s, t = gt // NTS, gt % NTS
            p = gt % 2
            q4 = gt % 4
            pjev = tok[("pjevt", gt)]
            csr = tok[("csr", gt)]
            scs = SCS[q4]
            qe = qr_st[p].rearrange("q (h two i) -> q h two i", two=2, i=32)
            se_ = qt_st[q4].rearrange("q (h two i) -> q h two i", two=2, i=32)
            cs_ = cosb[q4][:, 0:512].rearrange("q (h i) -> q h i", h=NH)
            sn_ = cosb[q4][:, 512:1024].rearrange("q (h i) -> q h i", h=NH)
            rh = lambda a: a.rearrange("q (h i) -> q h i", h=NH)
            sq_ = "sqsA" if p == 0 else "sqsB"
            W("vector", lambda e, se_=se_, cs_=cs_: e.tensor_mul(
                rh(rt[0][:, :]), se_[:, :, 0, :], cs_),
              waits=[("eA", pjev), (scs, csr), ("eV", wg_split),
                     ("eT", wgT_tp),
                     ("eV", tok.get(("ropeqt", gt - 1), 0))], inc="eV")
            W("vector", lambda e, se_=se_, sn_=sn_: e.tensor_mul(
                rh(rt[1][:, :]), se_[:, :, 1, :], sn_), inc="eV")
            W("vector", lambda e, se_=se_, sn_=sn_: e.tensor_mul(
                rh(rt[2][:, :]), se_[:, :, 0, :], sn_), inc="eV")
            W("vector", lambda e, se_=se_, cs_=cs_: e.tensor_mul(
                rh(rt[3][:, :]), se_[:, :, 1, :], cs_), inc="eV")
            mv = C["eV"]
            W("vector", lambda e, qe=qe: e.tensor_sub(
                qe[:, :, 0, :], rh(rt[0][:, :]), rh(rt[1][:, :])),
              waits=[("eV", mv), (sq_, 16 * (gt // 2))], inc="eV")
            W("vector", lambda e, qe=qe: e.tensor_add(
                qe[:, :, 1, :], rh(rt[2][:, :]), rh(rt[3][:, :])),
              inc="eV")
            ropeq = C["eV"]
            tok[("ropeqt", gt)] = ropeq
            ke = kr_st[p].rearrange("q (h two i) -> q h two i", two=2, i=32)
            ks_ = kt_st[q4].rearrange("q (h two i) -> q h two i", two=2, i=32)
            sk_ = "sksA" if p == 0 else "sksB"
            W("gpsimd", lambda e, ks_=ks_, cs_=cs_: e.tensor_mul(
                rh(pt[0][:, :]), ks_[:, :, 0, :], cs_),
              waits=[("eA", pjev), (scs, csr), ("eV", wg_split),
                     ("eT", wgT_tp),
                     ("eP", tok.get(("ropekt", gt - 1), 0))], inc="eP")
            W("gpsimd", lambda e, ks_=ks_, sn_=sn_: e.tensor_mul(
                rh(pt[1][:, :]), ks_[:, :, 1, :], sn_), inc="eP")
            mp = C["eP"]
            W("gpsimd", lambda e, ke=ke: e.tensor_sub(
                ke[:, :, 0, :], rh(pt[0][:, :]), rh(pt[1][:, :])),
              waits=[("eP", mp), (sk_, 16 * (gt // 2))], inc="eP")
            W("gpsimd", lambda e, ks_=ks_, sn_=sn_: e.tensor_mul(
                rh(pt[0][:, :]), ks_[:, :, 0, :], sn_),
              waits=[("eP", C["eP"])], inc="eP")
            W("gpsimd", lambda e, ks_=ks_, cs_=cs_: e.tensor_mul(
                rh(pt[1][:, :]), ks_[:, :, 1, :], cs_), inc="eP")
            mp = C["eP"]
            W("gpsimd", lambda e, ke=ke: e.tensor_add(
                ke[:, :, 1, :], rh(pt[0][:, :]), rh(pt[1][:, :])),
              waits=[("eP", mp)], inc="eP")
            ropek = C["eP"]
            tok[("ropekt", gt)] = ropek
            dma("sync", q_sp[s, t * 128:(t + 1) * 128, :],
                qr_st[p][:, :], sq_, waits=[("eV", ropeq)])
            dma("gpsimd", k_sp[s, t * 128:(t + 1) * 128, :],
                kr_st[p][:, :], sk_, waits=[("eP", ropek)])

        ln_s0(0)
        ln_s0(1)
        for it in range(NT + 8):
            if it + 2 < NT:
                ln_s0(it + 2)
            if it < NT:
                ln_s1(it)
            if 5 <= it < NT + 5:
                ln_s6(it - 5)
            if 1 <= it < NT + 1:
                ln_s2(it - 1)
            if 2 <= it < NT + 2:
                ln_s3(it - 2)
            if 3 <= it < NT + 3:
                ln_s4(it - 3)
            if 4 <= it < NT + 4:
                ln_s5(it - 4)
            if 8 <= it < 8 + 2 * NE and (it - 8) % 2 == 0:
                ei = (it - 8) // 2
                dma("gpsimd", we16[ei, :, :], we_h[ei, :, :], "swe",
                    waits=[("swe", 16 * ei)])
            if it == 8 + 2 * NE:
                dma("gpsimd", be16s[0:8, :], be_h[:, :], "sbe")
            if 6 <= it < NT + 6:
                gsp = it - 6
                dma("scalar",
                    xnTs_h.rearrange("p (c w) -> p c w", c=NCH)
                    [:, :, gsp * 128:(gsp + 1) * 128],
                    xnT.rearrange("p (c w) -> p c w", c=NCH)
                    [:, :, gsp * 128:(gsp + 1) * 128], "sxnT",
                    waits=[("sxnT", 16 * gsp), ("eV", tok[("xnTv", gsp)]),
                           ("eA", tok[("xnTa", gsp)])])
            if it >= 8:
                p1b(it - 8)
            if 6 <= it < NT + 6:
                p1a(it - 6)

        wbt = 0
        for ee in range(NE):
            for c2 in range(4):
                sl2 = wbt % 2
                wts = [("swe", 16 * (ee + 1)), ("swb", 32 * wbt)]
                if wbt >= 2:
                    wts.append(("swc", 16 * (wbt - 1)))
                for j in range(2):
                    c = c2 * 2 + j
                    dma("sync", bnc[sl2][:, j * 1024:(j + 1) * 1024],
                        we16[ee, :, c * 128:(c + 1) * 128], "swb",
                        transpose=True, waits=wts)
                    wts = []
                wbt += 1
                dma("scalar", weTsp_h[ee, :, c2 * 2048:(c2 + 1) * 2048],
                    bnc[sl2][:, :], "swc",
                    waits=[("swb", 32 * wbt), ("swc", 16 * (wbt - 1))])
        wcT = C["swc"]

        for t in range(NT):
            for c in range(NCH):
                for s3 in range(3):
                    W("tensor",
                      lambda e, t=t, c=c, s3=s3: e.matmul(
                          PC6[:, t * 8:(t + 1) * 8],
                          xnT[:, c * T + t * 128:c * T + (t + 1) * 128],
                          wgT[:, c * 24 + s3 * 8:c * 24 + s3 * 8 + 8],
                          start=(c == 0 and s3 == 0),
                          stop=(c == NCH - 1 and s3 == 2)),
                      waits=[("eA", max(C["eA"] if t == 0 and c == 0 and s3 == 0 else 0, wgT_done))], inc="eT")
        W("vector", lambda e: e.tensor_copy(lgall[:, :], PC6[:, 0:256]),
          waits=[("eT", C["eT"]), ("eV", C["eV"])], inc="eV")
        d3 = lambda a: a.rearrange("q (t e) -> q t e", e=8)
        bc = lambda a: a.unsqueeze(2).broadcast_to([128, NT, 8])
        W("vector", lambda e: e.tensor_copy(lgall[:, :], PC6[:, 0:256]),
          waits=[("eT", C["eT"]), ("eV", C["eV"])], inc="eV")
        W("vector", lambda e: e.reduce_max(v1a[:, :], d3(lgall), axis=AX.X),
          waits=[("eV", C["eV"])], inc="eV")
        W("vector", lambda e: e.tensor_tensor(
            d3(eq16[:, 0:256]), d3(lgall), bc(v1a[:, :]), op=ALU.is_equal),
          waits=[("eV", C["eV"])], inc="eV")
        W("vector", lambda e: e.scalar_tensor_tensor(
            l2a[:, :], eq16[:, 0:256], -1e30, lgall[:, :],
            op0=ALU.mult, op1=ALU.add),
          waits=[("eV", C["eV"])], inc="eV")
        W("vector", lambda e: e.reduce_max(v2a[:, :], d3(l2a), axis=AX.X),
          waits=[("eV", C["eV"])], inc="eV")
        W("vector", lambda e: e.tensor_tensor(
            d3(eq16[:, 256:512]), d3(l2a), bc(v2a[:, :]), op=ALU.is_equal),
          waits=[("eV", C["eV"])], inc="eV")
        W("vector", lambda e: e.tensor_sub(dst[:, :], v2a[:, :], v1a[:, :]),
          waits=[("eV", C["eV"])], inc="eV")

        W("scalar", lambda e: e.activation(edt[:, :], dst[:, :], AF.Exp),
          waits=[("eV", C["eV"])], inc="eA")
        W("vector", lambda e: e.tensor_scalar(dent[:, :], edt[:, :], 1.0,
                                              None, op0=ALU.add),
          waits=[("eA", C["eA"])], inc="eV")
        W("vector", lambda e: e.reciprocal(w1t[:, :], dent[:, :]),
          waits=[("eV", C["eV"])], inc="eV")
        W("vector", lambda e: e.tensor_mul(w2t[:, :], edt[:, :], w1t[:, :]),
          waits=[("eV", C["eV"])], inc="eV")
        W("vector", lambda e: e.tensor_copy(w116[:, :], w1t[:, :]),
          waits=[("eV", C["eV"])], inc="eV")
        W("vector", lambda e: e.tensor_copy(w216[:, :], w2t[:, :]),
          waits=[("eV", C["eV"])], inc="eV")
        W("vector", lambda e: e.tensor_copy(w132[:, :], w116[:, :]),
          waits=[("eV", C["eV"])], inc="eV")
        W("vector", lambda e: e.tensor_copy(w232[:, :], w216[:, :]),
          waits=[("eV", C["eV"])], inc="eV")
        for t in range(NT):
            W("vector", lambda e, t=t: e.tensor_scalar(
                tmp8[:, :], eq16[:, 256 + t * 8:256 + (t + 1) * 8],
                w232[:, t:t + 1], None, op0=ALU.mult),
              waits=[("eV", C["eV"])], inc="eV")
            W("vector", lambda e, t=t: e.scalar_tensor_tensor(
                wm32[:, t * 8:(t + 1) * 8], eq16[:, t * 8:(t + 1) * 8],
                w132[:, t:t + 1], tmp8[:, :], op0=ALU.mult, op1=ALU.add),
              waits=[("eV", C["eV"])], inc="eV")
            W("vector", lambda e, t=t: e.tensor_copy(
                wm16[:, t * 8:(t + 1) * 8], wm32[:, t * 8:(t + 1) * 8]),
              waits=[("eV", C["eV"])], inc="eV")
            W("vector", lambda e, t=t: e.tensor_copy(
                wm32[:, t * 8:(t + 1) * 8], wm16[:, t * 8:(t + 1) * 8]),
              waits=[("eV", C["eV"])], inc="eV")
            W("tensor", lambda e, t=t: e.transpose(
                pb7f16[0:8, 0:128], wm16[:, t * 8:(t + 1) * 8], idn[:, :]),
              waits=[("eV", C["eV"]), ("eA", C["eA"])], inc="eT")
            W("scalar", lambda e, t=t: e.copy(
                wmT[0:8, t * 128:(t + 1) * 128], pb7f16[0:8, 0:128]),
              waits=[("eT", C["eT"])], inc="eA")

        P.barrier()
        spw = [("sqsA", C["sqsA"]), ("sqsB", C["sqsB"]),
               ("sksA", C["sksA"]), ("sksB", C["sksB"]),
               ("svsA", C["svsA"]), ("svsB", C["svsB"]),
               ("sxnT", C["sxnT"]), ("sxA", C["sxA"]), ("sxB", C["sxB"]),
               ("sxC", C["sxC"]), ("sxD", C["sxD"]),
               ("scsA", C["scsA"]), ("scsB", C["scsB"]),
               ("scsC", C["scsC"]), ("scsD", C["scsD"])]

        dma("scalar", ow128[:, :].rearrange("k (c n) -> k c n", c=NCH),
            ow_h.rearrange("(c k) n -> k c n", k=128), "sow",
            waits=P.bw() + spw)
        oww = C["sow"]
        bew = C["sbe"]

        def xnq_load(k):
            dma("gpsimd", xnq[k % 2][:, :].rearrange("p (c w) -> p c w",
                                                     c=NCH),
                xnTs_h.rearrange("p (c w) -> p c w", c=NCH)
                [:, :, k * 512:(k + 1) * 512], "sgq",
                waits=P.bw() + spw +
                [("sgq", 16 * k), ("sxnT", 16 * (4 * k + 4)),
                 ("eT", tok.get(("xnq_free", k % 2), 0))])
            tok[("xnq_ld", k)] = C["sgq"]

        def wexp_load(k, e):
            ring = "scalar" if e % 2 == 0 else "gpsimd"
            sw = "sweA" if e % 2 == 0 else "sweB"
            wts = P.bw() + spw + [("swc", wcT),
                                  ("eT", tok.get(("wfree", k, e), 0))]
            for hh in range(4):
                dma(ring, weT[e % 2][:, hh * 2048:(hh + 1) * 2048],
                    weTsp_h[e, :, hh * 2048:(hh + 1) * 2048], sw, waits=wts)
                wts = []
            tok[("wld", k, e)] = C[sw]

        def p2_units(k):
            units = []
            sl = k % 2
            if k + 1 < 2 * NQG:
                units.append(lambda k=k: xnq_load(k + 1))

            def bias_unit(t, half, k=k):
                gt = 4 * k + t
                W("tensor", lambda e, gt=gt, half=half: e.matmul(
                    PC6[:, :], wmT[0:8, gt * 128:(gt + 1) * 128],
                    be16s[0:8, half * 512:(half + 1) * 512],
                    start=True, stop=True),
                  waits=[("sbe", bew), ("eA", tok.get("pc6a", 0)),
                         ("eV", tok.get("pc6v", 0))], inc="eT")
                W("scalar", lambda e, t=t, half=half: e.copy(
                    acc32[t][:, half * 512:(half + 1) * 512], PC6[:, :]),
                  waits=P.bw() + spw + [("eT", C["eT"]),
                                  ("eA", tok.get(("accfree", t), 0))] +
                  ([("soA", tok.get(("odma", 0), 0)),
                    ("soB", tok.get(("odma", 1), 0))] if t == 0 else []),
                  inc="eA")
                tok["pc6a"] = C["eA"]
                tok[("accinit", t, half)] = C["eA"]
            for t in range(4):
                for half in range(2):
                    units.append(lambda t=t, half=half: bias_unit(t, half))

            def mm_unit(e, t, half, k=k, sl=sl):
                be_ = e % 2
                sw = "sweA" if be_ == 0 else "sweB"
                wrd = tok[("wld", k, e)]
                gt = 4 * k + t
                for c in range(NCH):
                    wts = [(sw, wrd), ("sgq", tok[("xnq_ld", k)])]
                    if c == 0:
                        wts.append(("eV", tok.get(("pbfree", be_), 0)))
                    W("tensor", lambda e2, c=c, t=t, half=half, be_=be_,
                      sl=sl: e2.matmul(
                          PB[:, 512 * be_:512 * be_ + 512],
                          xnq[sl][:, c * 512 + t * 128:
                                  c * 512 + (t + 1) * 128],
                          weT[be_][:, c * D + half * 512:
                                   c * D + (half + 1) * 512],
                          start=(c == 0), stop=(c == NCH - 1)),
                      waits=wts, inc="eT")
                mm = C["eT"]
                if t == 3 and half == 1:
                    if e + 2 < NE:
                        tok[("wfree", k, e + 2)] = mm
                    else:
                        tok[("wfree", k + 1, e - 6)] = mm
                    if e == NE - 1:
                        tok[("xnq_free", sl)] = mm
                wts2 = [("eT", mm), ("eV", C["eV"])]
                if e == 0:
                    wts2.append(("eA", tok[("accinit", t, half)]))
                W("vector", lambda e2, t=t, half=half, be_=be_, gt=gt,
                  e=e: e2.scalar_tensor_tensor(
                      acc32[t][:, half * 512:(half + 1) * 512],
                      PB[:, 512 * be_:512 * be_ + 512],
                      wm32[:, gt * 8 + e:gt * 8 + e + 1],
                      acc32[t][:, half * 512:(half + 1) * 512],
                      op0=ALU.mult, op1=ALU.add),
                  waits=wts2, inc="eV")
                tok[("pbfree", be_)] = C["eV"]
                if e == NE - 1 and half == 1:
                    tok[("sttdone", t)] = C["eV"]
            for e in range(NE):
                for t in range(4):
                    for half in range(2):
                        units.append(
                            lambda e=e, t=t, half=half: mm_unit(e, t, half))
                if e + 2 < NE:
                    units.append(lambda e=e, k=k: wexp_load(k, e + 2))
                elif k + 1 < 2 * NQG:
                    units.append(lambda e=e, k=k: wexp_load(k + 1, e - 6))

            def silu_unit(t, k=k, sl=sl):
                W("scalar", lambda e, t=t, sl=sl: e.activation(
                    gsl[sl][:, t * 1024:(t + 1) * 1024], acc32[t][:, :],
                    AF.Silu),
                  waits=P.bw() + spw + [("eV", tok[("sttdone", t)]),
                                  ("eA", C["eA"]),
                                  ("eV", tok.get(("gslfree", sl), 0))],
                  inc="eA")
                tok[("accfree", t)] = C["eA"]
                if t == 3:
                    tok[("sild", k)] = C["eA"]
            for t in range(4):
                units.append(lambda t=t: silu_unit(t))
            return units

        def attn_units(s, qg):
            au = []
            nkc = 4 * qg + 4
            ktr = tok[("ktr", s)]
            vld = tok[("vld", s)]
            qrd = tok["qrd"]

            def emit_pv(h, kc):
                p2 = h % 2
                j = kc - 4 * qg
                sbk = kc % 3
                w0 = max(0, j) * 128
                mw = 65 if h == NH - 1 else 128
                pvw = [("eA", tok.get(("expC", kc), 0)), ("svld", vld)]
                if j >= 0:
                    pvw.append(("eP", tok.get(("maskdone", sbk), 0)))
                if kc == 0:
                    pvw.append(("eA", tok.get("pvrecip", 0)))
                    pvw.append(("eV", tok.get("pvevac", 0)))
                W("tensor", lambda e, kc=kc, h=h, sbk=sbk, w0=w0,
                  nkc=nkc, p2=p2, mw=mw: e.matmul(
                      PV2[0:mw, 512 * p2 + w0:512 * p2 + 512],
                      v_sb[:, kc * VF + h * VW:kc * VF + h * VW + mw],
                      expb3[sbk][:, 512 * p2 + w0:512 * p2 + 512],
                      start=(kc == 0), stop=(kc == nkc - 1)),
                  waits=pvw, inc="eT")
                tok[("exprd", sbk)] = C["eT"]

            def emit_tail_a(hp):
                pvmm = tok[("pvmm", hp)]
                W("scalar", lambda e: e.activation(
                    recL[96:97, 0:1024], PV2[64:65, 0:1024], AF.Ln),
                  waits=[("swc", wcT), ("eT", pvmm), ("eA", C["eA"]),
                         ("eA", tok.get("lnrd", 0))], inc="eA")
                W("scalar", lambda e: e.activation(
                    recE[64:65, 0:1024], recL[96:97, 0:1024],
                    AF.Exp, scale=-1.0),
                  waits=[("eA", C["eA"]),
                         ("eT", tok.get("rec16rd", 0))], inc="eA")
                tok["pvrecip"] = C["eA"]
                tok[("recv", hp)] = C["eA"]
                tok["lnrd"] = C["eA"]
                W("vector", lambda e: e.tensor_copy(
                    at2[0:64, :], PV2[0:64, 0:1024]),
                  waits=[("swc", wcT), ("eT", pvmm),
                         ("eV", tok.get("atrd", 0))], inc="eV")
                tok["pvevac"] = C["eV"]
                tok[("atv", hp)] = C["eV"]

            def emit_tail_b(hp):
                W("tensor", lambda e: e.matmul(
                    PC6[0:64, :], ones16[64:65, 0:64],
                    recE[64:65, 0:512], start=True, stop=True),
                  waits=[("eA", tok[("recv", hp)]),
                         ("eA", tok.get("pc6a", 0)),
                         ("eV", tok.get("pc6v", 0))], inc="eT")
                W("tensor", lambda e: e.matmul(
                    PC7[0:64, :], ones16[64:65, 0:64],
                    recE[64:65, 512:1024], start=True, stop=True),
                  waits=[("eV", tok.get("bcrd", 0))], inc="eT")
                tok["rec16rd"] = C["eT"]
                bcmm = C["eT"]
                W("vector", lambda e, hp=hp: e.tensor_mul(
                    attnT[0:64, hp * 512:(hp + 1) * 512],
                    at2[0:64, 0:512], PC6[0:64, :]),
                  waits=[("eV", tok[("atv", hp)]), ("eT", bcmm),
                         ("eT", tok.get("atT_rd", 0)), ("eV", C["eV"])],
                  inc="eV")
                tok["pc6v"] = C["eV"]
                W("vector", lambda e, hp=hp: e.tensor_mul(
                    attnT[64:128, hp * 512:(hp + 1) * 512],
                    at2[0:64, 512:1024], PC7[0:64, :]),
                  inc="eV")
                tok["atrd"] = C["eV"]
                tok["bcrd"] = C["eV"]

            def score_unit(hp, kc):
                ch = hp
                sbk = kc % 3
                j = kc - 4 * qg
                w0 = max(0, j) * 128
                for p2 in range(2):
                    base = p2 * 64
                    wts = [("skT", ktr), ("sqTA", qrd)]
                    if p2 == 0:
                        wts.append(("eA", tok.get("expdone", 0)))
                        if hp == 0 and kc == 0:
                            wts.append(("soA", tok.get("outsA", 0)))
                            wts.append(("soB", tok.get("outsB", 0)))
                    W("tensor", lambda e, base=base, ch=ch, kc=kc,
                      p2=p2: e.matmul(
                          PA[:, 512 * p2:512 * p2 + 512],
                          kT[base:base + 64,
                             ch * S + kc * 128:ch * S + (kc + 1) * 128],
                          qTb[base:base + 64, ch * 512:(ch + 1) * 512],
                          start=True, stop=True),
                      waits=wts, inc="eT")
                ev = (lambda a: a.rearrange("q (hh c) -> q hh c", hh=2)
                      [:, :, w0:512])
                W("scalar", lambda e, sbk=sbk, ev=ev: e.activation(
                    ev(expb3[sbk]), ev(PA), AF.Exp, scale=1.0 / 32.0),
                  waits=[("swc", wcT), ("eT", C["eT"]),
                         ("eT", tok.get(("exprd", sbk), 0))],
                  inc="eA")
                tok["expdone"] = C["eA"]
                tok[("expC", kc)] = C["eA"]
                if j >= 0:
                    tv = (lambda a: a.rearrange("q (hh c) -> q hh c", hh=2)
                          [:, :, j * 128:(j + 1) * 128])
                    W("gpsimd", lambda e, sbk=sbk, tv=tv: e.tensor_mul(
                        tv(expb3[sbk]), tv(expb3[sbk]),
                        tri[:, :].unsqueeze(1).broadcast_to([128, 2, 128])),
                      waits=[("swc", wcT), ("eA", C["eA"]), ("scst", cst),
                             ("eP", C["eP"])], inc="eP")
                    tok[("maskdone", sbk)] = C["eP"]
                if kc >= 2:
                    emit_pv(2 * hp, kc - 2)
                    emit_pv(2 * hp + 1, kc - 2)

            def hp_tail(hp):
                emit_pv(2 * hp, nkc - 2)
                emit_pv(2 * hp + 1, nkc - 2)
                emit_pv(2 * hp, nkc - 1)
                emit_pv(2 * hp + 1, nkc - 1)
                tok[("pvmm", hp)] = C["eT"]
                if hp >= 1:
                    emit_tail_b(hp - 1)
                emit_tail_a(hp)

            for hp in range(NH // 2):
                for kc in range(nkc):
                    au.append(lambda hp=hp, kc=kc: score_unit(hp, kc))
                au.append(lambda hp=hp: hp_tail(hp))
            au.append(lambda: emit_tail_b(NH // 2 - 1))
            return au

        def combine(s, qg, k):
            heads_done = C["eV"]
            r0 = s * S + qg * 512
            sild = tok[("sild", k)]
            for pc in range(2):
                dma("gpsimd", xq2[pc][:, :],
                    x_h[r0 + (pc // 2) * 128:r0 + (pc // 2 + 1) * 128,
                        (pc % 2) * 512:(pc % 2 + 1) * 512], "sxq",
                    waits=[("sxq", C["sxq"]),
                           ("eV", tok.get(("xqfree", pc % 2), 0))])
                tok[("xql", pc)] = C["sxq"]
            for jt in range(4):
                for half in range(2):
                    pc = jt * 2 + half
                    for cp in range(NCH):
                        wts = [("sow", oww), ("eV", heads_done)]
                        if cp == 0:
                            wts.append(("eA", tok["expdone"]))
                            pf = tok.get(("pafree", half))
                            if pf is not None:
                                wts.append(pf)
                        W("tensor", lambda e, jt=jt, half=half, cp=cp:
                          e.matmul(
                              PA[:, 512 * half:512 * half + 512],
                              attnT[0:128, cp * 512 + jt * 128:
                                    cp * 512 + (jt + 1) * 128],
                              ow128[:, cp * D + half * 512:
                                    cp * D + (half + 1) * 512],
                              start=(cp == 0), stop=(cp == NCH - 1)),
                          waits=wts, inc="eT")
                    omm = C["eT"]
                    so = "soA" if pc % 2 == 0 else "soB"
                    ob = acc32[0][:, (pc % 2) * 512:(pc % 2) * 512 + 512]
                    mw_ = [("eT", omm), ("eA", sild), ("eV", C["eV"]),
                           (so, tok.get(("odma", pc % 2), 0))]
                    W("vector", lambda e, jt=jt, half=half, sl=k % 2,
                      ob=ob: e.tensor_tensor(
                          ob,
                          PA[:, 512 * half:512 * half + 512],
                          gsl[sl][:, jt * 1024 + half * 512:
                                  jt * 1024 + (half + 1) * 512],
                          op=ALU.mult),
                      waits=mw_, inc="eV")
                    tok[("pafree", half)] = ("eV", C["eV"])
                    W("vector", lambda e, pc=pc, ob=ob:
                      e.tensor_tensor(ob, ob, xq2[pc % 2][:, :],
                                      op=ALU.add),
                      waits=[("sxq", tok[("xql", pc)]), ("eV", C["eV"])],
                      inc="eV")
                    tok[("xqfree", pc % 2)] = C["eV"]
                    dma("sync",
                        out_h[r0 + jt * 128:r0 + (jt + 1) * 128,
                              half * 512:(half + 1) * 512], ob, so,
                        waits=[("eV", C["eV"])])
                    tok[("odma", pc % 2)] = C[so]
                    if pc + 2 < 8:
                        npc = pc + 2
                        njt, nhalf = npc // 2, npc % 2
                        dma("gpsimd", xq2[npc % 2][:, :],
                            x_h[r0 + njt * 128:r0 + (njt + 1) * 128,
                                nhalf * 512:(nhalf + 1) * 512], "sxq",
                            waits=[("sxq", C["sxq"]),
                                   ("eV", tok.get(("xqfree", npc % 2), 0))])
                        tok[("xql", npc)] = C["sxq"]
            tok["atT_rd"] = C["eT"]
            tok["outsA"] = C["soA"]
            tok["outsB"] = C["soB"]

        wexp_load(0, 0)
        wexp_load(0, 1)
        xnq_load(0)
        for s in range(NSEQ):
            kw_ = P.bw() + spw + [("eT", tok.get("kT_free", 0))]
            for c in range(NCH):
                dma("sync", c16(AR1, c * 4096, 2048)[:, :],
                    k_sp[s, :, c * 128:(c + 1) * 128], "skT", transpose=True,
                    waits=kw_)
                kw_ = []
            tok[("ktr", s)] = C["skT"]
            dma("gpsimd", v_sb[:, :].rearrange("q (t f) -> q t f", t=NTS),
                v_sp[s].rearrange("(t q) f -> q t f", q=128), "svld",
                waits=P.bw() + spw + [("eT", tok.get("v_free", 0))])
            tok[("vld", s)] = C["svld"]
            for qg in range(NQG):
                k = s * NQG + qg
                qw_ = P.bw() + spw + [("eT", tok.get("qT_free", 0))]
                for c in range(NCH):
                    dma("sync", qTb[:, c * 512:(c + 1) * 512],
                        q_sp[s, qg * 512:(qg + 1) * 512,
                             c * 128:(c + 1) * 128], "sqTA",
                        transpose=True, waits=qw_)
                    qw_ = []
                tok["qrd"] = C["sqTA"]
                au = attn_units(s, qg)
                pu = p2_units(k)
                j = 0
                for i, u in enumerate(au):
                    u()
                    share = ((i + 1) * len(pu)) // len(au)
                    while j < share:
                        pu[j]()
                        j += 1
                tok["qT_free"] = C["eT"]
                combine(s, qg, k)
            tok["kT_free"] = C["eT"]
            tok["v_free"] = C["eT"]

        W("scalar", None, waits=[("soA", C["soA"]), ("soB", C["soB"])])
        P.barrier()

        with nc.Block() as blk:
            P.replay(blk)
    return nc


_CACHE = {}


def _constants():
    if "c" in _CACHE:
        return _CACHE["c"]
    inv = 1.0 / (10000.0 ** (np.arange(0, HD, 2, dtype=np.float64) / HD))
    pos = np.arange(S, dtype=np.float64)
    fr = pos[:, None] * inv[None, :]
    cos = np.tile(np.cos(fr), (1, NH)).astype(np.float16)
    sin = np.tile(np.sin(fr), (1, NH)).astype(np.float16)
    tri = np.triu(np.ones((128, 128), dtype=np.float16))
    idn = np.eye(128, dtype=np.float16)
    _CACHE["c"] = (cos, sin, tri, idn)
    return _CACHE["c"]


def kernel(L0_attn_x, L0_attn_qw, L0_attn_kw, L0_attn_vw, L0_attn_ow,
           L0_ffn_Wg, L0_ffn_We, L0_ffn_be):
    x = np.ascontiguousarray(np.asarray(L0_attn_x, dtype=np.float16))
    qw = np.ascontiguousarray(np.asarray(L0_attn_qw, dtype=np.float16))
    kw = np.ascontiguousarray(np.asarray(L0_attn_kw, dtype=np.float16))
    vw = np.ascontiguousarray(np.asarray(L0_attn_vw, dtype=np.float16))
    ow = np.ascontiguousarray(np.asarray(L0_attn_ow, dtype=np.float16))
    wg = np.ascontiguousarray(np.asarray(L0_ffn_Wg, dtype=np.float32))
    we = np.ascontiguousarray(np.asarray(L0_ffn_We, dtype=np.float32))
    be = np.ascontiguousarray(np.asarray(L0_ffn_be, dtype=np.float32))
    cos, sin, tri, idn = _constants()

    nc = build_nc()
    in_maps = []
    for i in range(NCORE):
        in_maps.append({
            "x_c": np.ascontiguousarray(
                x[2 * i:2 * i + 2].reshape(T, D)),
            "qw": qw, "kw": kw, "vw": vw, "ow": ow,
            "wg": wg, "we": we, "be": be,
            "c_cos": cos, "c_sin": sin, "c_tri": tri, "c_idn": idn,
        })
    res = run_bass_kernel_spmd(nc, in_maps, core_ids=list(range(NCORE)))
    out = np.concatenate(
        [np.asarray(r["out_c"]) for r in res.results], axis=0)
    return out.reshape(16, S, D).astype(np.float32, copy=False)


# version 22
# speedup vs baseline: 1.0075x; 1.0075x over previous
import contextlib

import numpy as np

import concourse.bass as bass
import concourse.mybir as mybir
from concourse.bass_utils import run_bass_kernel_spmd

F32 = mybir.dt.float32
F16 = mybir.dt.float16
U8 = mybir.dt.uint8
ALU = mybir.AluOpType
AF = mybir.ActivationFunctionType
AX = mybir.AxisListType

NCORE = 8
D = 1024
S = 2048
NSEQ = 2
T = NSEQ * S
NT = T // 128
NTS = S // 128
NH = 16
HD = 64
NCH = 8
NE = 8
NQG = 4
EPS16 = float(np.float16(1e-5))
VW = 66
VF = NH * VW


class Prog:

    def __init__(self, nc):
        self.nc = nc
        self.ops = []
        self.sems = {}
        self.cnt = {}
        self.wm = {}

    def decl(self, ctx, *names):
        for n in names:
            self.sems[n] = ctx.enter_context(self.nc.semaphore(n))
            self.cnt[n] = 0

    def emit(self, eng, fn, waits=(), inc=None, amt=1):
        ws = []
        for (sn, v) in waits:
            if v is None or v <= 0:
                continue
            k = (eng, sn)
            if self.wm.get(k, 0) >= v:
                continue
            self.wm[k] = v
            ws.append((sn, v))
        incs = []
        if inc is not None:
            self.cnt[inc] += amt
            incs.append((inc, amt))
        self.ops.append((eng, ws, fn, incs))
        return self.cnt[inc] if inc is not None else None

    def barrier(self):
        engs = ("vector", "scalar", "tensor", "gpsimd")
        marks = {}
        for e in engs:
            sn = "bar_" + e
            self.cnt[sn] += 1
            self.ops.append((e, [], None, [(sn, 1)]))
            marks[e] = self.cnt[sn]
        for e in engs:
            self.emit(e, None,
                      waits=[("bar_" + o, marks[o]) for o in engs if o != e])
        self.last_bar = [("bar_" + o, marks[o]) for o in engs]

    def bw(self):
        return list(getattr(self, "last_bar", []))

    def replay(self, blk):
        sems, ops = self.sems, self.ops

        def runner(name):
            def body(eng):
                for (e, ws, fn, incs) in ops:
                    if e != name:
                        continue
                    for (sn, v) in ws:
                        eng.wait_ge(sems[sn], v)
                    if fn is None:
                        if incs:
                            ins = eng.drain()
                            for (sn, a) in incs:
                                ins.then_inc(sems[sn], a)
                        continue
                    ins = fn(eng)
                    for (sn, a) in incs:
                        ins.then_inc(sems[sn], a)
            return body

        blk.vector(runner("vector"))
        blk.scalar(runner("scalar"))
        blk.tensor(runner("tensor"))
        blk.gpsimd(runner("gpsimd"))
        blk.sync(runner("sync"))


def build_nc():
    nc = bass.Bass("TRN2", target_bir_lowering=False)

    x_h = nc.dram_tensor("x_c", [T, D], F16, kind="ExternalInput").ap()
    qw_h = nc.dram_tensor("qw", [D, D], F16, kind="ExternalInput").ap()
    kw_h = nc.dram_tensor("kw", [D, D], F16, kind="ExternalInput").ap()
    vw_h = nc.dram_tensor("vw", [D, D], F16, kind="ExternalInput").ap()
    ow_h = nc.dram_tensor("ow", [D, D], F16, kind="ExternalInput").ap()
    wg_h = nc.dram_tensor("wg", [NE, D], F32, kind="ExternalInput").ap()
    we_h = nc.dram_tensor("we", [NE, D, D], F32, kind="ExternalInput").ap()
    be_h = nc.dram_tensor("be", [NE, D], F32, kind="ExternalInput").ap()
    cos_h = nc.dram_tensor("c_cos", [S, 512], F16, kind="ExternalInput").ap()
    sin_h = nc.dram_tensor("c_sin", [S, 512], F16, kind="ExternalInput").ap()
    tri_h = nc.dram_tensor("c_tri", [128, 128], F16, kind="ExternalInput").ap()
    idn_h = nc.dram_tensor("c_idn", [128, 128], F16, kind="ExternalInput").ap()
    out_h = nc.dram_tensor("out_c", [T, D], F32, kind="ExternalOutput").ap()

    xn_sp = nc.dram_tensor("xn_sp", [T, D], F16, kind="Internal").ap()
    q_sp = nc.dram_tensor("q_sp", [NSEQ, S, D], F16, kind="Internal").ap()
    k_sp = nc.dram_tensor("k_sp", [NSEQ, S, D], F16, kind="Internal").ap()
    v_sp = nc.dram_tensor("v_sp", [NSEQ, S, VF], F16, kind="Internal").ap()
    g_sp = nc.dram_tensor("g_sp", [T, D], F16, kind="Internal").ap()
    we16 = nc.dram_tensor("we16", [NE, D, D], F16, kind="Internal").ap()
    xnTs_h = nc.dram_tensor("xnTs", [128, NCH * T], F16, kind="Internal").ap()
    weTsp_h = nc.dram_tensor("weTsp", [NE, 128, NCH * D], F16,
                             kind="Internal").ap()

    ctx = contextlib.ExitStack()
    with ctx:
        sbt = lambda name, shape, dt: ctx.enter_context(
            nc.sbuf_tensor(name, shape, dt))[:]
        AR1 = sbt("AR1", [128, 65536], U8)
        AR2 = sbt("AR2", [128, 125952], U8)

        def c16(ar, off, cols):
            return ar[:, off:off + 2 * cols].bitcast(F16)

        def c32(ar, off, cols):
            return ar[:, off:off + 4 * cols].bitcast(F32)

        xnT = c16(AR1, 0, 32768)
        kT = c16(AR1, 0, 16384)
        qTb = c16(AR1, 32768, 4096)
        attnT = c16(AR1, 40960, 4096)
        ow128 = c16(AR1, 49152, 8192)

        xst = [c16(AR2, 92672 + 2048 * i, 1024) for i in range(4)]
        xcn16 = [c16(AR2, 100864 + 2048 * i, 1024) for i in range(4)]
        sqd16 = c16(AR2, 109056, 1024)
        lgall = c32(AR2, 111104, 256)
        l2a = c32(AR2, 112128, 256)
        m32 = [c32(AR2, 111104 + 32 * i, 1) for i in range(4)]
        m16 = [c16(AR2, 111232 + 32 * i, 1) for i in range(4)]
        mneg = [c32(AR2, 111360 + 32 * i, 1) for i in range(4)]
        ssum = [c32(AR2, 111488 + 32 * i, 1) for i in range(4)]
        v16t = [c16(AR2, 111616 + 32 * i, 1) for i in range(4)]
        s16t = [c16(AR2, 111744 + 32 * i, 1) for i in range(4)]
        r16t = [c16(AR2, 111872 + 32 * i, 1) for i in range(4)]
        rc32 = [c32(AR2, 112000 + 32 * i, 1) for i in range(4)]
        qwS = c16(AR2, 0, 8192)
        kwS = c16(AR2, 16384, 8192)
        vwS = c16(AR2, 32768, 8192)
        cosb = [c16(AR2, 49152 + 2048 * i, 1024) for i in range(4)]
        qt_st = [c16(AR2, 57344 + 2048 * i, 1024) for i in range(4)]
        kt_st = [c16(AR2, 65536 + 2048 * i, 1024) for i in range(4)]
        vt_st = [c16(AR2, 73728, VF), c16(AR2, 76032, VF)]
        qr_st = [c16(AR2, 78336, 1024), c16(AR2, 80384, 1024)]
        kr_st = [c16(AR2, 82432, 1024), c16(AR2, 84480, 1024)]
        rt = [c16(AR2, 86528 + 1024 * i, 512) for i in range(4)]
        pt = [c16(AR2, 90624 + 1024 * i, 512) for i in range(2)]
        wgf = c32(AR2, 78336, 1024)
        wgr = c32(AR2, 82432, 1024)
        dst = c32(AR2, 113152, 32)
        edt = c32(AR2, 113280, 32)
        dent = c32(AR2, 113408, 32)
        w1t = c32(AR2, 113536, 32)
        w2t = c32(AR2, 113664, 32)
        w132 = c32(AR2, 113792, 32)
        w232 = c32(AR2, 113920, 32)
        v1a = c32(AR2, 114048, 32)
        v2a = c32(AR2, 114176, 32)
        w116 = c16(AR2, 114304, 32)
        w216 = c16(AR2, 114368, 32)
        tmp8 = c32(AR2, 114432, 8)
        eq16 = c16(AR2, 114464, 512)
        bnc = [c16(AR2, 115712 + 4096 * i, 2048) for i in range(2)]
        v_sb = c16(AR2, 0, 16896)
        weT = [c16(AR2, 33792 + 16384 * i, 8192) for i in range(2)]
        xnq = [c16(AR2, 66560 + 8192 * i, 4096) for i in range(2)]
        acc32 = [c32(AR2, 82944 + 4096 * i, 1024) for i in range(4)]
        gsl = [c16(AR2, 99328 + 8192 * i, 4096) for i in range(2)]
        expb3 = [c16(AR2, 115712 + 2048 * i, 1024) for i in range(3)]
        at2 = c32(AR2, 121856, 1024)
        recL = c32(AR2, 121856, 1024)
        recE = c16(AR2, 121856, 2048)

        tri = sbt("tri", [128, 128], F16)
        idn = sbt("idn", [128, 128], F16)
        ones16 = sbt("ones16", [128, 64], F16)

        wm32 = sbt("wm32", [128, NT * 8], F32)
        wm16 = sbt("wm16", [128, NT * 8], F16)
        wmT = sbt("wmT", [8, NT * 128], F16)
        wgT = sbt("wgT", [128, NCH * 24], F16)
        wgs = sbt("wgs", [8, 3 * D], F16)
        be16s = sbt("be16s", [8, D], F16)
        xq2 = [sbt("xq2_%d" % i, [128, 512], F16) for i in range(2)]

        PA = ctx.enter_context(nc.psum_tensor("PA", [128, 1024], F32))[:]
        PB = ctx.enter_context(nc.psum_tensor("PB", [128, 1024], F32))[:]
        PV2 = ctx.enter_context(nc.psum_tensor("PV2", [128, 1024], F32))[:]
        PC6 = ctx.enter_context(nc.psum_tensor("PC6", [128, 512], F32))[:]
        PC7 = ctx.enter_context(nc.psum_tensor("PC7", [128, 512], F32))[:]
        pb7f16 = PC7.bitcast(F16)

        P = Prog(nc)
        P.decl(ctx, "eV", "eA", "eT", "eP",
               "bar_vector", "bar_scalar", "bar_tensor", "bar_gpsimd",
               "sxA", "sxB", "sxC", "sxD", "sspA", "sspB", "sspC", "sspD",
               "sxnT", "scst",
               "scsA", "scsB", "scsC", "scsD", "sw3", "sqsA", "sqsB", "sksA", "sksB",
               "svsA", "svsB", "swe", "sweA", "sweB", "sbe", "sgsA", "sgsB",
               "skT", "svld", "sqTA", "sqTB", "sgq", "sxq", "soA", "soB",
               "sow", "swb", "swc")
        W = P.emit
        C = P.cnt

        def dma(q, out, in_, sem, transpose=False, waits=()):
            def fn(e, out=out, in_=in_, tr=transpose):
                if tr:
                    return e.dma_start(out, in_, transpose=True)
                return e.dma_start(out, in_)
            return W(q, fn, waits=waits, inc=sem, amt=16)

        tok = {}

        dma("sync", qwS[:, :].rearrange("k (c n) -> k c n", c=NCH),
            qw_h.rearrange("(c k) n -> k c n", k=128), "sw3")
        dma("sync", kwS[:, :].rearrange("k (c n) -> k c n", c=NCH),
            kw_h.rearrange("(c k) n -> k c n", k=128), "sw3")
        dma("sync", vwS[:, :].rearrange("k (c n) -> k c n", c=NCH),
            vw_h.rearrange("(c k) n -> k c n", k=128), "sw3")
        dma("sync", tri[:, :], tri_h[:, :], "scst")
        dma("sync", idn[:, :], idn_h[:, :], "scst")
        dma("sync", wgf[0:8, :], wg_h[:, :], "scst")
        cst = C["scst"]
        W("vector", lambda e: e.memset(ones16[:, :], 1.0), inc="eV")

        W("vector", lambda e: e.tensor_copy(wgs[0:8, 0:D], wgf[0:8, :]),
          waits=[("scst", cst)], inc="eV")
        W("vector", lambda e: e.tensor_sub(wgr[0:8, :], wgf[0:8, :],
                                           wgs[0:8, 0:D]),
          waits=[("eV", C["eV"])], inc="eV")
        W("vector", lambda e: e.tensor_copy(wgs[0:8, D:2 * D], wgr[0:8, :]),
          waits=[("eV", C["eV"])], inc="eV")
        W("vector", lambda e: e.tensor_sub(wgr[0:8, :], wgr[0:8, :],
                                           wgs[0:8, D:2 * D]),
          waits=[("eV", C["eV"])], inc="eV")
        W("vector", lambda e: e.tensor_copy(wgs[0:8, 2 * D:3 * D], wgr[0:8, :]),
          waits=[("eV", C["eV"])], inc="eV")
        wg_split = C["eV"]
        for c in range(NCH):
            for s3 in range(3):
                W("tensor",
                  lambda e, c=c, s3=s3: e.transpose(
                      pb7f16[0:128, 0:8],
                      wgs[0:8, s3 * D + c * 128:s3 * D + (c + 1) * 128],
                      idn[0:8, 0:8]),
                  waits=[("eV", wg_split), ("scst", cst), ("eA", C["eA"])],
                  inc="eT")
                W("scalar",
                  lambda e, c=c, s3=s3: e.copy(
                      wgT[:, c * 24 + s3 * 8:c * 24 + s3 * 8 + 8],
                      pb7f16[0:128, 0:8]),
                  waits=[("eT", C["eT"])], inc="eA")
        wgT_done = C["eA"]
        wgT_tp = C["eT"]
        w3 = C["sw3"]

        SSP = ("sspA", "sspB", "sspC", "sspD")
        SX = ("sxA", "sxB", "sxC", "sxD")

        def ln_s0(t):
            q = t % 4
            dma("sync", xst[q][:, :], x_h[t * 128:(t + 1) * 128, :], SX[q],
                waits=[("eA", tok.get(("xcd", q), 0))])

        def ln_s1(t):
            q = t % 4
            W("vector", lambda e, q=q: e.reduce_sum(m32[q][:, :],
                                                    xst[q][:, :], axis=AX.X),
              waits=[(SX[q], C[SX[q]]), ("eV", C["eV"])],
              inc="eV")
            W("vector", lambda e, q=q: e.tensor_scalar(
                m16[q][:, :], m32[q][:, :], 1.0 / D, None, op0=ALU.mult),
              waits=[("eV", C["eV"])], inc="eV")
            W("vector", lambda e, q=q: e.tensor_scalar(
                mneg[q][:, :], m16[q][:, :], -1.0, None, op0=ALU.mult),
              waits=[("eV", C["eV"])], inc="eV")
            tok[("mnegd", q)] = C["eV"]

        def ln_s2(t):
            q = t % 4
            W("scalar", lambda e, q=q: e.activation(
                xcn16[q][:, :], xst[q][:, :], AF.Identity,
                bias=mneg[q][:, :], scale=1.0),
              waits=[("eV", tok[("mnegd", q)]),
                     ("eT", tok.get(("xnrd", q), 0))], inc="eA")
            tok[("xcd", q)] = C["eA"]
            W("scalar", lambda e, q=q: e.activation(
                sqd16[:, :], xcn16[q][:, :], AF.Square,
                accum_out=ssum[q][:, :]),
              waits=[("eA", C["eA"]), ("eV", tok.get(("vrd", q), 0))],
              inc="eA")
            tok[("sqdone", q)] = C["eA"]

        def ln_s3(t):
            q = t % 4
            W("vector", lambda e, q=q: e.tensor_scalar(
                v16t[q][:, :], ssum[q][:, :], 1.0 / D, None, op0=ALU.mult),
              waits=[("eA", tok[("sqdone", q)]), ("eV", C["eV"])], inc="eV")
            tok[("vrd", q)] = C["eV"]
            W("vector", lambda e, q=q: e.tensor_scalar(
                s16t[q][:, :], v16t[q][:, :], EPS16, None, op0=ALU.add),
              waits=[("eV", C["eV"])], inc="eV")
            tok[("s16d", q)] = C["eV"]

        def ln_s4(t):
            q = t % 4
            W("scalar", lambda e, q=q: e.activation(r16t[q][:, :],
                                                    s16t[q][:, :], AF.Sqrt),
              waits=[("eV", tok[("s16d", q)]), ("eA", C["eA"])], inc="eA")
            tok[("sqrtd", q)] = C["eA"]

        def ln_s5(t):
            q = t % 4
            W("vector", lambda e, q=q: e.reciprocal(rc32[q][:, :],
                                                    r16t[q][:, :]),
              waits=[("eA", tok[("sqrtd", q)]), ("eV", C["eV"])], inc="eV")
            W("vector", lambda e, q=q: e.tensor_scalar(
                xcn16[q][:, :], xcn16[q][:, :], rc32[q][:, :], None,
                op0=ALU.mult),
              waits=[("eV", C["eV"]), ("eA", tok[("sqdone", q)])], inc="eV")
            tok[("xc16", q)] = C["eV"]

        PV2f16 = PV2.bitcast(F16)
        tp16 = [PV2f16[:, 0:128], PV2f16[:, 1024:1152]]

        def ln_s6(t):
            q = t % 4
            for c in range(NCH):
                bp = c % 2
                W("tensor", lambda e, q=q, c=c, bp=bp: e.transpose(
                    tp16[bp][:, :], xcn16[q][:, c * 128:(c + 1) * 128],
                    idn[:, :]),
                  waits=[("eV", tok[("xc16", q)]), ("scst", cst),
                         ("eV", tok.get(("tpev", bp), 0)),
                         ("eA", tok.get(("tpea", bp), 0))], inc="eT")
                eng = "vector" if c % 4 < 2 else "scalar"
                def ef(e, q=q, c=c, bp=bp, t=t):
                    dst = xnT[:, c * T + t * 128:c * T + (t + 1) * 128]
                    if hasattr(e, "activation"):
                        return e.copy(dst, tp16[bp][:, :])
                    return e.tensor_copy(dst, tp16[bp][:, :])
                W(eng, ef, waits=[("eT", C["eT"])],
                  inc="eV" if eng == "vector" else "eA")
                tok[("tpev" if eng == "vector" else "tpea", bp)] = \
                    C["eV" if eng == "vector" else "eA"]
            tok[("xnrd", q)] = C["eT"]
            tok[("xnTv", t)] = C["eV"]
            tok[("xnTa", t)] = C["eA"]


        SCS = ("scsA", "scsB", "scsC", "scsD")

        def p1a(gt):
            s, t = gt // NTS, gt % NTS
            p = gt % 2
            q4 = gt % 4
            scs = SCS[q4]
            dma("scalar", cosb[q4][:, 0:512], cos_h[t * 128:(t + 1) * 128, :],
                scs, waits=[("eV", tok.get(("ropeqt", gt - 4), 0)),
                            ("eP", tok.get(("ropekt", gt - 4), 0))])
            dma("scalar", cosb[q4][:, 512:1024],
                sin_h[t * 128:(t + 1) * 128, :], scs)
            tok[("csr", gt)] = C[scs]
            sv = "svsA" if p == 0 else "svsB"
            vv = vt_st[p].rearrange("q (h w) -> q h w", h=NH)
            for grp in range(2):
                for c in range(NCH):
                    qv = qwS[:, c * D:(c + 1) * D].rearrange(
                        "k (h i two) -> k h two i", two=2, i=32)
                    kv = kwS[:, c * D:(c + 1) * D].rearrange(
                        "k (h i two) -> k h two i", two=2, i=32)
                    lhs = xnT[:, c * T + gt * 128:c * T + (gt + 1) * 128]
                    hs = slice(8 * grp, 8 * grp + 8)
                    PG = PA if grp == 0 else PB
                    PVg = PC6 if grp == 0 else PC7
                    rhss = [qv[:, hs, :, :], kv[:, hs, :, :],
                            vwS[:, c * D + 512 * grp:
                                c * D + 512 * grp + 512]]
                    outs_ = [PG[:, 0:512], PG[:, 512:1024], PVg[:, :]]
                    for bi3 in range(3):
                        wts = [("sw3", w3), ("eV", tok[("xnTv", gt)]),
                               ("eA", tok[("xnTa", gt)])]
                        if c == 0:
                            wts.append(("eA", tok.get(("pjev", grp), 0)))
                            if grp == 1 and bi3 == 2:
                                wts.append(("eA", wgT_done))
                        W("tensor",
                          lambda e, o_=outs_[bi3], lhs=lhs,
                          rhs=rhss[bi3], c=c:
                          e.matmul(o_, lhs, rhs,
                                   start=(c == 0), stop=(c == NCH - 1)),
                          waits=wts, inc="eT")
                mm = C["eT"]
                PG = PA if grp == 0 else PB
                PVg = PC6 if grp == 0 else PC7
                W("scalar", lambda e, q4=q4, grp=grp, PG=PG: e.copy(
                    qt_st[q4][:, 512 * grp:512 * grp + 512],
                    PG[:, 0:512]),
                  waits=[("eT", mm),
                         ("eV", tok.get(("ropeqt", gt - 4), 0)),
                         ("eP", tok.get(("ropekt", gt - 4), 0))], inc="eA")
                W("scalar", lambda e, q4=q4, grp=grp, PG=PG: e.copy(
                    kt_st[q4][:, 512 * grp:512 * grp + 512],
                    PG[:, 512:1024]), inc="eA")
                W("scalar", lambda e, vv=vv, grp=grp, PVg=PVg: e.copy(
                    vv[:, 8 * grp:8 * grp + 8, 0:64],
                    PVg[:, :].rearrange("q (h d) -> q h d", h=8)),
                  waits=[(sv, 16 * (gt // 2))], inc="eA")
                tok[("pjev", grp)] = C["eA"]
            if gt < 2:
                W("vector", lambda e, vv=vv: e.memset(vv[:, :, 64:65],
                                                      1.0), inc="eV")
                W("vector", lambda e, vv=vv: e.memset(vv[:, :, 65:66],
                                                      0.0), inc="eV")
                tok[("vones", p)] = C["eV"]
            tok[("pjevt", gt)] = C["eA"]
            dma("gpsimd", v_sp[s, t * 128:(t + 1) * 128, :],
                vt_st[p][:, :], sv,
                waits=[("eA", C["eA"]), ("eV", tok.get(("vones", p), 0))])

        def p1b(gt):
            s, t = gt // NTS, gt % NTS
            p = gt % 2
            q4 = gt % 4
            pjev = tok[("pjevt", gt)]
            csr = tok[("csr", gt)]
            scs = SCS[q4]
            qe = qr_st[p].rearrange("q (h two i) -> q h two i", two=2, i=32)
            se_ = qt_st[q4].rearrange("q (h two i) -> q h two i", two=2, i=32)
            cs_ = cosb[q4][:, 0:512].rearrange("q (h i) -> q h i", h=NH)
            sn_ = cosb[q4][:, 512:1024].rearrange("q (h i) -> q h i", h=NH)
            rh = lambda a: a.rearrange("q (h i) -> q h i", h=NH)
            sq_ = "sqsA" if p == 0 else "sqsB"
            W("vector", lambda e, se_=se_, cs_=cs_: e.tensor_mul(
                rh(rt[0][:, :]), se_[:, :, 0, :], cs_),
              waits=[("eA", pjev), (scs, csr), ("eV", wg_split),
                     ("eT", wgT_tp),
                     ("eV", tok.get(("ropeqt", gt - 1), 0))], inc="eV")
            W("vector", lambda e, se_=se_, sn_=sn_: e.tensor_mul(
                rh(rt[1][:, :]), se_[:, :, 1, :], sn_), inc="eV")
            W("vector", lambda e, se_=se_, sn_=sn_: e.tensor_mul(
                rh(rt[2][:, :]), se_[:, :, 0, :], sn_), inc="eV")
            W("vector", lambda e, se_=se_, cs_=cs_: e.tensor_mul(
                rh(rt[3][:, :]), se_[:, :, 1, :], cs_), inc="eV")
            mv = C["eV"]
            W("vector", lambda e, qe=qe: e.tensor_sub(
                qe[:, :, 0, :], rh(rt[0][:, :]), rh(rt[1][:, :])),
              waits=[("eV", mv), (sq_, 16 * (gt // 2))], inc="eV")
            W("vector", lambda e, qe=qe: e.tensor_add(
                qe[:, :, 1, :], rh(rt[2][:, :]), rh(rt[3][:, :])),
              inc="eV")
            ropeq = C["eV"]
            tok[("ropeqt", gt)] = ropeq
            ke = kr_st[p].rearrange("q (h two i) -> q h two i", two=2, i=32)
            ks_ = kt_st[q4].rearrange("q (h two i) -> q h two i", two=2, i=32)
            sk_ = "sksA" if p == 0 else "sksB"
            W("gpsimd", lambda e, ks_=ks_, cs_=cs_: e.tensor_mul(
                rh(pt[0][:, :]), ks_[:, :, 0, :], cs_),
              waits=[("eA", pjev), (scs, csr), ("eV", wg_split),
                     ("eT", wgT_tp),
                     ("eP", tok.get(("ropekt", gt - 1), 0))], inc="eP")
            W("gpsimd", lambda e, ks_=ks_, sn_=sn_: e.tensor_mul(
                rh(pt[1][:, :]), ks_[:, :, 1, :], sn_), inc="eP")
            mp = C["eP"]
            W("gpsimd", lambda e, ke=ke: e.tensor_sub(
                ke[:, :, 0, :], rh(pt[0][:, :]), rh(pt[1][:, :])),
              waits=[("eP", mp), (sk_, 16 * (gt // 2))], inc="eP")
            W("gpsimd", lambda e, ks_=ks_, sn_=sn_: e.tensor_mul(
                rh(pt[0][:, :]), ks_[:, :, 0, :], sn_),
              waits=[("eP", C["eP"])], inc="eP")
            W("gpsimd", lambda e, ks_=ks_, cs_=cs_: e.tensor_mul(
                rh(pt[1][:, :]), ks_[:, :, 1, :], cs_), inc="eP")
            mp = C["eP"]
            W("gpsimd", lambda e, ke=ke: e.tensor_add(
                ke[:, :, 1, :], rh(pt[0][:, :]), rh(pt[1][:, :])),
              waits=[("eP", mp)], inc="eP")
            ropek = C["eP"]
            tok[("ropekt", gt)] = ropek
            dma("sync", q_sp[s, t * 128:(t + 1) * 128, :],
                qr_st[p][:, :], sq_, waits=[("eV", ropeq)])
            dma("gpsimd", k_sp[s, t * 128:(t + 1) * 128, :],
                kr_st[p][:, :], sk_, waits=[("eP", ropek)])

        def _wtsp_fill(ee, c2, wbt):
            sl2 = wbt % 2
            wts = [("swe", 16 * (ee + 1)), ("swb", 32 * wbt)]
            if wbt >= 2:
                wts.append(("swc", 16 * (wbt - 1)))
            for j in range(2):
                c = c2 * 2 + j
                dma("sync", bnc[sl2][:, j * 1024:(j + 1) * 1024],
                    we16[ee, :, c * 128:(c + 1) * 128], "swb",
                    transpose=True, waits=wts)
                wts = []
            dma("scalar", weTsp_h[ee, :, c2 * 2048:(c2 + 1) * 2048],
                bnc[sl2][:, :], "swc",
                waits=[("swb", 32 * (wbt + 1)), ("swc", 16 * wbt)])
        wtsp_units = []
        _wbt = 0
        for ee in range(NE):
            for c2 in range(4):
                wtsp_units.append(
                    lambda ee=ee, c2=c2, w=_wbt: _wtsp_fill(ee, c2, w))
                _wbt += 1
        _wti = [0]

        def drain_wtsp(n):
            while _wti[0] < len(wtsp_units) and n > 0:
                wtsp_units[_wti[0]]()
                _wti[0] += 1
                n -= 1

        ln_s0(0)
        ln_s0(1)
        for it in range(NT + 8):
            if it + 2 < NT:
                ln_s0(it + 2)
            if it < NT:
                ln_s1(it)
            if 5 <= it < NT + 5:
                ln_s6(it - 5)
            if 1 <= it < NT + 1:
                ln_s2(it - 1)
            if 2 <= it < NT + 2:
                ln_s3(it - 2)
            if 3 <= it < NT + 3:
                ln_s4(it - 3)
            if 4 <= it < NT + 4:
                ln_s5(it - 4)
            if 8 <= it < 8 + 2 * NE and (it - 8) % 2 == 0:
                ei = (it - 8) // 2
                dma("gpsimd", we16[ei, :, :], we_h[ei, :, :], "swe",
                    waits=[("swe", 16 * ei)])
            if it == 8 + 2 * NE:
                dma("gpsimd", be16s[0:8, :], be_h[:, :], "sbe")
            if 6 <= it < NT + 6:
                gsp = it - 6
                dma("scalar",
                    xnTs_h.rearrange("p (c w) -> p c w", c=NCH)
                    [:, :, gsp * 128:(gsp + 1) * 128],
                    xnT.rearrange("p (c w) -> p c w", c=NCH)
                    [:, :, gsp * 128:(gsp + 1) * 128], "sxnT",
                    waits=[("sxnT", 16 * gsp), ("eV", tok[("xnTv", gsp)]),
                           ("eA", tok[("xnTa", gsp)])])
            if it >= 8:
                p1b(it - 8)
            if 6 <= it < NT + 6:
                p1a(it - 6)
            if it >= 12:
                drain_wtsp(2)

        drain_wtsp(len(wtsp_units))
        wcT = C["swc"]

        for t in range(NT):
            for c in range(NCH):
                for s3 in range(3):
                    W("tensor",
                      lambda e, t=t, c=c, s3=s3: e.matmul(
                          PC6[:, t * 8:(t + 1) * 8],
                          xnT[:, c * T + t * 128:c * T + (t + 1) * 128],
                          wgT[:, c * 24 + s3 * 8:c * 24 + s3 * 8 + 8],
                          start=(c == 0 and s3 == 0),
                          stop=(c == NCH - 1 and s3 == 2)),
                      waits=[("eA", max(C["eA"] if t == 0 and c == 0 and s3 == 0 else 0, wgT_done))], inc="eT")
        W("vector", lambda e: e.tensor_copy(lgall[:, :], PC6[:, 0:256]),
          waits=[("eT", C["eT"]), ("eV", C["eV"])], inc="eV")
        d3 = lambda a: a.rearrange("q (t e) -> q t e", e=8)
        bc = lambda a: a.unsqueeze(2).broadcast_to([128, NT, 8])
        W("vector", lambda e: e.tensor_copy(lgall[:, :], PC6[:, 0:256]),
          waits=[("eT", C["eT"]), ("eV", C["eV"])], inc="eV")
        W("vector", lambda e: e.reduce_max(v1a[:, :], d3(lgall), axis=AX.X),
          waits=[("eV", C["eV"])], inc="eV")
        W("vector", lambda e: e.tensor_tensor(
            d3(eq16[:, 0:256]), d3(lgall), bc(v1a[:, :]), op=ALU.is_equal),
          waits=[("eV", C["eV"])], inc="eV")
        W("vector", lambda e: e.scalar_tensor_tensor(
            l2a[:, :], eq16[:, 0:256], -1e30, lgall[:, :],
            op0=ALU.mult, op1=ALU.add),
          waits=[("eV", C["eV"])], inc="eV")
        W("vector", lambda e: e.reduce_max(v2a[:, :], d3(l2a), axis=AX.X),
          waits=[("eV", C["eV"])], inc="eV")
        W("vector", lambda e: e.tensor_tensor(
            d3(eq16[:, 256:512]), d3(l2a), bc(v2a[:, :]), op=ALU.is_equal),
          waits=[("eV", C["eV"])], inc="eV")
        W("vector", lambda e: e.tensor_sub(dst[:, :], v2a[:, :], v1a[:, :]),
          waits=[("eV", C["eV"])], inc="eV")

        W("scalar", lambda e: e.activation(edt[:, :], dst[:, :], AF.Exp),
          waits=[("eV", C["eV"])], inc="eA")
        W("vector", lambda e: e.tensor_scalar(dent[:, :], edt[:, :], 1.0,
                                              None, op0=ALU.add),
          waits=[("eA", C["eA"])], inc="eV")
        W("vector", lambda e: e.reciprocal(w1t[:, :], dent[:, :]),
          waits=[("eV", C["eV"])], inc="eV")
        W("vector", lambda e: e.tensor_mul(w2t[:, :], edt[:, :], w1t[:, :]),
          waits=[("eV", C["eV"])], inc="eV")
        W("vector", lambda e: e.tensor_copy(w116[:, :], w1t[:, :]),
          waits=[("eV", C["eV"])], inc="eV")
        W("vector", lambda e: e.tensor_copy(w216[:, :], w2t[:, :]),
          waits=[("eV", C["eV"])], inc="eV")
        W("vector", lambda e: e.tensor_copy(w132[:, :], w116[:, :]),
          waits=[("eV", C["eV"])], inc="eV")
        W("vector", lambda e: e.tensor_copy(w232[:, :], w216[:, :]),
          waits=[("eV", C["eV"])], inc="eV")
        for t in range(NT):
            W("vector", lambda e, t=t: e.tensor_scalar(
                tmp8[:, :], eq16[:, 256 + t * 8:256 + (t + 1) * 8],
                w232[:, t:t + 1], None, op0=ALU.mult),
              waits=[("eV", C["eV"])], inc="eV")
            W("vector", lambda e, t=t: e.scalar_tensor_tensor(
                wm32[:, t * 8:(t + 1) * 8], eq16[:, t * 8:(t + 1) * 8],
                w132[:, t:t + 1], tmp8[:, :], op0=ALU.mult, op1=ALU.add),
              waits=[("eV", C["eV"])], inc="eV")
            W("vector", lambda e, t=t: e.tensor_copy(
                wm16[:, t * 8:(t + 1) * 8], wm32[:, t * 8:(t + 1) * 8]),
              waits=[("eV", C["eV"])], inc="eV")
            W("vector", lambda e, t=t: e.tensor_copy(
                wm32[:, t * 8:(t + 1) * 8], wm16[:, t * 8:(t + 1) * 8]),
              waits=[("eV", C["eV"])], inc="eV")
            W("tensor", lambda e, t=t: e.transpose(
                pb7f16[0:8, 0:128], wm16[:, t * 8:(t + 1) * 8], idn[:, :]),
              waits=[("eV", C["eV"]), ("eA", C["eA"])], inc="eT")
            W("scalar", lambda e, t=t: e.copy(
                wmT[0:8, t * 128:(t + 1) * 128], pb7f16[0:8, 0:128]),
              waits=[("eT", C["eT"])], inc="eA")

        P.barrier()
        spw = [("sqsA", C["sqsA"]), ("sqsB", C["sqsB"]),
               ("sksA", C["sksA"]), ("sksB", C["sksB"]),
               ("svsA", C["svsA"]), ("svsB", C["svsB"]),
               ("sxnT", C["sxnT"]), ("sxA", C["sxA"]), ("sxB", C["sxB"]),
               ("sxC", C["sxC"]), ("sxD", C["sxD"]),
               ("scsA", C["scsA"]), ("scsB", C["scsB"]),
               ("scsC", C["scsC"]), ("scsD", C["scsD"])]

        dma("scalar", ow128[:, :].rearrange("k (c n) -> k c n", c=NCH),
            ow_h.rearrange("(c k) n -> k c n", k=128), "sow",
            waits=P.bw() + spw)
        oww = C["sow"]
        bew = C["sbe"]

        def xnq_load(k):
            dma("gpsimd", xnq[k % 2][:, :].rearrange("p (c w) -> p c w",
                                                     c=NCH),
                xnTs_h.rearrange("p (c w) -> p c w", c=NCH)
                [:, :, k * 512:(k + 1) * 512], "sgq",
                waits=P.bw() + spw +
                [("sgq", 16 * k), ("sxnT", 16 * (4 * k + 4)),
                 ("eT", tok.get(("xnq_free", k % 2), 0))])
            tok[("xnq_ld", k)] = C["sgq"]

        def wexp_load(k, e):
            ring = "scalar" if e % 2 == 0 else "gpsimd"
            sw = "sweA" if e % 2 == 0 else "sweB"
            wts = P.bw() + spw + [("swc", wcT),
                                  ("eT", tok.get(("wfree", k, e), 0))]
            for hh in range(4):
                dma(ring, weT[e % 2][:, hh * 2048:(hh + 1) * 2048],
                    weTsp_h[e, :, hh * 2048:(hh + 1) * 2048], sw, waits=wts)
                wts = []
            tok[("wld", k, e)] = C[sw]

        def p2_units(k):
            units = []
            sl = k % 2
            if k + 1 < 2 * NQG:
                units.append(lambda k=k: xnq_load(k + 1))

            def bias_unit(t, half, k=k):
                gt = 4 * k + t
                W("tensor", lambda e, gt=gt, half=half: e.matmul(
                    PC6[:, :], wmT[0:8, gt * 128:(gt + 1) * 128],
                    be16s[0:8, half * 512:(half + 1) * 512],
                    start=True, stop=True),
                  waits=[("sbe", bew), ("eA", tok.get("pc6a", 0)),
                         ("eV", tok.get("pc6v", 0))], inc="eT")
                W("scalar", lambda e, t=t, half=half: e.copy(
                    acc32[t][:, half * 512:(half + 1) * 512], PC6[:, :]),
                  waits=P.bw() + spw + [("eT", C["eT"]),
                                  ("eA", tok.get(("accfree", t), 0))] +
                  ([("soA", tok.get(("odma", 0), 0)),
                    ("soB", tok.get(("odma", 1), 0))] if t == 0 else []),
                  inc="eA")
                tok["pc6a"] = C["eA"]
                tok[("accinit", t, half)] = C["eA"]
            for t in range(4):
                for half in range(2):
                    units.append(lambda t=t, half=half: bias_unit(t, half))

            def mm_unit(e, t, half, k=k, sl=sl):
                be_ = e % 2
                sw = "sweA" if be_ == 0 else "sweB"
                wrd = tok[("wld", k, e)]
                gt = 4 * k + t
                for c in range(NCH):
                    wts = [(sw, wrd), ("sgq", tok[("xnq_ld", k)])]
                    if c == 0:
                        wts.append(("eV", tok.get(("pbfree", be_), 0)))
                    W("tensor", lambda e2, c=c, t=t, half=half, be_=be_,
                      sl=sl: e2.matmul(
                          PB[:, 512 * be_:512 * be_ + 512],
                          xnq[sl][:, c * 512 + t * 128:
                                  c * 512 + (t + 1) * 128],
                          weT[be_][:, c * D + half * 512:
                                   c * D + (half + 1) * 512],
                          start=(c == 0), stop=(c == NCH - 1)),
                      waits=wts, inc="eT")
                mm = C["eT"]
                if t == 3 and half == 1:
                    if e + 2 < NE:
                        tok[("wfree", k, e + 2)] = mm
                    else:
                        tok[("wfree", k + 1, e - 6)] = mm
                    if e == NE - 1:
                        tok[("xnq_free", sl)] = mm
                wts2 = [("eT", mm), ("eV", C["eV"])]
                if e == 0:
                    wts2.append(("eA", tok[("accinit", t, half)]))
                W("vector", lambda e2, t=t, half=half, be_=be_, gt=gt,
                  e=e: e2.scalar_tensor_tensor(
                      acc32[t][:, half * 512:(half + 1) * 512],
                      PB[:, 512 * be_:512 * be_ + 512],
                      wm32[:, gt * 8 + e:gt * 8 + e + 1],
                      acc32[t][:, half * 512:(half + 1) * 512],
                      op0=ALU.mult, op1=ALU.add),
                  waits=wts2, inc="eV")
                tok[("pbfree", be_)] = C["eV"]
                if e == NE - 1 and half == 1:
                    tok[("sttdone", t)] = C["eV"]
            for e in range(NE):
                for t in range(4):
                    for half in range(2):
                        units.append(
                            lambda e=e, t=t, half=half: mm_unit(e, t, half))
                if e + 2 < NE:
                    units.append(lambda e=e, k=k: wexp_load(k, e + 2))
                elif k + 1 < 2 * NQG:
                    units.append(lambda e=e, k=k: wexp_load(k + 1, e - 6))

            def silu_unit(t, k=k, sl=sl):
                W("scalar", lambda e, t=t, sl=sl: e.activation(
                    gsl[sl][:, t * 1024:(t + 1) * 1024], acc32[t][:, :],
                    AF.Silu),
                  waits=P.bw() + spw + [("eV", tok[("sttdone", t)]),
                                  ("eA", C["eA"]),
                                  ("eV", tok.get(("gslfree", sl), 0))],
                  inc="eA")
                tok[("accfree", t)] = C["eA"]
                if t == 3:
                    tok[("sild", k)] = C["eA"]
            for t in range(4):
                units.append(lambda t=t: silu_unit(t))
            return units

        def attn_units(s, qg):
            au = []
            nkc = 4 * qg + 4
            ktr = tok[("ktr", s)]
            vld = tok[("vld", s)]
            qrd = tok["qrd"]

            def emit_pv(h, kc):
                p2 = h % 2
                j = kc - 4 * qg
                sbk = kc % 3
                w0 = max(0, j) * 128
                mw = 65 if h == NH - 1 else 128
                pvw = [("eA", tok.get(("expC", kc), 0)), ("svld", vld)]
                if j >= 0:
                    pvw.append(("eP", tok.get(("maskdone", sbk), 0)))
                if kc == 0:
                    pvw.append(("eA", tok.get("pvrecip", 0)))
                    pvw.append(("eV", tok.get("pvevac", 0)))
                W("tensor", lambda e, kc=kc, h=h, sbk=sbk, w0=w0,
                  nkc=nkc, p2=p2, mw=mw: e.matmul(
                      PV2[0:mw, 512 * p2 + w0:512 * p2 + 512],
                      v_sb[:, kc * VF + h * VW:kc * VF + h * VW + mw],
                      expb3[sbk][:, 512 * p2 + w0:512 * p2 + 512],
                      start=(kc == 0), stop=(kc == nkc - 1)),
                  waits=pvw, inc="eT")
                tok[("exprd", sbk)] = C["eT"]

            def emit_tail_a(hp):
                pvmm = tok[("pvmm", hp)]
                W("scalar", lambda e: e.activation(
                    recL[96:97, 0:1024], PV2[64:65, 0:1024], AF.Ln),
                  waits=[("swc", wcT), ("eT", pvmm), ("eA", C["eA"]),
                         ("eA", tok.get("lnrd", 0))], inc="eA")
                W("scalar", lambda e: e.activation(
                    recE[64:65, 0:1024], recL[96:97, 0:1024],
                    AF.Exp, scale=-1.0),
                  waits=[("eA", C["eA"]),
                         ("eT", tok.get("rec16rd", 0))], inc="eA")
                tok["pvrecip"] = C["eA"]
                tok[("recv", hp)] = C["eA"]
                tok["lnrd"] = C["eA"]
                W("vector", lambda e: e.tensor_copy(
                    at2[0:64, :], PV2[0:64, 0:1024]),
                  waits=[("swc", wcT), ("eT", pvmm),
                         ("eV", tok.get("atrd", 0))], inc="eV")
                tok["pvevac"] = C["eV"]
                tok[("atv", hp)] = C["eV"]

            def emit_tail_b(hp):
                W("tensor", lambda e: e.matmul(
                    PC6[0:64, :], ones16[64:65, 0:64],
                    recE[64:65, 0:512], start=True, stop=True),
                  waits=[("eA", tok[("recv", hp)]),
                         ("eA", tok.get("pc6a", 0)),
                         ("eV", tok.get("pc6v", 0))], inc="eT")
                W("tensor", lambda e: e.matmul(
                    PC7[0:64, :], ones16[64:65, 0:64],
                    recE[64:65, 512:1024], start=True, stop=True),
                  waits=[("eV", tok.get("bcrd", 0))], inc="eT")
                tok["rec16rd"] = C["eT"]
                bcmm = C["eT"]
                W("vector", lambda e, hp=hp: e.tensor_mul(
                    attnT[0:64, hp * 512:(hp + 1) * 512],
                    at2[0:64, 0:512], PC6[0:64, :]),
                  waits=[("eV", tok[("atv", hp)]), ("eT", bcmm),
                         ("eT", tok.get("atT_rd", 0)), ("eV", C["eV"])],
                  inc="eV")
                tok["pc6v"] = C["eV"]
                W("vector", lambda e, hp=hp: e.tensor_mul(
                    attnT[64:128, hp * 512:(hp + 1) * 512],
                    at2[0:64, 512:1024], PC7[0:64, :]),
                  inc="eV")
                tok["atrd"] = C["eV"]
                tok["bcrd"] = C["eV"]

            def score_unit(hp, kc):
                ch = hp
                sbk = kc % 3
                j = kc - 4 * qg
                w0 = max(0, j) * 128
                for p2 in range(2):
                    base = p2 * 64
                    wts = [("skT", ktr), ("sqTA", qrd)]
                    if p2 == 0:
                        wts.append(("eA", tok.get("expdone", 0)))
                        if hp == 0 and kc == 0:
                            wts.append(("soA", tok.get("outsA", 0)))
                            wts.append(("soB", tok.get("outsB", 0)))
                    W("tensor", lambda e, base=base, ch=ch, kc=kc,
                      p2=p2: e.matmul(
                          PA[:, 512 * p2:512 * p2 + 512],
                          kT[base:base + 64,
                             ch * S + kc * 128:ch * S + (kc + 1) * 128],
                          qTb[base:base + 64, ch * 512:(ch + 1) * 512],
                          start=True, stop=True),
                      waits=wts, inc="eT")
                ev = (lambda a: a.rearrange("q (hh c) -> q hh c", hh=2)
                      [:, :, w0:512])
                W("scalar", lambda e, sbk=sbk, ev=ev: e.activation(
                    ev(expb3[sbk]), ev(PA), AF.Exp, scale=1.0 / 32.0),
                  waits=[("swc", wcT), ("eT", C["eT"]),
                         ("eT", tok.get(("exprd", sbk), 0))],
                  inc="eA")
                tok["expdone"] = C["eA"]
                tok[("expC", kc)] = C["eA"]
                if j >= 0:
                    tv = (lambda a: a.rearrange("q (hh c) -> q hh c", hh=2)
                          [:, :, j * 128:(j + 1) * 128])
                    W("gpsimd", lambda e, sbk=sbk, tv=tv: e.tensor_mul(
                        tv(expb3[sbk]), tv(expb3[sbk]),
                        tri[:, :].unsqueeze(1).broadcast_to([128, 2, 128])),
                      waits=[("swc", wcT), ("eA", C["eA"]), ("scst", cst),
                             ("eP", C["eP"])], inc="eP")
                    tok[("maskdone", sbk)] = C["eP"]
                if kc >= 2:
                    emit_pv(2 * hp, kc - 2)
                    emit_pv(2 * hp + 1, kc - 2)

            def hp_tail(hp):
                emit_pv(2 * hp, nkc - 2)
                emit_pv(2 * hp + 1, nkc - 2)
                emit_pv(2 * hp, nkc - 1)
                emit_pv(2 * hp + 1, nkc - 1)
                tok[("pvmm", hp)] = C["eT"]
                if hp >= 1:
                    emit_tail_b(hp - 1)
                emit_tail_a(hp)

            for hp in range(NH // 2):
                for kc in range(nkc):
                    au.append(lambda hp=hp, kc=kc: score_unit(hp, kc))
                au.append(lambda hp=hp: hp_tail(hp))
            au.append(lambda: emit_tail_b(NH // 2 - 1))
            return au

        def combine(s, qg, k):
            heads_done = C["eV"]
            r0 = s * S + qg * 512
            sild = tok[("sild", k)]
            for pc in range(2):
                dma("gpsimd", xq2[pc][:, :],
                    x_h[r0 + (pc // 2) * 128:r0 + (pc // 2 + 1) * 128,
                        (pc % 2) * 512:(pc % 2 + 1) * 512], "sxq",
                    waits=[("sxq", C["sxq"]),
                           ("eV", tok.get(("xqfree", pc % 2), 0))])
                tok[("xql", pc)] = C["sxq"]
            for jt in range(4):
                for half in range(2):
                    pc = jt * 2 + half
                    for cp in range(NCH):
                        wts = [("sow", oww), ("eV", heads_done)]
                        if cp == 0:
                            wts.append(("eA", tok["expdone"]))
                            pf = tok.get(("pafree", half))
                            if pf is not None:
                                wts.append(pf)
                        W("tensor", lambda e, jt=jt, half=half, cp=cp:
                          e.matmul(
                              PA[:, 512 * half:512 * half + 512],
                              attnT[0:128, cp * 512 + jt * 128:
                                    cp * 512 + (jt + 1) * 128],
                              ow128[:, cp * D + half * 512:
                                    cp * D + (half + 1) * 512],
                              start=(cp == 0), stop=(cp == NCH - 1)),
                          waits=wts, inc="eT")
                    omm = C["eT"]
                    so = "soA" if pc % 2 == 0 else "soB"
                    ob = acc32[0][:, (pc % 2) * 512:(pc % 2) * 512 + 512]
                    mw_ = [("eT", omm), ("eA", sild), ("eV", C["eV"]),
                           (so, tok.get(("odma", pc % 2), 0))]
                    W("vector", lambda e, jt=jt, half=half, sl=k % 2,
                      ob=ob: e.tensor_tensor(
                          ob,
                          PA[:, 512 * half:512 * half + 512],
                          gsl[sl][:, jt * 1024 + half * 512:
                                  jt * 1024 + (half + 1) * 512],
                          op=ALU.mult),
                      waits=mw_, inc="eV")
                    tok[("pafree", half)] = ("eV", C["eV"])
                    W("vector", lambda e, pc=pc, ob=ob:
                      e.tensor_tensor(ob, ob, xq2[pc % 2][:, :],
                                      op=ALU.add),
                      waits=[("sxq", tok[("xql", pc)]), ("eV", C["eV"])],
                      inc="eV")
                    tok[("xqfree", pc % 2)] = C["eV"]
                    dma("sync",
                        out_h[r0 + jt * 128:r0 + (jt + 1) * 128,
                              half * 512:(half + 1) * 512], ob, so,
                        waits=[("eV", C["eV"])])
                    tok[("odma", pc % 2)] = C[so]
                    if pc + 2 < 8:
                        npc = pc + 2
                        njt, nhalf = npc // 2, npc % 2
                        dma("gpsimd", xq2[npc % 2][:, :],
                            x_h[r0 + njt * 128:r0 + (njt + 1) * 128,
                                nhalf * 512:(nhalf + 1) * 512], "sxq",
                            waits=[("sxq", C["sxq"]),
                                   ("eV", tok.get(("xqfree", npc % 2), 0))])
                        tok[("xql", npc)] = C["sxq"]
            tok["atT_rd"] = C["eT"]
            tok["outsA"] = C["soA"]
            tok["outsB"] = C["soB"]

        wexp_load(0, 0)
        wexp_load(0, 1)
        xnq_load(0)
        for s in range(NSEQ):
            kw_ = P.bw() + spw + [("eT", tok.get("kT_free", 0))]
            for c in range(NCH):
                dma("sync", c16(AR1, c * 4096, 2048)[:, :],
                    k_sp[s, :, c * 128:(c + 1) * 128], "skT", transpose=True,
                    waits=kw_)
                kw_ = []
            tok[("ktr", s)] = C["skT"]
            dma("gpsimd", v_sb[:, :].rearrange("q (t f) -> q t f", t=NTS),
                v_sp[s].rearrange("(t q) f -> q t f", q=128), "svld",
                waits=P.bw() + spw + [("eT", tok.get("v_free", 0))])
            tok[("vld", s)] = C["svld"]
            for qg in range(NQG):
                k = s * NQG + qg
                qw_ = P.bw() + spw + [("eT", tok.get("qT_free", 0))]
                for c in range(NCH):
                    dma("sync", qTb[:, c * 512:(c + 1) * 512],
                        q_sp[s, qg * 512:(qg + 1) * 512,
                             c * 128:(c + 1) * 128], "sqTA",
                        transpose=True, waits=qw_)
                    qw_ = []
                tok["qrd"] = C["sqTA"]
                au = attn_units(s, qg)
                pu = p2_units(k)
                j = 0
                for i, u in enumerate(au):
                    share = ((i + 1) * len(pu)) // len(au)
                    while j < share:
                        pu[j]()
                        j += 1
                    u()
                while j < len(pu):
                    pu[j]()
                    j += 1
                tok["qT_free"] = C["eT"]
                combine(s, qg, k)
            tok["kT_free"] = C["eT"]
            tok["v_free"] = C["eT"]

        W("scalar", None, waits=[("soA", C["soA"]), ("soB", C["soB"])])
        P.barrier()

        with nc.Block() as blk:
            P.replay(blk)
    return nc


_CACHE = {}


def _constants():
    if "c" in _CACHE:
        return _CACHE["c"]
    inv = 1.0 / (10000.0 ** (np.arange(0, HD, 2, dtype=np.float64) / HD))
    pos = np.arange(S, dtype=np.float64)
    fr = pos[:, None] * inv[None, :]
    cos = np.tile(np.cos(fr), (1, NH)).astype(np.float16)
    sin = np.tile(np.sin(fr), (1, NH)).astype(np.float16)
    tri = np.triu(np.ones((128, 128), dtype=np.float16))
    idn = np.eye(128, dtype=np.float16)
    _CACHE["c"] = (cos, sin, tri, idn)
    return _CACHE["c"]


def kernel(L0_attn_x, L0_attn_qw, L0_attn_kw, L0_attn_vw, L0_attn_ow,
           L0_ffn_Wg, L0_ffn_We, L0_ffn_be):
    x = np.ascontiguousarray(np.asarray(L0_attn_x, dtype=np.float16))
    qw = np.ascontiguousarray(np.asarray(L0_attn_qw, dtype=np.float16))
    kw = np.ascontiguousarray(np.asarray(L0_attn_kw, dtype=np.float16))
    vw = np.ascontiguousarray(np.asarray(L0_attn_vw, dtype=np.float16))
    ow = np.ascontiguousarray(np.asarray(L0_attn_ow, dtype=np.float16))
    wg = np.ascontiguousarray(np.asarray(L0_ffn_Wg, dtype=np.float32))
    we = np.ascontiguousarray(np.asarray(L0_ffn_We, dtype=np.float32))
    be = np.ascontiguousarray(np.asarray(L0_ffn_be, dtype=np.float32))
    cos, sin, tri, idn = _constants()

    nc = build_nc()
    in_maps = []
    for i in range(NCORE):
        in_maps.append({
            "x_c": np.ascontiguousarray(
                x[2 * i:2 * i + 2].reshape(T, D)),
            "qw": qw, "kw": kw, "vw": vw, "ow": ow,
            "wg": wg, "we": we, "be": be,
            "c_cos": cos, "c_sin": sin, "c_tri": tri, "c_idn": idn,
        })
    res = run_bass_kernel_spmd(nc, in_maps, core_ids=list(range(NCORE)))
    out = np.concatenate(
        [np.asarray(r["out_c"]) for r in res.results], axis=0)
    return out.reshape(16, S, D).astype(np.float32, copy=False)


# version 23
# speedup vs baseline: 1.0156x; 1.0080x over previous
import contextlib

import numpy as np

import concourse.bass as bass
import concourse.mybir as mybir
from concourse.bass_utils import run_bass_kernel_spmd

F32 = mybir.dt.float32
F16 = mybir.dt.float16
U8 = mybir.dt.uint8
ALU = mybir.AluOpType
AF = mybir.ActivationFunctionType
AX = mybir.AxisListType

NCORE = 8
D = 1024
S = 2048
NSEQ = 2
T = NSEQ * S
NT = T // 128
NTS = S // 128
NH = 16
HD = 64
NCH = 8
NE = 8
NQG = 4
EPS16 = float(np.float16(1e-5))
VW = 66
VF = NH * VW


class Prog:

    def __init__(self, nc):
        self.nc = nc
        self.ops = []
        self.sems = {}
        self.cnt = {}
        self.wm = {}

    def decl(self, ctx, *names):
        for n in names:
            self.sems[n] = ctx.enter_context(self.nc.semaphore(n))
            self.cnt[n] = 0

    def emit(self, eng, fn, waits=(), inc=None, amt=1):
        ws = []
        for (sn, v) in waits:
            if v is None or v <= 0:
                continue
            k = (eng, sn)
            if self.wm.get(k, 0) >= v:
                continue
            self.wm[k] = v
            ws.append((sn, v))
        incs = []
        if inc is not None:
            self.cnt[inc] += amt
            incs.append((inc, amt))
        self.ops.append((eng, ws, fn, incs))
        return self.cnt[inc] if inc is not None else None

    def barrier(self):
        engs = ("vector", "scalar", "tensor", "gpsimd")
        marks = {}
        for e in engs:
            sn = "bar_" + e
            self.cnt[sn] += 1
            self.ops.append((e, [], None, [(sn, 1)]))
            marks[e] = self.cnt[sn]
        for e in engs:
            self.emit(e, None,
                      waits=[("bar_" + o, marks[o]) for o in engs if o != e])
        self.last_bar = [("bar_" + o, marks[o]) for o in engs]

    def bw(self):
        return list(getattr(self, "last_bar", []))

    def replay(self, blk):
        sems, ops = self.sems, self.ops

        def runner(name):
            def body(eng):
                for (e, ws, fn, incs) in ops:
                    if e != name:
                        continue
                    for (sn, v) in ws:
                        eng.wait_ge(sems[sn], v)
                    if fn is None:
                        if incs:
                            ins = eng.drain()
                            for (sn, a) in incs:
                                ins.then_inc(sems[sn], a)
                        continue
                    ins = fn(eng)
                    for (sn, a) in incs:
                        ins.then_inc(sems[sn], a)
            return body

        blk.vector(runner("vector"))
        blk.scalar(runner("scalar"))
        blk.tensor(runner("tensor"))
        blk.gpsimd(runner("gpsimd"))
        blk.sync(runner("sync"))


def build_nc():
    nc = bass.Bass("TRN2", target_bir_lowering=False)

    x_h = nc.dram_tensor("x_c", [T, D], F16, kind="ExternalInput").ap()
    qw_h = nc.dram_tensor("qw", [D, D], F16, kind="ExternalInput").ap()
    kw_h = nc.dram_tensor("kw", [D, D], F16, kind="ExternalInput").ap()
    vw_h = nc.dram_tensor("vw", [D, D], F16, kind="ExternalInput").ap()
    ow_h = nc.dram_tensor("ow", [D, D], F16, kind="ExternalInput").ap()
    wg_h = nc.dram_tensor("wg", [NE, D], F32, kind="ExternalInput").ap()
    we_h = nc.dram_tensor("we", [NE, D, D], F32, kind="ExternalInput").ap()
    be_h = nc.dram_tensor("be", [NE, D], F32, kind="ExternalInput").ap()
    cos_h = nc.dram_tensor("c_cos", [S, 512], F16, kind="ExternalInput").ap()
    sin_h = nc.dram_tensor("c_sin", [S, 512], F16, kind="ExternalInput").ap()
    tri_h = nc.dram_tensor("c_tri", [128, 128], F16, kind="ExternalInput").ap()
    idn_h = nc.dram_tensor("c_idn", [128, 128], F16, kind="ExternalInput").ap()
    out_h = nc.dram_tensor("out_c", [T, D], F32, kind="ExternalOutput").ap()

    xn_sp = nc.dram_tensor("xn_sp", [T, D], F16, kind="Internal").ap()
    q_sp = nc.dram_tensor("q_sp", [NSEQ, S, D], F16, kind="Internal").ap()
    k_sp = nc.dram_tensor("k_sp", [NSEQ, S, D], F16, kind="Internal").ap()
    v_sp = nc.dram_tensor("v_sp", [NSEQ, S, VF], F16, kind="Internal").ap()
    g_sp = nc.dram_tensor("g_sp", [T, D], F16, kind="Internal").ap()
    we16 = nc.dram_tensor("we16", [NE, D, D], F16, kind="Internal").ap()
    xnTs_h = nc.dram_tensor("xnTs", [128, NCH * T], F16, kind="Internal").ap()
    weTsp_h = nc.dram_tensor("weTsp", [NE, 128, NCH * D], F16,
                             kind="Internal").ap()

    ctx = contextlib.ExitStack()
    with ctx:
        sbt = lambda name, shape, dt: ctx.enter_context(
            nc.sbuf_tensor(name, shape, dt))[:]
        AR1 = sbt("AR1", [128, 65536], U8)
        AR2 = sbt("AR2", [128, 125952], U8)

        def c16(ar, off, cols):
            return ar[:, off:off + 2 * cols].bitcast(F16)

        def c32(ar, off, cols):
            return ar[:, off:off + 4 * cols].bitcast(F32)

        xnT = c16(AR1, 0, 32768)
        kT = c16(AR1, 0, 16384)
        qTb = c16(AR1, 32768, 4096)
        attnT = c16(AR1, 40960, 4096)
        ow128 = c16(AR1, 49152, 8192)

        xst = [c16(AR2, 92672 + 2048 * i, 1024) for i in range(4)]
        xcn16 = [c16(AR2, 100864 + 2048 * i, 1024) for i in range(4)]
        sqd16 = c16(AR2, 109056, 1024)
        lgall = c32(AR2, 111104, 256)
        l2a = c32(AR2, 112128, 256)
        m32 = [c32(AR2, 111104 + 32 * i, 1) for i in range(4)]
        m16 = [c16(AR2, 111232 + 32 * i, 1) for i in range(4)]
        mneg = [c32(AR2, 111360 + 32 * i, 1) for i in range(4)]
        ssum = [c32(AR2, 111488 + 32 * i, 1) for i in range(4)]
        v16t = [c16(AR2, 111616 + 32 * i, 1) for i in range(4)]
        s16t = [c16(AR2, 111744 + 32 * i, 1) for i in range(4)]
        r16t = [c16(AR2, 111872 + 32 * i, 1) for i in range(4)]
        rc32 = [c32(AR2, 112000 + 32 * i, 1) for i in range(4)]
        qwS = c16(AR2, 0, 8192)
        kwS = c16(AR2, 16384, 8192)
        vwS = c16(AR2, 32768, 8192)
        cosb = [c16(AR2, 49152 + 2048 * i, 1024) for i in range(4)]
        qt_st = [c16(AR2, 57344 + 2048 * i, 1024) for i in range(4)]
        kt_st = [c16(AR2, 65536 + 2048 * i, 1024) for i in range(4)]
        vt_st = [c16(AR2, 73728, VF), c16(AR2, 76032, VF)]
        qr_st = [c16(AR2, 78336, 1024), c16(AR2, 80384, 1024)]
        kr_st = [c16(AR2, 82432, 1024), c16(AR2, 84480, 1024)]
        rt = [c16(AR2, 86528 + 1024 * i, 512) for i in range(4)]
        pt = [c16(AR2, 90624 + 1024 * i, 512) for i in range(2)]
        wgf = c32(AR2, 78336, 1024)
        wgr = c32(AR2, 82432, 1024)
        dst = c32(AR2, 113152, 32)
        edt = c32(AR2, 113280, 32)
        dent = c32(AR2, 113408, 32)
        w1t = c32(AR2, 113536, 32)
        w2t = c32(AR2, 113664, 32)
        w132 = c32(AR2, 113792, 32)
        w232 = c32(AR2, 113920, 32)
        v1a = c32(AR2, 114048, 32)
        v2a = c32(AR2, 114176, 32)
        w116 = c16(AR2, 114304, 32)
        w216 = c16(AR2, 114368, 32)
        tmp8 = c32(AR2, 114432, 8)
        eq16 = c16(AR2, 114464, 512)
        bnc = [c16(AR2, 115712 + 4096 * i, 2048) for i in range(2)]
        v_sb = c16(AR2, 0, 16896)
        weT = [c16(AR2, 33792 + 16384 * i, 8192) for i in range(2)]
        xnq = [c16(AR2, 66560 + 8192 * i, 4096) for i in range(2)]
        acc32 = [c32(AR2, 82944 + 4096 * i, 1024) for i in range(4)]
        gsl = [c16(AR2, 99328 + 8192 * i, 4096) for i in range(2)]
        expb3 = [c16(AR2, 115712 + 2048 * i, 1024) for i in range(3)]
        at2 = c32(AR2, 121856, 1024)
        recL = c32(AR2, 121856, 1024)
        recE = c16(AR2, 121856, 2048)

        tri = sbt("tri", [128, 128], F16)
        idn = sbt("idn", [128, 128], F16)
        ones16 = sbt("ones16", [128, 64], F16)

        wm32 = sbt("wm32", [128, NT * 8], F32)
        wm16 = sbt("wm16", [128, NT * 8], F16)
        wmT = sbt("wmT", [8, NT * 128], F16)
        wgT = sbt("wgT", [128, NCH * 24], F16)
        wgs = sbt("wgs", [8, 3 * D], F16)
        be16s = sbt("be16s", [8, D], F16)
        xq2 = [sbt("xq2_%d" % i, [128, 512], F16) for i in range(2)]

        PA = ctx.enter_context(nc.psum_tensor("PA", [128, 1024], F32))[:]
        PB = ctx.enter_context(nc.psum_tensor("PB", [128, 1024], F32))[:]
        PV2 = ctx.enter_context(nc.psum_tensor("PV2", [128, 1024], F32))[:]
        PC6 = ctx.enter_context(nc.psum_tensor("PC6", [128, 512], F32))[:]
        PC7 = ctx.enter_context(nc.psum_tensor("PC7", [128, 512], F32))[:]
        pb7f16 = PC7.bitcast(F16)

        P = Prog(nc)
        P.decl(ctx, "eV", "eA", "eT", "eP",
               "bar_vector", "bar_scalar", "bar_tensor", "bar_gpsimd",
               "sxA", "sxB", "sxC", "sxD", "sspA", "sspB", "sspC", "sspD",
               "sxnT", "scst",
               "scsA", "scsB", "scsC", "scsD", "sw3", "sqsA", "sqsB", "sksA", "sksB",
               "svsA", "svsB", "swe", "sweA", "sweB", "sbe", "sgsA", "sgsB",
               "skT", "svld", "sqTA", "sqTB", "sgq", "sxq", "soA", "soB",
               "sow", "swb", "swc")
        W = P.emit
        C = P.cnt

        def dma(q, out, in_, sem, transpose=False, waits=()):
            def fn(e, out=out, in_=in_, tr=transpose):
                if tr:
                    return e.dma_start(out, in_, transpose=True)
                return e.dma_start(out, in_)
            return W(q, fn, waits=waits, inc=sem, amt=16)

        tok = {}

        dma("sync", qwS[:, :].rearrange("k (c n) -> k c n", c=NCH),
            qw_h.rearrange("(c k) n -> k c n", k=128), "sw3")
        dma("sync", kwS[:, :].rearrange("k (c n) -> k c n", c=NCH),
            kw_h.rearrange("(c k) n -> k c n", k=128), "sw3")
        dma("sync", vwS[:, :].rearrange("k (c n) -> k c n", c=NCH),
            vw_h.rearrange("(c k) n -> k c n", k=128), "sw3")
        dma("sync", tri[:, :], tri_h[:, :], "scst")
        dma("sync", idn[:, :], idn_h[:, :], "scst")
        dma("sync", wgf[0:8, :], wg_h[:, :], "scst")
        cst = C["scst"]
        W("vector", lambda e: e.memset(ones16[:, :], 1.0), inc="eV")

        W("vector", lambda e: e.tensor_copy(wgs[0:8, 0:D], wgf[0:8, :]),
          waits=[("scst", cst)], inc="eV")
        W("vector", lambda e: e.tensor_sub(wgr[0:8, :], wgf[0:8, :],
                                           wgs[0:8, 0:D]),
          waits=[("eV", C["eV"])], inc="eV")
        W("vector", lambda e: e.tensor_copy(wgs[0:8, D:2 * D], wgr[0:8, :]),
          waits=[("eV", C["eV"])], inc="eV")
        W("vector", lambda e: e.tensor_sub(wgr[0:8, :], wgr[0:8, :],
                                           wgs[0:8, D:2 * D]),
          waits=[("eV", C["eV"])], inc="eV")
        W("vector", lambda e: e.tensor_copy(wgs[0:8, 2 * D:3 * D], wgr[0:8, :]),
          waits=[("eV", C["eV"])], inc="eV")
        wg_split = C["eV"]
        for c in range(NCH):
            for s3 in range(3):
                W("tensor",
                  lambda e, c=c, s3=s3: e.transpose(
                      pb7f16[0:128, 0:8],
                      wgs[0:8, s3 * D + c * 128:s3 * D + (c + 1) * 128],
                      idn[0:8, 0:8]),
                  waits=[("eV", wg_split), ("scst", cst), ("eA", C["eA"])],
                  inc="eT")
                W("scalar",
                  lambda e, c=c, s3=s3: e.copy(
                      wgT[:, c * 24 + s3 * 8:c * 24 + s3 * 8 + 8],
                      pb7f16[0:128, 0:8]),
                  waits=[("eT", C["eT"])], inc="eA")
        wgT_done = C["eA"]
        wgT_tp = C["eT"]
        w3 = C["sw3"]

        SSP = ("sspA", "sspB", "sspC", "sspD")
        SX = ("sxA", "sxB", "sxC", "sxD")

        def ln_s0(t):
            q = t % 4
            dma("scalar", xst[q][:, :], x_h[t * 128:(t + 1) * 128, :], SX[q],
                waits=[("eA", tok.get(("xcd", q), 0))])

        def ln_s1(t):
            q = t % 4
            W("vector", lambda e, q=q: e.reduce_sum(m32[q][:, :],
                                                    xst[q][:, :], axis=AX.X),
              waits=[(SX[q], C[SX[q]]), ("eV", C["eV"])],
              inc="eV")
            W("vector", lambda e, q=q: e.tensor_scalar(
                m16[q][:, :], m32[q][:, :], 1.0 / D, None, op0=ALU.mult),
              waits=[("eV", C["eV"])], inc="eV")
            W("vector", lambda e, q=q: e.tensor_scalar(
                mneg[q][:, :], m16[q][:, :], -1.0, None, op0=ALU.mult),
              waits=[("eV", C["eV"])], inc="eV")
            tok[("mnegd", q)] = C["eV"]

        def ln_s2(t):
            q = t % 4
            W("scalar", lambda e, q=q: e.activation(
                xcn16[q][:, :], xst[q][:, :], AF.Identity,
                bias=mneg[q][:, :], scale=1.0),
              waits=[("eV", tok[("mnegd", q)]),
                     ("eT", tok.get(("xnrd", q), 0))], inc="eA")
            tok[("xcd", q)] = C["eA"]
            W("scalar", lambda e, q=q: e.activation(
                sqd16[:, :], xcn16[q][:, :], AF.Square,
                accum_out=ssum[q][:, :]),
              waits=[("eA", C["eA"]), ("eV", tok.get(("vrd", q), 0))],
              inc="eA")
            tok[("sqdone", q)] = C["eA"]

        def ln_s3(t):
            q = t % 4
            W("vector", lambda e, q=q: e.tensor_scalar(
                v16t[q][:, :], ssum[q][:, :], 1.0 / D, None, op0=ALU.mult),
              waits=[("eA", tok[("sqdone", q)]), ("eV", C["eV"])], inc="eV")
            tok[("vrd", q)] = C["eV"]
            W("vector", lambda e, q=q: e.tensor_scalar(
                s16t[q][:, :], v16t[q][:, :], EPS16, None, op0=ALU.add),
              waits=[("eV", C["eV"])], inc="eV")
            tok[("s16d", q)] = C["eV"]

        def ln_s4(t):
            q = t % 4
            W("scalar", lambda e, q=q: e.activation(r16t[q][:, :],
                                                    s16t[q][:, :], AF.Sqrt),
              waits=[("eV", tok[("s16d", q)]), ("eA", C["eA"])], inc="eA")
            tok[("sqrtd", q)] = C["eA"]

        def ln_s5(t):
            q = t % 4
            W("vector", lambda e, q=q: e.reciprocal(rc32[q][:, :],
                                                    r16t[q][:, :]),
              waits=[("eA", tok[("sqrtd", q)]), ("eV", C["eV"])], inc="eV")
            W("vector", lambda e, q=q: e.tensor_scalar(
                xcn16[q][:, :], xcn16[q][:, :], rc32[q][:, :], None,
                op0=ALU.mult),
              waits=[("eV", C["eV"]), ("eA", tok[("sqdone", q)])], inc="eV")
            tok[("xc16", q)] = C["eV"]

        PV2f16 = PV2.bitcast(F16)
        tp16 = [PV2f16[:, 0:128], PV2f16[:, 1024:1152]]

        def ln_s6(t):
            q = t % 4
            for c in range(NCH):
                bp = c % 2
                W("tensor", lambda e, q=q, c=c, bp=bp: e.transpose(
                    tp16[bp][:, :], xcn16[q][:, c * 128:(c + 1) * 128],
                    idn[:, :]),
                  waits=[("eV", tok[("xc16", q)]), ("scst", cst),
                         ("eV", tok.get(("tpev", bp), 0)),
                         ("eA", tok.get(("tpea", bp), 0))], inc="eT")
                eng = "vector" if c % 4 < 2 else "scalar"
                def ef(e, q=q, c=c, bp=bp, t=t):
                    dst = xnT[:, c * T + t * 128:c * T + (t + 1) * 128]
                    if hasattr(e, "activation"):
                        return e.copy(dst, tp16[bp][:, :])
                    return e.tensor_copy(dst, tp16[bp][:, :])
                W(eng, ef, waits=[("eT", C["eT"])],
                  inc="eV" if eng == "vector" else "eA")
                tok[("tpev" if eng == "vector" else "tpea", bp)] = \
                    C["eV" if eng == "vector" else "eA"]
            tok[("xnrd", q)] = C["eT"]
            tok[("xnTv", t)] = C["eV"]
            tok[("xnTa", t)] = C["eA"]


        SCS = ("scsA", "scsB", "scsC", "scsD")

        def p1a(gt):
            s, t = gt // NTS, gt % NTS
            p = gt % 2
            q4 = gt % 4
            scs = SCS[q4]
            dma("scalar", cosb[q4][:, 0:512], cos_h[t * 128:(t + 1) * 128, :],
                scs, waits=[("eV", tok.get(("ropeqt", gt - 4), 0)),
                            ("eP", tok.get(("ropekt", gt - 4), 0))])
            dma("scalar", cosb[q4][:, 512:1024],
                sin_h[t * 128:(t + 1) * 128, :], scs)
            tok[("csr", gt)] = C[scs]
            sv = "svsA" if p == 0 else "svsB"
            vv = vt_st[p].rearrange("q (h w) -> q h w", h=NH)
            for grp in range(2):
                for c in range(NCH):
                    qv = qwS[:, c * D:(c + 1) * D].rearrange(
                        "k (h i two) -> k h two i", two=2, i=32)
                    kv = kwS[:, c * D:(c + 1) * D].rearrange(
                        "k (h i two) -> k h two i", two=2, i=32)
                    lhs = xnT[:, c * T + gt * 128:c * T + (gt + 1) * 128]
                    hs = slice(8 * grp, 8 * grp + 8)
                    PG = PA if grp == 0 else PB
                    PVg = PC6 if grp == 0 else PC7
                    rhss = [qv[:, hs, :, :], kv[:, hs, :, :],
                            vwS[:, c * D + 512 * grp:
                                c * D + 512 * grp + 512]]
                    outs_ = [PG[:, 0:512], PG[:, 512:1024], PVg[:, :]]
                    for bi3 in range(3):
                        wts = [("sw3", w3), ("eV", tok[("xnTv", gt)]),
                               ("eA", tok[("xnTa", gt)])]
                        if c == 0:
                            wts.append(("eA", tok.get(("pjev", grp), 0)))
                            if grp == 1 and bi3 == 2:
                                wts.append(("eA", wgT_done))
                        W("tensor",
                          lambda e, o_=outs_[bi3], lhs=lhs,
                          rhs=rhss[bi3], c=c:
                          e.matmul(o_, lhs, rhs,
                                   start=(c == 0), stop=(c == NCH - 1)),
                          waits=wts, inc="eT")
                mm = C["eT"]
                PG = PA if grp == 0 else PB
                PVg = PC6 if grp == 0 else PC7
                W("scalar", lambda e, q4=q4, grp=grp, PG=PG: e.copy(
                    qt_st[q4][:, 512 * grp:512 * grp + 512],
                    PG[:, 0:512]),
                  waits=[("eT", mm),
                         ("eV", tok.get(("ropeqt", gt - 4), 0)),
                         ("eP", tok.get(("ropekt", gt - 4), 0))], inc="eA")
                W("scalar", lambda e, q4=q4, grp=grp, PG=PG: e.copy(
                    kt_st[q4][:, 512 * grp:512 * grp + 512],
                    PG[:, 512:1024]), inc="eA")
                W("scalar", lambda e, vv=vv, grp=grp, PVg=PVg: e.copy(
                    vv[:, 8 * grp:8 * grp + 8, 0:64],
                    PVg[:, :].rearrange("q (h d) -> q h d", h=8)),
                  waits=[(sv, 16 * (gt // 2))], inc="eA")
                tok[("pjev", grp)] = C["eA"]
            if gt < 2:
                W("vector", lambda e, vv=vv: e.memset(vv[:, :, 64:65],
                                                      1.0), inc="eV")
                W("vector", lambda e, vv=vv: e.memset(vv[:, :, 65:66],
                                                      0.0), inc="eV")
                tok[("vones", p)] = C["eV"]
            tok[("pjevt", gt)] = C["eA"]
            dma("gpsimd", v_sp[s, t * 128:(t + 1) * 128, :],
                vt_st[p][:, :], sv,
                waits=[("eA", C["eA"]), ("eV", tok.get(("vones", p), 0))])

        def p1b(gt):
            s, t = gt // NTS, gt % NTS
            p = gt % 2
            q4 = gt % 4
            pjev = tok[("pjevt", gt)]
            csr = tok[("csr", gt)]
            scs = SCS[q4]
            qe = qr_st[p].rearrange("q (h two i) -> q h two i", two=2, i=32)
            se_ = qt_st[q4].rearrange("q (h two i) -> q h two i", two=2, i=32)
            cs_ = cosb[q4][:, 0:512].rearrange("q (h i) -> q h i", h=NH)
            sn_ = cosb[q4][:, 512:1024].rearrange("q (h i) -> q h i", h=NH)
            rh = lambda a: a.rearrange("q (h i) -> q h i", h=NH)
            sq_ = "sqsA" if p == 0 else "sqsB"
            W("vector", lambda e, se_=se_, cs_=cs_: e.tensor_mul(
                rh(rt[0][:, :]), se_[:, :, 0, :], cs_),
              waits=[("eA", pjev), (scs, csr), ("eV", wg_split),
                     ("eT", wgT_tp),
                     ("eV", tok.get(("ropeqt", gt - 1), 0))], inc="eV")
            W("vector", lambda e, se_=se_, sn_=sn_: e.tensor_mul(
                rh(rt[1][:, :]), se_[:, :, 1, :], sn_), inc="eV")
            W("vector", lambda e, se_=se_, sn_=sn_: e.tensor_mul(
                rh(rt[2][:, :]), se_[:, :, 0, :], sn_), inc="eV")
            W("vector", lambda e, se_=se_, cs_=cs_: e.tensor_mul(
                rh(rt[3][:, :]), se_[:, :, 1, :], cs_), inc="eV")
            mv = C["eV"]
            W("vector", lambda e, qe=qe: e.tensor_sub(
                qe[:, :, 0, :], rh(rt[0][:, :]), rh(rt[1][:, :])),
              waits=[("eV", mv), (sq_, 16 * (gt // 2))], inc="eV")
            W("vector", lambda e, qe=qe: e.tensor_add(
                qe[:, :, 1, :], rh(rt[2][:, :]), rh(rt[3][:, :])),
              inc="eV")
            ropeq = C["eV"]
            tok[("ropeqt", gt)] = ropeq
            ke = kr_st[p].rearrange("q (h two i) -> q h two i", two=2, i=32)
            ks_ = kt_st[q4].rearrange("q (h two i) -> q h two i", two=2, i=32)
            sk_ = "sksA" if p == 0 else "sksB"
            W("gpsimd", lambda e, ks_=ks_, cs_=cs_: e.tensor_mul(
                rh(pt[0][:, :]), ks_[:, :, 0, :], cs_),
              waits=[("eA", pjev), (scs, csr), ("eV", wg_split),
                     ("eT", wgT_tp),
                     ("eP", tok.get(("ropekt", gt - 1), 0))], inc="eP")
            W("gpsimd", lambda e, ks_=ks_, sn_=sn_: e.tensor_mul(
                rh(pt[1][:, :]), ks_[:, :, 1, :], sn_), inc="eP")
            mp = C["eP"]
            W("gpsimd", lambda e, ke=ke: e.tensor_sub(
                ke[:, :, 0, :], rh(pt[0][:, :]), rh(pt[1][:, :])),
              waits=[("eP", mp), (sk_, 16 * (gt // 2))], inc="eP")
            W("gpsimd", lambda e, ks_=ks_, sn_=sn_: e.tensor_mul(
                rh(pt[0][:, :]), ks_[:, :, 0, :], sn_),
              waits=[("eP", C["eP"])], inc="eP")
            W("gpsimd", lambda e, ks_=ks_, cs_=cs_: e.tensor_mul(
                rh(pt[1][:, :]), ks_[:, :, 1, :], cs_), inc="eP")
            mp = C["eP"]
            W("gpsimd", lambda e, ke=ke: e.tensor_add(
                ke[:, :, 1, :], rh(pt[0][:, :]), rh(pt[1][:, :])),
              waits=[("eP", mp)], inc="eP")
            ropek = C["eP"]
            tok[("ropekt", gt)] = ropek
            dma("sync", q_sp[s, t * 128:(t + 1) * 128, :],
                qr_st[p][:, :], sq_, waits=[("eV", ropeq)])
            dma("gpsimd", k_sp[s, t * 128:(t + 1) * 128, :],
                kr_st[p][:, :], sk_, waits=[("eP", ropek)])

        def _wtsp_fill(ee, c2, wbt):
            sl2 = wbt % 2
            wts = [("swe", 16 * (ee + 1)), ("swb", 32 * wbt)]
            if wbt >= 2:
                wts.append(("swc", 16 * (wbt - 1)))
            for j in range(2):
                c = c2 * 2 + j
                dma("sync", bnc[sl2][:, j * 1024:(j + 1) * 1024],
                    we16[ee, :, c * 128:(c + 1) * 128], "swb",
                    transpose=True, waits=wts)
                wts = []
            dma("scalar", weTsp_h[ee, :, c2 * 2048:(c2 + 1) * 2048],
                bnc[sl2][:, :], "swc",
                waits=[("swb", 32 * (wbt + 1)), ("swc", 16 * wbt)])
        wtsp_units = []
        _wbt = 0
        for ee in range(NE):
            for c2 in range(4):
                wtsp_units.append(
                    lambda ee=ee, c2=c2, w=_wbt: _wtsp_fill(ee, c2, w))
                _wbt += 1
        _wti = [0]

        def drain_wtsp(n):
            while _wti[0] < len(wtsp_units) and n > 0:
                wtsp_units[_wti[0]]()
                _wti[0] += 1
                n -= 1

        ln_s0(0)
        ln_s0(1)
        ln_s0(2)
        for it in range(NT + 8):
            if it < NT:
                ln_s1(it)
            if 5 <= it < NT + 5:
                ln_s6(it - 5)
            if 1 <= it < NT + 1:
                ln_s2(it - 1)
            if it + 3 < NT:
                ln_s0(it + 3)
            if 2 <= it < NT + 2:
                ln_s3(it - 2)
            if 3 <= it < NT + 3:
                ln_s4(it - 3)
            if 4 <= it < NT + 4:
                ln_s5(it - 4)
            if 8 <= it < 8 + 2 * NE and (it - 8) % 2 == 0:
                ei = (it - 8) // 2
                dma("gpsimd", we16[ei, :, :], we_h[ei, :, :], "swe",
                    waits=[("swe", 16 * ei)])
            if it == 8 + 2 * NE:
                dma("gpsimd", be16s[0:8, :], be_h[:, :], "sbe")
            if 6 <= it < NT + 6:
                gsp = it - 6
                dma("scalar",
                    xnTs_h.rearrange("p (c w) -> p c w", c=NCH)
                    [:, :, gsp * 128:(gsp + 1) * 128],
                    xnT.rearrange("p (c w) -> p c w", c=NCH)
                    [:, :, gsp * 128:(gsp + 1) * 128], "sxnT",
                    waits=[("sxnT", 16 * gsp), ("eV", tok[("xnTv", gsp)]),
                           ("eA", tok[("xnTa", gsp)])])
            if it >= 8:
                p1b(it - 8)
            if 6 <= it < NT + 6:
                p1a(it - 6)
            if it >= 12:
                drain_wtsp(2)

        drain_wtsp(len(wtsp_units))
        wcT = C["swc"]

        for t in range(NT):
            for c in range(NCH):
                for s3 in range(3):
                    W("tensor",
                      lambda e, t=t, c=c, s3=s3: e.matmul(
                          PC6[:, t * 8:(t + 1) * 8],
                          xnT[:, c * T + t * 128:c * T + (t + 1) * 128],
                          wgT[:, c * 24 + s3 * 8:c * 24 + s3 * 8 + 8],
                          start=(c == 0 and s3 == 0),
                          stop=(c == NCH - 1 and s3 == 2)),
                      waits=[("eA", max(C["eA"] if t == 0 and c == 0 and s3 == 0 else 0, wgT_done))], inc="eT")
        W("vector", lambda e: e.tensor_copy(lgall[:, :], PC6[:, 0:256]),
          waits=[("eT", C["eT"]), ("eV", C["eV"])], inc="eV")
        d3 = lambda a: a.rearrange("q (t e) -> q t e", e=8)
        bc = lambda a: a.unsqueeze(2).broadcast_to([128, NT, 8])
        W("vector", lambda e: e.tensor_copy(lgall[:, :], PC6[:, 0:256]),
          waits=[("eT", C["eT"]), ("eV", C["eV"])], inc="eV")
        W("vector", lambda e: e.reduce_max(v1a[:, :], d3(lgall), axis=AX.X),
          waits=[("eV", C["eV"])], inc="eV")
        W("vector", lambda e: e.tensor_tensor(
            d3(eq16[:, 0:256]), d3(lgall), bc(v1a[:, :]), op=ALU.is_equal),
          waits=[("eV", C["eV"])], inc="eV")
        W("vector", lambda e: e.scalar_tensor_tensor(
            l2a[:, :], eq16[:, 0:256], -1e30, lgall[:, :],
            op0=ALU.mult, op1=ALU.add),
          waits=[("eV", C["eV"])], inc="eV")
        W("vector", lambda e: e.reduce_max(v2a[:, :], d3(l2a), axis=AX.X),
          waits=[("eV", C["eV"])], inc="eV")
        W("vector", lambda e: e.tensor_tensor(
            d3(eq16[:, 256:512]), d3(l2a), bc(v2a[:, :]), op=ALU.is_equal),
          waits=[("eV", C["eV"])], inc="eV")
        W("vector", lambda e: e.tensor_sub(dst[:, :], v2a[:, :], v1a[:, :]),
          waits=[("eV", C["eV"])], inc="eV")

        W("scalar", lambda e: e.activation(edt[:, :], dst[:, :], AF.Exp),
          waits=[("eV", C["eV"])], inc="eA")
        W("vector", lambda e: e.tensor_scalar(dent[:, :], edt[:, :], 1.0,
                                              None, op0=ALU.add),
          waits=[("eA", C["eA"])], inc="eV")
        W("vector", lambda e: e.reciprocal(w1t[:, :], dent[:, :]),
          waits=[("eV", C["eV"])], inc="eV")
        W("vector", lambda e: e.tensor_mul(w2t[:, :], edt[:, :], w1t[:, :]),
          waits=[("eV", C["eV"])], inc="eV")
        W("vector", lambda e: e.tensor_copy(w116[:, :], w1t[:, :]),
          waits=[("eV", C["eV"])], inc="eV")
        W("vector", lambda e: e.tensor_copy(w216[:, :], w2t[:, :]),
          waits=[("eV", C["eV"])], inc="eV")
        W("vector", lambda e: e.tensor_copy(w132[:, :], w116[:, :]),
          waits=[("eV", C["eV"])], inc="eV")
        W("vector", lambda e: e.tensor_copy(w232[:, :], w216[:, :]),
          waits=[("eV", C["eV"])], inc="eV")
        for t in range(NT):
            W("vector", lambda e, t=t: e.tensor_scalar(
                tmp8[:, :], eq16[:, 256 + t * 8:256 + (t + 1) * 8],
                w232[:, t:t + 1], None, op0=ALU.mult),
              waits=[("eV", C["eV"])], inc="eV")
            W("vector", lambda e, t=t: e.scalar_tensor_tensor(
                wm32[:, t * 8:(t + 1) * 8], eq16[:, t * 8:(t + 1) * 8],
                w132[:, t:t + 1], tmp8[:, :], op0=ALU.mult, op1=ALU.add),
              waits=[("eV", C["eV"])], inc="eV")
            W("vector", lambda e, t=t: e.tensor_copy(
                wm16[:, t * 8:(t + 1) * 8], wm32[:, t * 8:(t + 1) * 8]),
              waits=[("eV", C["eV"])], inc="eV")
            W("vector", lambda e, t=t: e.tensor_copy(
                wm32[:, t * 8:(t + 1) * 8], wm16[:, t * 8:(t + 1) * 8]),
              waits=[("eV", C["eV"])], inc="eV")
            W("tensor", lambda e, t=t: e.transpose(
                pb7f16[0:8, 0:128], wm16[:, t * 8:(t + 1) * 8], idn[:, :]),
              waits=[("eV", C["eV"]), ("eA", C["eA"])], inc="eT")
            W("scalar", lambda e, t=t: e.copy(
                wmT[0:8, t * 128:(t + 1) * 128], pb7f16[0:8, 0:128]),
              waits=[("eT", C["eT"])], inc="eA")

        P.barrier()
        spw = [("sqsA", C["sqsA"]), ("sqsB", C["sqsB"]),
               ("sksA", C["sksA"]), ("sksB", C["sksB"]),
               ("svsA", C["svsA"]), ("svsB", C["svsB"]),
               ("sxnT", C["sxnT"]), ("sxA", C["sxA"]), ("sxB", C["sxB"]),
               ("sxC", C["sxC"]), ("sxD", C["sxD"]),
               ("scsA", C["scsA"]), ("scsB", C["scsB"]),
               ("scsC", C["scsC"]), ("scsD", C["scsD"])]

        dma("scalar", ow128[:, :].rearrange("k (c n) -> k c n", c=NCH),
            ow_h.rearrange("(c k) n -> k c n", k=128), "sow",
            waits=P.bw() + spw)
        oww = C["sow"]
        bew = C["sbe"]

        def xnq_load(k):
            dma("gpsimd", xnq[k % 2][:, :].rearrange("p (c w) -> p c w",
                                                     c=NCH),
                xnTs_h.rearrange("p (c w) -> p c w", c=NCH)
                [:, :, k * 512:(k + 1) * 512], "sgq",
                waits=P.bw() + spw +
                [("sgq", 16 * k), ("sxnT", 16 * (4 * k + 4)),
                 ("eT", tok.get(("xnq_free", k % 2), 0))])
            tok[("xnq_ld", k)] = C["sgq"]

        def wexp_load(k, e):
            ring = "scalar" if e % 2 == 0 else "gpsimd"
            sw = "sweA" if e % 2 == 0 else "sweB"
            wts = P.bw() + spw + [("swc", wcT),
                                  ("eT", tok.get(("wfree", k, e), 0))]
            for hh in range(4):
                dma(ring, weT[e % 2][:, hh * 2048:(hh + 1) * 2048],
                    weTsp_h[e, :, hh * 2048:(hh + 1) * 2048], sw, waits=wts)
                wts = []
            tok[("wld", k, e)] = C[sw]

        def p2_units(k):
            units = []
            sl = k % 2
            if k + 1 < 2 * NQG:
                units.append(lambda k=k: xnq_load(k + 1))

            def bias_unit(t, half, k=k):
                gt = 4 * k + t
                W("tensor", lambda e, gt=gt, half=half: e.matmul(
                    PC6[:, :], wmT[0:8, gt * 128:(gt + 1) * 128],
                    be16s[0:8, half * 512:(half + 1) * 512],
                    start=True, stop=True),
                  waits=[("sbe", bew), ("eA", tok.get("pc6a", 0)),
                         ("eV", tok.get("pc6v", 0))], inc="eT")
                W("scalar", lambda e, t=t, half=half: e.copy(
                    acc32[t][:, half * 512:(half + 1) * 512], PC6[:, :]),
                  waits=P.bw() + spw + [("eT", C["eT"]),
                                  ("eA", tok.get(("accfree", t), 0))] +
                  ([("soA", tok.get(("odma", 0), 0)),
                    ("soB", tok.get(("odma", 1), 0))] if t == 0 else []),
                  inc="eA")
                tok["pc6a"] = C["eA"]
                tok[("accinit", t, half)] = C["eA"]
            for t in range(4):
                for half in range(2):
                    units.append(lambda t=t, half=half: bias_unit(t, half))

            def mm_unit(e, t, half, k=k, sl=sl):
                be_ = e % 2
                sw = "sweA" if be_ == 0 else "sweB"
                wrd = tok[("wld", k, e)]
                gt = 4 * k + t
                for c in range(NCH):
                    wts = [(sw, wrd), ("sgq", tok[("xnq_ld", k)])]
                    if c == 0:
                        wts.append(("eV", tok.get(("pbfree", be_), 0)))
                    W("tensor", lambda e2, c=c, t=t, half=half, be_=be_,
                      sl=sl: e2.matmul(
                          PB[:, 512 * be_:512 * be_ + 512],
                          xnq[sl][:, c * 512 + t * 128:
                                  c * 512 + (t + 1) * 128],
                          weT[be_][:, c * D + half * 512:
                                   c * D + (half + 1) * 512],
                          start=(c == 0), stop=(c == NCH - 1)),
                      waits=wts, inc="eT")
                mm = C["eT"]
                if t == 3 and half == 1:
                    if e + 2 < NE:
                        tok[("wfree", k, e + 2)] = mm
                    else:
                        tok[("wfree", k + 1, e - 6)] = mm
                    if e == NE - 1:
                        tok[("xnq_free", sl)] = mm
                wts2 = [("eT", mm), ("eV", C["eV"])]
                if e == 0:
                    wts2.append(("eA", tok[("accinit", t, half)]))
                W("vector", lambda e2, t=t, half=half, be_=be_, gt=gt,
                  e=e: e2.scalar_tensor_tensor(
                      acc32[t][:, half * 512:(half + 1) * 512],
                      PB[:, 512 * be_:512 * be_ + 512],
                      wm32[:, gt * 8 + e:gt * 8 + e + 1],
                      acc32[t][:, half * 512:(half + 1) * 512],
                      op0=ALU.mult, op1=ALU.add),
                  waits=wts2, inc="eV")
                tok[("pbfree", be_)] = C["eV"]
                if e == NE - 1 and half == 1:
                    tok[("sttdone", t)] = C["eV"]
            for e in range(NE):
                for t in range(4):
                    for half in range(2):
                        units.append(
                            lambda e=e, t=t, half=half: mm_unit(e, t, half))
                if e + 2 < NE:
                    units.append(lambda e=e, k=k: wexp_load(k, e + 2))
                elif k + 1 < 2 * NQG:
                    units.append(lambda e=e, k=k: wexp_load(k + 1, e - 6))

            def silu_unit(t, k=k, sl=sl):
                W("scalar", lambda e, t=t, sl=sl: e.activation(
                    gsl[sl][:, t * 1024:(t + 1) * 1024], acc32[t][:, :],
                    AF.Silu),
                  waits=P.bw() + spw + [("eV", tok[("sttdone", t)]),
                                  ("eA", C["eA"]),
                                  ("eV", tok.get(("gslfree", sl), 0))],
                  inc="eA")
                tok[("accfree", t)] = C["eA"]
                if t == 3:
                    tok[("sild", k)] = C["eA"]
            for t in range(4):
                units.append(lambda t=t: silu_unit(t))
            return units

        def attn_units(s, qg):
            au = []
            nkc = 4 * qg + 4
            ktr = tok[("ktr", s)]
            vld = tok[("vld", s)]
            qrd = tok["qrd"]

            def emit_pv(h, kc):
                p2 = h % 2
                j = kc - 4 * qg
                sbk = kc % 3
                w0 = max(0, j) * 128
                mw = 65 if h == NH - 1 else 128
                pvw = [("eA", tok.get(("expC", kc), 0)), ("svld", vld)]
                if j >= 0:
                    pvw.append(("eP", tok.get(("maskdone", sbk), 0)))
                if kc == 0:
                    pvw.append(("eA", tok.get("pvrecip", 0)))
                    pvw.append(("eV", tok.get("pvevac", 0)))
                W("tensor", lambda e, kc=kc, h=h, sbk=sbk, w0=w0,
                  nkc=nkc, p2=p2, mw=mw: e.matmul(
                      PV2[0:mw, 512 * p2 + w0:512 * p2 + 512],
                      v_sb[:, kc * VF + h * VW:kc * VF + h * VW + mw],
                      expb3[sbk][:, 512 * p2 + w0:512 * p2 + 512],
                      start=(kc == 0), stop=(kc == nkc - 1)),
                  waits=pvw, inc="eT")
                tok[("exprd", sbk)] = C["eT"]

            def emit_tail_a(hp):
                pvmm = tok[("pvmm", hp)]
                W("scalar", lambda e: e.activation(
                    recL[96:97, 0:1024], PV2[64:65, 0:1024], AF.Ln),
                  waits=[("swc", wcT), ("eT", pvmm), ("eA", C["eA"]),
                         ("eA", tok.get("lnrd", 0))], inc="eA")
                W("scalar", lambda e: e.activation(
                    recE[64:65, 0:1024], recL[96:97, 0:1024],
                    AF.Exp, scale=-1.0),
                  waits=[("eA", C["eA"]),
                         ("eT", tok.get("rec16rd", 0))], inc="eA")
                tok["pvrecip"] = C["eA"]
                tok[("recv", hp)] = C["eA"]
                tok["lnrd"] = C["eA"]
                W("vector", lambda e: e.tensor_copy(
                    at2[0:64, :], PV2[0:64, 0:1024]),
                  waits=[("swc", wcT), ("eT", pvmm),
                         ("eV", tok.get("atrd", 0))], inc="eV")
                tok["pvevac"] = C["eV"]
                tok[("atv", hp)] = C["eV"]

            def emit_tail_b(hp):
                W("tensor", lambda e: e.matmul(
                    PC6[0:64, :], ones16[64:65, 0:64],
                    recE[64:65, 0:512], start=True, stop=True),
                  waits=[("eA", tok[("recv", hp)]),
                         ("eA", tok.get("pc6a", 0)),
                         ("eV", tok.get("pc6v", 0))], inc="eT")
                W("tensor", lambda e: e.matmul(
                    PC7[0:64, :], ones16[64:65, 0:64],
                    recE[64:65, 512:1024], start=True, stop=True),
                  waits=[("eV", tok.get("bcrd", 0))], inc="eT")
                tok["rec16rd"] = C["eT"]
                bcmm = C["eT"]
                W("vector", lambda e, hp=hp: e.tensor_mul(
                    attnT[0:64, hp * 512:(hp + 1) * 512],
                    at2[0:64, 0:512], PC6[0:64, :]),
                  waits=[("eV", tok[("atv", hp)]), ("eT", bcmm),
                         ("eT", tok.get("atT_rd", 0)), ("eV", C["eV"])],
                  inc="eV")
                tok["pc6v"] = C["eV"]
                W("vector", lambda e, hp=hp: e.tensor_mul(
                    attnT[64:128, hp * 512:(hp + 1) * 512],
                    at2[0:64, 512:1024], PC7[0:64, :]),
                  inc="eV")
                tok["atrd"] = C["eV"]
                tok["bcrd"] = C["eV"]

            def score_unit(hp, kc):
                ch = hp
                sbk = kc % 3
                j = kc - 4 * qg
                w0 = max(0, j) * 128
                for p2 in range(2):
                    base = p2 * 64
                    wts = [("skT", ktr), ("sqTA", qrd)]
                    if p2 == 0:
                        wts.append(("eA", tok.get("expdone", 0)))
                        if hp == 0 and kc == 0:
                            wts.append(("soA", tok.get("outsA", 0)))
                            wts.append(("soB", tok.get("outsB", 0)))
                    W("tensor", lambda e, base=base, ch=ch, kc=kc,
                      p2=p2: e.matmul(
                          PA[:, 512 * p2:512 * p2 + 512],
                          kT[base:base + 64,
                             ch * S + kc * 128:ch * S + (kc + 1) * 128],
                          qTb[base:base + 64, ch * 512:(ch + 1) * 512],
                          start=True, stop=True),
                      waits=wts, inc="eT")
                ev = (lambda a: a.rearrange("q (hh c) -> q hh c", hh=2)
                      [:, :, w0:512])
                W("scalar", lambda e, sbk=sbk, ev=ev: e.activation(
                    ev(expb3[sbk]), ev(PA), AF.Exp, scale=1.0 / 32.0),
                  waits=[("swc", wcT), ("eT", C["eT"]),
                         ("eT", tok.get(("exprd", sbk), 0))],
                  inc="eA")
                tok["expdone"] = C["eA"]
                tok[("expC", kc)] = C["eA"]
                if j >= 0:
                    tv = (lambda a: a.rearrange("q (hh c) -> q hh c", hh=2)
                          [:, :, j * 128:(j + 1) * 128])
                    W("gpsimd", lambda e, sbk=sbk, tv=tv: e.tensor_mul(
                        tv(expb3[sbk]), tv(expb3[sbk]),
                        tri[:, :].unsqueeze(1).broadcast_to([128, 2, 128])),
                      waits=[("swc", wcT), ("eA", C["eA"]), ("scst", cst),
                             ("eP", C["eP"])], inc="eP")
                    tok[("maskdone", sbk)] = C["eP"]
                if kc >= 2:
                    emit_pv(2 * hp, kc - 2)
                    emit_pv(2 * hp + 1, kc - 2)

            def hp_tail(hp):
                emit_pv(2 * hp, nkc - 2)
                emit_pv(2 * hp + 1, nkc - 2)
                emit_pv(2 * hp, nkc - 1)
                emit_pv(2 * hp + 1, nkc - 1)
                tok[("pvmm", hp)] = C["eT"]
                if hp >= 1:
                    emit_tail_b(hp - 1)
                emit_tail_a(hp)

            for hp in range(NH // 2):
                for kc in range(nkc):
                    au.append(lambda hp=hp, kc=kc: score_unit(hp, kc))
                au.append(lambda hp=hp: hp_tail(hp))
            au.append(lambda: emit_tail_b(NH // 2 - 1))
            return au

        def combine(s, qg, k):
            heads_done = C["eV"]
            r0 = s * S + qg * 512
            sild = tok[("sild", k)]
            for pc in range(2):
                dma("gpsimd", xq2[pc][:, :],
                    x_h[r0 + (pc // 2) * 128:r0 + (pc // 2 + 1) * 128,
                        (pc % 2) * 512:(pc % 2 + 1) * 512], "sxq",
                    waits=[("sxq", C["sxq"]),
                           ("eV", tok.get(("xqfree", pc % 2), 0))])
                tok[("xql", pc)] = C["sxq"]
            for jt in range(4):
                for half in range(2):
                    pc = jt * 2 + half
                    for cp in range(NCH):
                        wts = [("sow", oww), ("eV", heads_done)]
                        if cp == 0:
                            wts.append(("eA", tok["expdone"]))
                            pf = tok.get(("pafree", half))
                            if pf is not None:
                                wts.append(pf)
                        W("tensor", lambda e, jt=jt, half=half, cp=cp:
                          e.matmul(
                              PA[:, 512 * half:512 * half + 512],
                              attnT[0:128, cp * 512 + jt * 128:
                                    cp * 512 + (jt + 1) * 128],
                              ow128[:, cp * D + half * 512:
                                    cp * D + (half + 1) * 512],
                              start=(cp == 0), stop=(cp == NCH - 1)),
                          waits=wts, inc="eT")
                    omm = C["eT"]
                    so = "soA" if pc % 2 == 0 else "soB"
                    ob = acc32[0][:, (pc % 2) * 512:(pc % 2) * 512 + 512]
                    mw_ = [("eT", omm), ("eA", sild), ("eV", C["eV"]),
                           (so, tok.get(("odma", pc % 2), 0))]
                    W("vector", lambda e, jt=jt, half=half, sl=k % 2,
                      ob=ob: e.tensor_tensor(
                          ob,
                          PA[:, 512 * half:512 * half + 512],
                          gsl[sl][:, jt * 1024 + half * 512:
                                  jt * 1024 + (half + 1) * 512],
                          op=ALU.mult),
                      waits=mw_, inc="eV")
                    tok[("pafree", half)] = ("eV", C["eV"])
                    W("vector", lambda e, pc=pc, ob=ob:
                      e.tensor_tensor(ob, ob, xq2[pc % 2][:, :],
                                      op=ALU.add),
                      waits=[("sxq", tok[("xql", pc)]), ("eV", C["eV"])],
                      inc="eV")
                    tok[("xqfree", pc % 2)] = C["eV"]
                    dma("sync",
                        out_h[r0 + jt * 128:r0 + (jt + 1) * 128,
                              half * 512:(half + 1) * 512], ob, so,
                        waits=[("eV", C["eV"])])
                    tok[("odma", pc % 2)] = C[so]
                    if pc + 2 < 8:
                        npc = pc + 2
                        njt, nhalf = npc // 2, npc % 2
                        dma("gpsimd", xq2[npc % 2][:, :],
                            x_h[r0 + njt * 128:r0 + (njt + 1) * 128,
                                nhalf * 512:(nhalf + 1) * 512], "sxq",
                            waits=[("sxq", C["sxq"]),
                                   ("eV", tok.get(("xqfree", npc % 2), 0))])
                        tok[("xql", npc)] = C["sxq"]
            tok["atT_rd"] = C["eT"]
            tok["outsA"] = C["soA"]
            tok["outsB"] = C["soB"]

        wexp_load(0, 0)
        wexp_load(0, 1)
        xnq_load(0)
        for s in range(NSEQ):
            kw_ = P.bw() + spw + [("eT", tok.get("kT_free", 0))]
            for c in range(NCH):
                dma("sync", c16(AR1, c * 4096, 2048)[:, :],
                    k_sp[s, :, c * 128:(c + 1) * 128], "skT", transpose=True,
                    waits=kw_)
                kw_ = []
            tok[("ktr", s)] = C["skT"]
            dma("gpsimd", v_sb[:, :].rearrange("q (t f) -> q t f", t=NTS),
                v_sp[s].rearrange("(t q) f -> q t f", q=128), "svld",
                waits=P.bw() + spw + [("eT", tok.get("v_free", 0))])
            tok[("vld", s)] = C["svld"]
            for qg in range(NQG):
                k = s * NQG + qg
                qw_ = P.bw() + spw + [("eT", tok.get("qT_free", 0))]
                for c in range(NCH):
                    dma("sync", qTb[:, c * 512:(c + 1) * 512],
                        q_sp[s, qg * 512:(qg + 1) * 512,
                             c * 128:(c + 1) * 128], "sqTA",
                        transpose=True, waits=qw_)
                    qw_ = []
                tok["qrd"] = C["sqTA"]
                au = attn_units(s, qg)
                pu = p2_units(k)
                j = 0
                for i, u in enumerate(au):
                    share = max(min(6, len(pu)),
                                ((i + 1) * len(pu)) // len(au))
                    while j < share:
                        pu[j]()
                        j += 1
                    u()
                while j < len(pu):
                    pu[j]()
                    j += 1
                tok["qT_free"] = C["eT"]
                combine(s, qg, k)
            tok["kT_free"] = C["eT"]
            tok["v_free"] = C["eT"]

        W("scalar", None, waits=[("soA", C["soA"]), ("soB", C["soB"])])
        P.barrier()

        with nc.Block() as blk:
            P.replay(blk)
    return nc


_CACHE = {}


def _constants():
    if "c" in _CACHE:
        return _CACHE["c"]
    inv = 1.0 / (10000.0 ** (np.arange(0, HD, 2, dtype=np.float64) / HD))
    pos = np.arange(S, dtype=np.float64)
    fr = pos[:, None] * inv[None, :]
    cos = np.tile(np.cos(fr), (1, NH)).astype(np.float16)
    sin = np.tile(np.sin(fr), (1, NH)).astype(np.float16)
    tri = np.triu(np.ones((128, 128), dtype=np.float16))
    idn = np.eye(128, dtype=np.float16)
    _CACHE["c"] = (cos, sin, tri, idn)
    return _CACHE["c"]


def kernel(L0_attn_x, L0_attn_qw, L0_attn_kw, L0_attn_vw, L0_attn_ow,
           L0_ffn_Wg, L0_ffn_We, L0_ffn_be):
    x = np.ascontiguousarray(np.asarray(L0_attn_x, dtype=np.float16))
    qw = np.ascontiguousarray(np.asarray(L0_attn_qw, dtype=np.float16))
    kw = np.ascontiguousarray(np.asarray(L0_attn_kw, dtype=np.float16))
    vw = np.ascontiguousarray(np.asarray(L0_attn_vw, dtype=np.float16))
    ow = np.ascontiguousarray(np.asarray(L0_attn_ow, dtype=np.float16))
    wg = np.ascontiguousarray(np.asarray(L0_ffn_Wg, dtype=np.float32))
    we = np.ascontiguousarray(np.asarray(L0_ffn_We, dtype=np.float32))
    be = np.ascontiguousarray(np.asarray(L0_ffn_be, dtype=np.float32))
    cos, sin, tri, idn = _constants()

    nc = build_nc()
    in_maps = []
    for i in range(NCORE):
        in_maps.append({
            "x_c": np.ascontiguousarray(
                x[2 * i:2 * i + 2].reshape(T, D)),
            "qw": qw, "kw": kw, "vw": vw, "ow": ow,
            "wg": wg, "we": we, "be": be,
            "c_cos": cos, "c_sin": sin, "c_tri": tri, "c_idn": idn,
        })
    res = run_bass_kernel_spmd(nc, in_maps, core_ids=list(range(NCORE)))
    out = np.concatenate(
        [np.asarray(r["out_c"]) for r in res.results], axis=0)
    return out.reshape(16, S, D).astype(np.float32, copy=False)


# version 24
# speedup vs baseline: 1.0177x; 1.0021x over previous
import contextlib

import numpy as np

import concourse.bass as bass
import concourse.mybir as mybir
from concourse.bass_utils import run_bass_kernel_spmd

F32 = mybir.dt.float32
F16 = mybir.dt.float16
U8 = mybir.dt.uint8
ALU = mybir.AluOpType
AF = mybir.ActivationFunctionType
AX = mybir.AxisListType

NCORE = 8
D = 1024
S = 2048
NSEQ = 2
T = NSEQ * S
NT = T // 128
NTS = S // 128
NH = 16
HD = 64
NCH = 8
NE = 8
NQG = 4
EPS16 = float(np.float16(1e-5))
VW = 66
VF = NH * VW


class Prog:

    def __init__(self, nc):
        self.nc = nc
        self.ops = []
        self.sems = {}
        self.cnt = {}
        self.wm = {}

    def decl(self, ctx, *names):
        for n in names:
            self.sems[n] = ctx.enter_context(self.nc.semaphore(n))
            self.cnt[n] = 0

    def emit(self, eng, fn, waits=(), inc=None, amt=1):
        ws = []
        for (sn, v) in waits:
            if v is None or v <= 0:
                continue
            k = (eng, sn)
            if self.wm.get(k, 0) >= v:
                continue
            self.wm[k] = v
            ws.append((sn, v))
        incs = []
        if inc is not None:
            self.cnt[inc] += amt
            incs.append((inc, amt))
        self.ops.append((eng, ws, fn, incs))
        return self.cnt[inc] if inc is not None else None

    def barrier(self):
        engs = ("vector", "scalar", "tensor", "gpsimd")
        marks = {}
        for e in engs:
            sn = "bar_" + e
            self.cnt[sn] += 1
            self.ops.append((e, [], None, [(sn, 1)]))
            marks[e] = self.cnt[sn]
        for e in engs:
            self.emit(e, None,
                      waits=[("bar_" + o, marks[o]) for o in engs if o != e])
        self.last_bar = [("bar_" + o, marks[o]) for o in engs]

    def bw(self):
        return list(getattr(self, "last_bar", []))

    def replay(self, blk):
        sems, ops = self.sems, self.ops

        def runner(name):
            def body(eng):
                for (e, ws, fn, incs) in ops:
                    if e != name:
                        continue
                    for (sn, v) in ws:
                        eng.wait_ge(sems[sn], v)
                    if fn is None:
                        if incs:
                            ins = eng.drain()
                            for (sn, a) in incs:
                                ins.then_inc(sems[sn], a)
                        continue
                    ins = fn(eng)
                    for (sn, a) in incs:
                        ins.then_inc(sems[sn], a)
            return body

        blk.vector(runner("vector"))
        blk.scalar(runner("scalar"))
        blk.tensor(runner("tensor"))
        blk.gpsimd(runner("gpsimd"))
        blk.sync(runner("sync"))


def build_nc():
    nc = bass.Bass("TRN2", target_bir_lowering=False)

    x_h = nc.dram_tensor("x_c", [T, D], F16, kind="ExternalInput").ap()
    qw_h = nc.dram_tensor("qw", [D, D], F16, kind="ExternalInput").ap()
    kw_h = nc.dram_tensor("kw", [D, D], F16, kind="ExternalInput").ap()
    vw_h = nc.dram_tensor("vw", [D, D], F16, kind="ExternalInput").ap()
    ow_h = nc.dram_tensor("ow", [D, D], F16, kind="ExternalInput").ap()
    wg_h = nc.dram_tensor("wg", [NE, D], F32, kind="ExternalInput").ap()
    we_h = nc.dram_tensor("we", [NE, D, D], F32, kind="ExternalInput").ap()
    be_h = nc.dram_tensor("be", [NE, D], F32, kind="ExternalInput").ap()
    cos_h = nc.dram_tensor("c_cos", [S, 512], F16, kind="ExternalInput").ap()
    sin_h = nc.dram_tensor("c_sin", [S, 512], F16, kind="ExternalInput").ap()
    tri_h = nc.dram_tensor("c_tri", [128, 128], F16, kind="ExternalInput").ap()
    idn_h = nc.dram_tensor("c_idn", [128, 128], F16, kind="ExternalInput").ap()
    out_h = nc.dram_tensor("out_c", [T, D], F32, kind="ExternalOutput").ap()

    xn_sp = nc.dram_tensor("xn_sp", [T, D], F16, kind="Internal").ap()
    q_sp = nc.dram_tensor("q_sp", [NSEQ, S, D], F16, kind="Internal").ap()
    k_sp = nc.dram_tensor("k_sp", [NSEQ, S, D], F16, kind="Internal").ap()
    v_sp = nc.dram_tensor("v_sp", [NSEQ, S, VF], F16, kind="Internal").ap()
    g_sp = nc.dram_tensor("g_sp", [T, D], F16, kind="Internal").ap()
    we16 = nc.dram_tensor("we16", [NE, D, D], F16, kind="Internal").ap()
    xnTs_h = nc.dram_tensor("xnTs", [128, NCH * T], F16, kind="Internal").ap()
    weTsp_h = nc.dram_tensor("weTsp", [NE, 128, NCH * D], F16,
                             kind="Internal").ap()

    ctx = contextlib.ExitStack()
    with ctx:
        sbt = lambda name, shape, dt: ctx.enter_context(
            nc.sbuf_tensor(name, shape, dt))[:]
        AR1 = sbt("AR1", [128, 65536], U8)
        AR2 = sbt("AR2", [128, 125952], U8)

        def c16(ar, off, cols):
            return ar[:, off:off + 2 * cols].bitcast(F16)

        def c32(ar, off, cols):
            return ar[:, off:off + 4 * cols].bitcast(F32)

        xnT = c16(AR1, 0, 32768)
        kT = c16(AR1, 0, 16384)
        qTb = c16(AR1, 32768, 4096)
        attnT = c16(AR1, 40960, 4096)
        ow128 = c16(AR1, 49152, 8192)

        xst = [c16(AR2, 92672 + 2048 * i, 1024) for i in range(4)]
        xcn16 = [c16(AR2, 100864 + 2048 * i, 1024) for i in range(4)]
        sqd16 = c16(AR2, 109056, 1024)
        lgall = c32(AR2, 111104, 256)
        l2a = c32(AR2, 112128, 256)
        m32 = [c32(AR2, 111104 + 32 * i, 1) for i in range(4)]
        m16 = [c16(AR2, 111232 + 32 * i, 1) for i in range(4)]
        mneg = [c32(AR2, 111360 + 32 * i, 1) for i in range(4)]
        ssum = [c32(AR2, 111488 + 32 * i, 1) for i in range(4)]
        v16t = [c16(AR2, 111616 + 32 * i, 1) for i in range(4)]
        s16t = [c16(AR2, 111744 + 32 * i, 1) for i in range(4)]
        r16t = [c16(AR2, 111872 + 32 * i, 1) for i in range(4)]
        rc32 = [c32(AR2, 112000 + 32 * i, 1) for i in range(4)]
        qwS = c16(AR2, 0, 8192)
        kwS = c16(AR2, 16384, 8192)
        vwS = c16(AR2, 32768, 8192)
        cosb = [c16(AR2, 49152 + 2048 * i, 1024) for i in range(4)]
        qt_st = [c16(AR2, 57344 + 2048 * i, 1024) for i in range(4)]
        kt_st = [c16(AR2, 65536 + 2048 * i, 1024) for i in range(4)]
        vt_st = [c16(AR2, 73728, VF), c16(AR2, 76032, VF)]
        qr_st = [c16(AR2, 78336, 1024), c16(AR2, 80384, 1024)]
        kr_st = [c16(AR2, 82432, 1024), c16(AR2, 84480, 1024)]
        rt = [c16(AR2, 86528 + 1024 * i, 512) for i in range(4)]
        pt = [c16(AR2, 90624 + 1024 * i, 512) for i in range(2)]
        wgf = c32(AR2, 78336, 1024)
        wgr = c32(AR2, 82432, 1024)
        dst = c32(AR2, 113152, 32)
        edt = c32(AR2, 113280, 32)
        dent = c32(AR2, 113408, 32)
        w1t = c32(AR2, 113536, 32)
        w2t = c32(AR2, 113664, 32)
        w132 = c32(AR2, 113792, 32)
        w232 = c32(AR2, 113920, 32)
        v1a = c32(AR2, 114048, 32)
        v2a = c32(AR2, 114176, 32)
        w116 = c16(AR2, 114304, 32)
        w216 = c16(AR2, 114368, 32)
        tmp8 = c32(AR2, 114432, 8)
        eq16 = c16(AR2, 114464, 512)
        bnc = [c16(AR2, 115712 + 4096 * i, 2048) for i in range(2)]
        v_sb = c16(AR2, 0, 16896)
        weT = [c16(AR2, 33792 + 16384 * i, 8192) for i in range(2)]
        xnq = [c16(AR2, 66560 + 8192 * i, 4096) for i in range(2)]
        acc32 = [c32(AR2, 82944 + 4096 * i, 1024) for i in range(4)]
        gsl = [c16(AR2, 99328 + 8192 * i, 4096) for i in range(2)]
        expb3 = [c16(AR2, 115712 + 2048 * i, 1024) for i in range(3)]
        at2 = c32(AR2, 121856, 1024)
        recL = c32(AR2, 121856, 1024)
        recE = c16(AR2, 121856, 2048)

        tri = sbt("tri", [128, 128], F16)
        idn = sbt("idn", [128, 128], F16)
        ones16 = sbt("ones16", [128, 64], F16)

        wm32 = sbt("wm32", [128, NT * 8], F32)
        wm16 = sbt("wm16", [128, NT * 8], F16)
        wmT = sbt("wmT", [8, NT * 128], F16)
        wgT = sbt("wgT", [128, NCH * 24], F16)
        wgs = sbt("wgs", [8, 3 * D], F16)
        be16s = sbt("be16s", [8, D], F16)
        xq2 = [sbt("xq2_%d" % i, [128, 512], F16) for i in range(2)]

        PA = ctx.enter_context(nc.psum_tensor("PA", [128, 1024], F32))[:]
        PB = ctx.enter_context(nc.psum_tensor("PB", [128, 1024], F32))[:]
        PV2 = ctx.enter_context(nc.psum_tensor("PV2", [128, 1024], F32))[:]
        PC6 = ctx.enter_context(nc.psum_tensor("PC6", [128, 512], F32))[:]
        PC7 = ctx.enter_context(nc.psum_tensor("PC7", [128, 512], F32))[:]
        pb7f16 = PC7.bitcast(F16)

        P = Prog(nc)
        P.decl(ctx, "eV", "eA", "eT", "eP",
               "bar_vector", "bar_scalar", "bar_tensor", "bar_gpsimd",
               "sxA", "sxB", "sxC", "sxD", "sspA", "sspB", "sspC", "sspD",
               "sxnT", "scst",
               "scsA", "scsB", "scsC", "scsD", "sw3", "sqsA", "sqsB", "sksA", "sksB",
               "svsA", "svsB", "swe", "sweA", "sweB", "sbe", "sgsA", "sgsB",
               "skT", "svld", "sqTA", "sqTB", "sgq", "sxq", "soA", "soB",
               "sow", "swb", "swc")
        W = P.emit
        C = P.cnt

        def dma(q, out, in_, sem, transpose=False, waits=()):
            def fn(e, out=out, in_=in_, tr=transpose):
                if tr:
                    return e.dma_start(out, in_, transpose=True)
                return e.dma_start(out, in_)
            return W(q, fn, waits=waits, inc=sem, amt=16)

        tok = {}

        dma("sync", qwS[:, :].rearrange("k (c n) -> k c n", c=NCH),
            qw_h.rearrange("(c k) n -> k c n", k=128), "sw3")
        dma("sync", kwS[:, :].rearrange("k (c n) -> k c n", c=NCH),
            kw_h.rearrange("(c k) n -> k c n", k=128), "sw3")
        dma("sync", vwS[:, :].rearrange("k (c n) -> k c n", c=NCH),
            vw_h.rearrange("(c k) n -> k c n", k=128), "sw3")
        dma("sync", tri[:, :], tri_h[:, :], "scst")
        dma("sync", idn[:, :], idn_h[:, :], "scst")
        dma("sync", wgf[0:8, :], wg_h[:, :], "scst")
        cst = C["scst"]
        W("vector", lambda e: e.memset(ones16[:, :], 1.0), inc="eV")

        W("vector", lambda e: e.tensor_copy(wgs[0:8, 0:D], wgf[0:8, :]),
          waits=[("scst", cst)], inc="eV")
        W("vector", lambda e: e.tensor_sub(wgr[0:8, :], wgf[0:8, :],
                                           wgs[0:8, 0:D]),
          waits=[("eV", C["eV"])], inc="eV")
        W("vector", lambda e: e.tensor_copy(wgs[0:8, D:2 * D], wgr[0:8, :]),
          waits=[("eV", C["eV"])], inc="eV")
        W("vector", lambda e: e.tensor_sub(wgr[0:8, :], wgr[0:8, :],
                                           wgs[0:8, D:2 * D]),
          waits=[("eV", C["eV"])], inc="eV")
        W("vector", lambda e: e.tensor_copy(wgs[0:8, 2 * D:3 * D], wgr[0:8, :]),
          waits=[("eV", C["eV"])], inc="eV")
        wg_split = C["eV"]
        for c in range(NCH):
            for s3 in range(3):
                W("tensor",
                  lambda e, c=c, s3=s3: e.transpose(
                      pb7f16[0:128, 0:8],
                      wgs[0:8, s3 * D + c * 128:s3 * D + (c + 1) * 128],
                      idn[0:8, 0:8]),
                  waits=[("eV", wg_split), ("scst", cst), ("eA", C["eA"])],
                  inc="eT")
                W("scalar",
                  lambda e, c=c, s3=s3: e.copy(
                      wgT[:, c * 24 + s3 * 8:c * 24 + s3 * 8 + 8],
                      pb7f16[0:128, 0:8]),
                  waits=[("eT", C["eT"])], inc="eA")
        wgT_done = C["eA"]
        wgT_tp = C["eT"]
        w3 = C["sw3"]

        SSP = ("sspA", "sspB", "sspC", "sspD")
        SX = ("sxA", "sxB", "sxC", "sxD")

        def ln_s0(t):
            q = t % 4
            dma("sync", xst[q][:, :], x_h[t * 128:(t + 1) * 128, :], SX[q],
                waits=[("eA", tok.get(("xcd", q), 0))])

        def ln_s1(t):
            q = t % 4
            W("vector", lambda e, q=q: e.reduce_sum(m32[q][:, :],
                                                    xst[q][:, :], axis=AX.X),
              waits=[(SX[q], C[SX[q]]), ("eV", C["eV"])],
              inc="eV")
            W("vector", lambda e, q=q: e.tensor_scalar(
                m16[q][:, :], m32[q][:, :], 1.0 / D, None, op0=ALU.mult),
              waits=[("eV", C["eV"])], inc="eV")
            W("vector", lambda e, q=q: e.tensor_scalar(
                mneg[q][:, :], m16[q][:, :], -1.0, None, op0=ALU.mult),
              waits=[("eV", C["eV"])], inc="eV")
            tok[("mnegd", q)] = C["eV"]

        def ln_s2(t):
            q = t % 4
            W("scalar", lambda e, q=q: e.activation(
                xcn16[q][:, :], xst[q][:, :], AF.Identity,
                bias=mneg[q][:, :], scale=1.0),
              waits=[("eV", tok[("mnegd", q)]),
                     ("eT", tok.get(("xnrd", q), 0))], inc="eA")
            tok[("xcd", q)] = C["eA"]
            W("scalar", lambda e, q=q: e.activation(
                sqd16[:, :], xcn16[q][:, :], AF.Square,
                accum_out=ssum[q][:, :]),
              waits=[("eA", C["eA"]), ("eV", tok.get(("vrd", q), 0))],
              inc="eA")
            tok[("sqdone", q)] = C["eA"]

        def ln_s3(t):
            q = t % 4
            W("vector", lambda e, q=q: e.tensor_scalar(
                v16t[q][:, :], ssum[q][:, :], 1.0 / D, None, op0=ALU.mult),
              waits=[("eA", tok[("sqdone", q)]), ("eV", C["eV"])], inc="eV")
            tok[("vrd", q)] = C["eV"]
            W("vector", lambda e, q=q: e.tensor_scalar(
                s16t[q][:, :], v16t[q][:, :], EPS16, None, op0=ALU.add),
              waits=[("eV", C["eV"])], inc="eV")
            tok[("s16d", q)] = C["eV"]

        def ln_s4(t):
            q = t % 4
            W("scalar", lambda e, q=q: e.activation(r16t[q][:, :],
                                                    s16t[q][:, :], AF.Sqrt),
              waits=[("eV", tok[("s16d", q)]), ("eA", C["eA"])], inc="eA")
            tok[("sqrtd", q)] = C["eA"]

        def ln_s5(t):
            q = t % 4
            W("vector", lambda e, q=q: e.reciprocal(rc32[q][:, :],
                                                    r16t[q][:, :]),
              waits=[("eA", tok[("sqrtd", q)]), ("eV", C["eV"])], inc="eV")
            W("vector", lambda e, q=q: e.tensor_scalar(
                xcn16[q][:, :], xcn16[q][:, :], rc32[q][:, :], None,
                op0=ALU.mult),
              waits=[("eV", C["eV"]), ("eA", tok[("sqdone", q)])], inc="eV")
            tok[("xc16", q)] = C["eV"]

        PV2f16 = PV2.bitcast(F16)
        tp16 = [PV2f16[:, 0:128], PV2f16[:, 1024:1152]]

        def ln_s6(t):
            q = t % 4
            for c in range(NCH):
                bp = c % 2
                W("tensor", lambda e, q=q, c=c, bp=bp: e.transpose(
                    tp16[bp][:, :], xcn16[q][:, c * 128:(c + 1) * 128],
                    idn[:, :]),
                  waits=[("eV", tok[("xc16", q)]), ("scst", cst),
                         ("eV", tok.get(("tpev", bp), 0)),
                         ("eA", tok.get(("tpea", bp), 0))], inc="eT")
                eng = "vector" if c % 4 < 2 else "scalar"
                def ef(e, q=q, c=c, bp=bp, t=t):
                    dst = xnT[:, c * T + t * 128:c * T + (t + 1) * 128]
                    if hasattr(e, "activation"):
                        return e.copy(dst, tp16[bp][:, :])
                    return e.tensor_copy(dst, tp16[bp][:, :])
                W(eng, ef, waits=[("eT", C["eT"])],
                  inc="eV" if eng == "vector" else "eA")
                tok[("tpev" if eng == "vector" else "tpea", bp)] = \
                    C["eV" if eng == "vector" else "eA"]
            tok[("xnrd", q)] = C["eT"]
            tok[("xnTv", t)] = C["eV"]
            tok[("xnTa", t)] = C["eA"]


        SCS = ("scsA", "scsB", "scsC", "scsD")

        def p1a(gt):
            s, t = gt // NTS, gt % NTS
            p = gt % 2
            q4 = gt % 4
            scs = SCS[q4]
            dma("scalar", cosb[q4][:, 0:512], cos_h[t * 128:(t + 1) * 128, :],
                scs, waits=[("eV", tok.get(("ropeqt", gt - 4), 0)),
                            ("eP", tok.get(("ropekt", gt - 4), 0))])
            dma("scalar", cosb[q4][:, 512:1024],
                sin_h[t * 128:(t + 1) * 128, :], scs)
            tok[("csr", gt)] = C[scs]
            sv = "svsA" if p == 0 else "svsB"
            vv = vt_st[p].rearrange("q (h w) -> q h w", h=NH)
            for grp in range(2):
                for c in range(NCH):
                    qv = qwS[:, c * D:(c + 1) * D].rearrange(
                        "k (h i two) -> k h two i", two=2, i=32)
                    kv = kwS[:, c * D:(c + 1) * D].rearrange(
                        "k (h i two) -> k h two i", two=2, i=32)
                    lhs = xnT[:, c * T + gt * 128:c * T + (gt + 1) * 128]
                    hs = slice(8 * grp, 8 * grp + 8)
                    PG = PA if grp == 0 else PB
                    PVg = PC6 if grp == 0 else PC7
                    rhss = [qv[:, hs, :, :], kv[:, hs, :, :],
                            vwS[:, c * D + 512 * grp:
                                c * D + 512 * grp + 512]]
                    outs_ = [PG[:, 0:512], PG[:, 512:1024], PVg[:, :]]
                    for bi3 in range(3):
                        wts = [("sw3", w3), ("eV", tok[("xnTv", gt)]),
                               ("eA", tok[("xnTa", gt)])]
                        if c == 0:
                            wts.append(("eA", tok.get(("pjev", grp), 0)))
                            if grp == 1 and bi3 == 2:
                                wts.append(("eA", wgT_done))
                        W("tensor",
                          lambda e, o_=outs_[bi3], lhs=lhs,
                          rhs=rhss[bi3], c=c:
                          e.matmul(o_, lhs, rhs,
                                   start=(c == 0), stop=(c == NCH - 1)),
                          waits=wts, inc="eT")
                mm = C["eT"]
                PG = PA if grp == 0 else PB
                PVg = PC6 if grp == 0 else PC7
                W("scalar", lambda e, q4=q4, grp=grp, PG=PG: e.copy(
                    qt_st[q4][:, 512 * grp:512 * grp + 512],
                    PG[:, 0:512]),
                  waits=[("eT", mm),
                         ("eV", tok.get(("ropeqt", gt - 4), 0)),
                         ("eP", tok.get(("ropekt", gt - 4), 0))], inc="eA")
                W("scalar", lambda e, q4=q4, grp=grp, PG=PG: e.copy(
                    kt_st[q4][:, 512 * grp:512 * grp + 512],
                    PG[:, 512:1024]), inc="eA")
                W("scalar", lambda e, vv=vv, grp=grp, PVg=PVg: e.copy(
                    vv[:, 8 * grp:8 * grp + 8, 0:64],
                    PVg[:, :].rearrange("q (h d) -> q h d", h=8)),
                  waits=[(sv, 16 * (gt // 2))], inc="eA")
                tok[("pjev", grp)] = C["eA"]
            if gt < 2:
                W("vector", lambda e, vv=vv: e.memset(vv[:, :, 64:65],
                                                      1.0), inc="eV")
                W("vector", lambda e, vv=vv: e.memset(vv[:, :, 65:66],
                                                      0.0), inc="eV")
                tok[("vones", p)] = C["eV"]
            tok[("pjevt", gt)] = C["eA"]
            dma("gpsimd", v_sp[s, t * 128:(t + 1) * 128, :],
                vt_st[p][:, :], sv,
                waits=[("eA", C["eA"]), ("eV", tok.get(("vones", p), 0))])

        def p1b(gt):
            s, t = gt // NTS, gt % NTS
            p = gt % 2
            q4 = gt % 4
            pjev = tok[("pjevt", gt)]
            csr = tok[("csr", gt)]
            scs = SCS[q4]
            qe = qr_st[p].rearrange("q (h two i) -> q h two i", two=2, i=32)
            se_ = qt_st[q4].rearrange("q (h two i) -> q h two i", two=2, i=32)
            cs_ = cosb[q4][:, 0:512].rearrange("q (h i) -> q h i", h=NH)
            sn_ = cosb[q4][:, 512:1024].rearrange("q (h i) -> q h i", h=NH)
            rh = lambda a: a.rearrange("q (h i) -> q h i", h=NH)
            sq_ = "sqsA" if p == 0 else "sqsB"
            W("vector", lambda e, se_=se_, cs_=cs_: e.tensor_mul(
                rh(rt[0][:, :]), se_[:, :, 0, :], cs_),
              waits=[("eA", pjev), (scs, csr), ("eV", wg_split),
                     ("eT", wgT_tp),
                     ("eV", tok.get(("ropeqt", gt - 1), 0))], inc="eV")
            W("vector", lambda e, se_=se_, sn_=sn_: e.tensor_mul(
                rh(rt[1][:, :]), se_[:, :, 1, :], sn_), inc="eV")
            W("vector", lambda e, se_=se_, sn_=sn_: e.tensor_mul(
                rh(rt[2][:, :]), se_[:, :, 0, :], sn_), inc="eV")
            W("vector", lambda e, se_=se_, cs_=cs_: e.tensor_mul(
                rh(rt[3][:, :]), se_[:, :, 1, :], cs_), inc="eV")
            mv = C["eV"]
            W("vector", lambda e, qe=qe: e.tensor_sub(
                qe[:, :, 0, :], rh(rt[0][:, :]), rh(rt[1][:, :])),
              waits=[("eV", mv), (sq_, 16 * (gt // 2))], inc="eV")
            W("vector", lambda e, qe=qe: e.tensor_add(
                qe[:, :, 1, :], rh(rt[2][:, :]), rh(rt[3][:, :])),
              inc="eV")
            ropeq = C["eV"]
            tok[("ropeqt", gt)] = ropeq
            ke = kr_st[p].rearrange("q (h two i) -> q h two i", two=2, i=32)
            ks_ = kt_st[q4].rearrange("q (h two i) -> q h two i", two=2, i=32)
            sk_ = "sksA" if p == 0 else "sksB"
            W("gpsimd", lambda e, ks_=ks_, cs_=cs_: e.tensor_mul(
                rh(pt[0][:, :]), ks_[:, :, 0, :], cs_),
              waits=[("eA", pjev), (scs, csr), ("eV", wg_split),
                     ("eT", wgT_tp),
                     ("eP", tok.get(("ropekt", gt - 1), 0))], inc="eP")
            W("gpsimd", lambda e, ks_=ks_, sn_=sn_: e.tensor_mul(
                rh(pt[1][:, :]), ks_[:, :, 1, :], sn_), inc="eP")
            mp = C["eP"]
            W("gpsimd", lambda e, ke=ke: e.tensor_sub(
                ke[:, :, 0, :], rh(pt[0][:, :]), rh(pt[1][:, :])),
              waits=[("eP", mp), (sk_, 16 * (gt // 2))], inc="eP")
            W("gpsimd", lambda e, ks_=ks_, sn_=sn_: e.tensor_mul(
                rh(pt[0][:, :]), ks_[:, :, 0, :], sn_),
              waits=[("eP", C["eP"])], inc="eP")
            W("gpsimd", lambda e, ks_=ks_, cs_=cs_: e.tensor_mul(
                rh(pt[1][:, :]), ks_[:, :, 1, :], cs_), inc="eP")
            mp = C["eP"]
            W("gpsimd", lambda e, ke=ke: e.tensor_add(
                ke[:, :, 1, :], rh(pt[0][:, :]), rh(pt[1][:, :])),
              waits=[("eP", mp)], inc="eP")
            ropek = C["eP"]
            tok[("ropekt", gt)] = ropek
            dma("sync", q_sp[s, t * 128:(t + 1) * 128, :],
                qr_st[p][:, :], sq_, waits=[("eV", ropeq)])
            dma("gpsimd", k_sp[s, t * 128:(t + 1) * 128, :],
                kr_st[p][:, :], sk_, waits=[("eP", ropek)])

        def _wtsp_fill(ee, c2, wbt):
            sl2 = wbt % 2
            wts = [("swe", 16 * (ee + 1)), ("swb", 32 * wbt)]
            if wbt >= 2:
                wts.append(("swc", 16 * (wbt - 1)))
            for j in range(2):
                c = c2 * 2 + j
                dma("sync", bnc[sl2][:, j * 1024:(j + 1) * 1024],
                    we16[ee, :, c * 128:(c + 1) * 128], "swb",
                    transpose=True, waits=wts)
                wts = []
            dma("scalar", weTsp_h[ee, :, c2 * 2048:(c2 + 1) * 2048],
                bnc[sl2][:, :], "swc",
                waits=[("swb", 32 * (wbt + 1)), ("swc", 16 * wbt)])
        wtsp_units = []
        _wbt = 0
        for ee in range(NE):
            for c2 in range(4):
                wtsp_units.append(
                    lambda ee=ee, c2=c2, w=_wbt: _wtsp_fill(ee, c2, w))
                _wbt += 1
        _wti = [0]

        def drain_wtsp(n):
            while _wti[0] < len(wtsp_units) and n > 0:
                wtsp_units[_wti[0]]()
                _wti[0] += 1
                n -= 1

        ln_s0(0)
        ln_s0(1)
        ln_s0(2)
        for it in range(NT + 8):
            if it < NT:
                ln_s1(it)
            if 5 <= it < NT + 5:
                ln_s6(it - 5)
            if 1 <= it < NT + 1:
                ln_s2(it - 1)
            if it + 3 < NT:
                ln_s0(it + 3)
            if 2 <= it < NT + 2:
                ln_s3(it - 2)
            if 3 <= it < NT + 3:
                ln_s4(it - 3)
            if 4 <= it < NT + 4:
                ln_s5(it - 4)
            if 8 <= it < 8 + 2 * NE and (it - 8) % 2 == 0:
                ei = (it - 8) // 2
                dma("gpsimd", we16[ei, :, :], we_h[ei, :, :], "swe",
                    waits=[("swe", 16 * ei)])
            if it == 8 + 2 * NE:
                dma("gpsimd", be16s[0:8, :], be_h[:, :], "sbe")
            if 6 <= it < NT + 6:
                gsp = it - 6
                dma("scalar",
                    xnTs_h.rearrange("p (c w) -> p c w", c=NCH)
                    [:, :, gsp * 128:(gsp + 1) * 128],
                    xnT.rearrange("p (c w) -> p c w", c=NCH)
                    [:, :, gsp * 128:(gsp + 1) * 128], "sxnT",
                    waits=[("sxnT", 16 * gsp), ("eV", tok[("xnTv", gsp)]),
                           ("eA", tok[("xnTa", gsp)])])
            if it >= 8:
                p1b(it - 8)
            if 6 <= it < NT + 6:
                p1a(it - 6)
            if it >= 12:
                drain_wtsp(2)

        drain_wtsp(len(wtsp_units))
        wcT = C["swc"]

        for t in range(NT):
            for c in range(NCH):
                for s3 in range(3):
                    W("tensor",
                      lambda e, t=t, c=c, s3=s3: e.matmul(
                          PC6[:, t * 8:(t + 1) * 8],
                          xnT[:, c * T + t * 128:c * T + (t + 1) * 128],
                          wgT[:, c * 24 + s3 * 8:c * 24 + s3 * 8 + 8],
                          start=(c == 0 and s3 == 0),
                          stop=(c == NCH - 1 and s3 == 2)),
                      waits=[("eA", max(C["eA"] if t == 0 and c == 0 and s3 == 0 else 0, wgT_done))], inc="eT")
        W("vector", lambda e: e.tensor_copy(lgall[:, :], PC6[:, 0:256]),
          waits=[("eT", C["eT"]), ("eV", C["eV"])], inc="eV")
        d3 = lambda a: a.rearrange("q (t e) -> q t e", e=8)
        bc = lambda a: a.unsqueeze(2).broadcast_to([128, NT, 8])
        W("vector", lambda e: e.tensor_copy(lgall[:, :], PC6[:, 0:256]),
          waits=[("eT", C["eT"]), ("eV", C["eV"])], inc="eV")
        W("vector", lambda e: e.reduce_max(v1a[:, :], d3(lgall), axis=AX.X),
          waits=[("eV", C["eV"])], inc="eV")
        W("vector", lambda e: e.tensor_tensor(
            d3(eq16[:, 0:256]), d3(lgall), bc(v1a[:, :]), op=ALU.is_equal),
          waits=[("eV", C["eV"])], inc="eV")
        W("vector", lambda e: e.scalar_tensor_tensor(
            l2a[:, :], eq16[:, 0:256], -1e30, lgall[:, :],
            op0=ALU.mult, op1=ALU.add),
          waits=[("eV", C["eV"])], inc="eV")
        W("vector", lambda e: e.reduce_max(v2a[:, :], d3(l2a), axis=AX.X),
          waits=[("eV", C["eV"])], inc="eV")
        W("vector", lambda e: e.tensor_tensor(
            d3(eq16[:, 256:512]), d3(l2a), bc(v2a[:, :]), op=ALU.is_equal),
          waits=[("eV", C["eV"])], inc="eV")
        W("vector", lambda e: e.tensor_sub(dst[:, :], v2a[:, :], v1a[:, :]),
          waits=[("eV", C["eV"])], inc="eV")

        W("scalar", lambda e: e.activation(edt[:, :], dst[:, :], AF.Exp),
          waits=[("eV", C["eV"])], inc="eA")
        W("vector", lambda e: e.tensor_scalar(dent[:, :], edt[:, :], 1.0,
                                              None, op0=ALU.add),
          waits=[("eA", C["eA"])], inc="eV")
        W("vector", lambda e: e.reciprocal(w1t[:, :], dent[:, :]),
          waits=[("eV", C["eV"])], inc="eV")
        W("vector", lambda e: e.tensor_mul(w2t[:, :], edt[:, :], w1t[:, :]),
          waits=[("eV", C["eV"])], inc="eV")
        W("vector", lambda e: e.tensor_copy(w116[:, :], w1t[:, :]),
          waits=[("eV", C["eV"])], inc="eV")
        W("vector", lambda e: e.tensor_copy(w216[:, :], w2t[:, :]),
          waits=[("eV", C["eV"])], inc="eV")
        W("vector", lambda e: e.tensor_copy(w132[:, :], w116[:, :]),
          waits=[("eV", C["eV"])], inc="eV")
        W("vector", lambda e: e.tensor_copy(w232[:, :], w216[:, :]),
          waits=[("eV", C["eV"])], inc="eV")
        for t in range(NT):
            W("vector", lambda e, t=t: e.tensor_scalar(
                tmp8[:, :], eq16[:, 256 + t * 8:256 + (t + 1) * 8],
                w232[:, t:t + 1], None, op0=ALU.mult),
              waits=[("eV", C["eV"])], inc="eV")
            W("vector", lambda e, t=t: e.scalar_tensor_tensor(
                wm32[:, t * 8:(t + 1) * 8], eq16[:, t * 8:(t + 1) * 8],
                w132[:, t:t + 1], tmp8[:, :], op0=ALU.mult, op1=ALU.add),
              waits=[("eV", C["eV"])], inc="eV")
            W("vector", lambda e, t=t: e.tensor_copy(
                wm16[:, t * 8:(t + 1) * 8], wm32[:, t * 8:(t + 1) * 8]),
              waits=[("eV", C["eV"])], inc="eV")
            W("vector", lambda e, t=t: e.tensor_copy(
                wm32[:, t * 8:(t + 1) * 8], wm16[:, t * 8:(t + 1) * 8]),
              waits=[("eV", C["eV"])], inc="eV")
            W("tensor", lambda e, t=t: e.transpose(
                pb7f16[0:8, 0:128], wm16[:, t * 8:(t + 1) * 8], idn[:, :]),
              waits=[("eV", C["eV"]), ("eA", C["eA"])], inc="eT")
            W("scalar", lambda e, t=t: e.copy(
                wmT[0:8, t * 128:(t + 1) * 128], pb7f16[0:8, 0:128]),
              waits=[("eT", C["eT"])], inc="eA")

        P.barrier()
        spw = [("sqsA", C["sqsA"]), ("sqsB", C["sqsB"]),
               ("sksA", C["sksA"]), ("sksB", C["sksB"]),
               ("svsA", C["svsA"]), ("svsB", C["svsB"]),
               ("sxnT", C["sxnT"]), ("sxA", C["sxA"]), ("sxB", C["sxB"]),
               ("sxC", C["sxC"]), ("sxD", C["sxD"]),
               ("scsA", C["scsA"]), ("scsB", C["scsB"]),
               ("scsC", C["scsC"]), ("scsD", C["scsD"])]

        dma("scalar", ow128[:, :].rearrange("k (c n) -> k c n", c=NCH),
            ow_h.rearrange("(c k) n -> k c n", k=128), "sow",
            waits=P.bw() + spw)
        oww = C["sow"]
        bew = C["sbe"]

        def xnq_load(k):
            dma("gpsimd", xnq[k % 2][:, :].rearrange("p (c w) -> p c w",
                                                     c=NCH),
                xnTs_h.rearrange("p (c w) -> p c w", c=NCH)
                [:, :, k * 512:(k + 1) * 512], "sgq",
                waits=P.bw() + spw +
                [("sgq", 16 * k), ("sxnT", 16 * (4 * k + 4)),
                 ("eT", tok.get(("xnq_free", k % 2), 0))])
            tok[("xnq_ld", k)] = C["sgq"]

        def wexp_load(k, e):
            ring = "scalar" if e % 2 == 0 else "gpsimd"
            sw = "sweA" if e % 2 == 0 else "sweB"
            wts = P.bw() + spw + [("swc", wcT),
                                  ("eT", tok.get(("wfree", k, e), 0))]
            for hh in range(4):
                dma(ring, weT[e % 2][:, hh * 2048:(hh + 1) * 2048],
                    weTsp_h[e, :, hh * 2048:(hh + 1) * 2048], sw, waits=wts)
                wts = []
            tok[("wld", k, e)] = C[sw]

        def p2_units(k):
            units = []
            sl = k % 2
            if k + 1 < 2 * NQG:
                units.append(lambda k=k: xnq_load(k + 1))

            def bias_unit(t, half, k=k):
                gt = 4 * k + t
                W("tensor", lambda e, gt=gt, half=half: e.matmul(
                    PC6[:, :], wmT[0:8, gt * 128:(gt + 1) * 128],
                    be16s[0:8, half * 512:(half + 1) * 512],
                    start=True, stop=True),
                  waits=[("sbe", bew), ("eA", tok.get("pc6a", 0)),
                         ("eV", tok.get("pc6v", 0))], inc="eT")
                W("scalar", lambda e, t=t, half=half: e.copy(
                    acc32[t][:, half * 512:(half + 1) * 512], PC6[:, :]),
                  waits=P.bw() + spw + [("eT", C["eT"]),
                                  ("eA", tok.get(("accfree", t), 0))] +
                  ([("soA", tok.get(("odma", 0), 0)),
                    ("soB", tok.get(("odma", 1), 0))] if t == 0 else []),
                  inc="eA")
                tok["pc6a"] = C["eA"]
                tok[("accinit", t, half)] = C["eA"]
            for t in range(4):
                for half in range(2):
                    units.append(lambda t=t, half=half: bias_unit(t, half))

            def mm_unit(e, t, half, k=k, sl=sl):
                be_ = e % 2
                sw = "sweA" if be_ == 0 else "sweB"
                wrd = tok[("wld", k, e)]
                gt = 4 * k + t
                for c in range(NCH):
                    wts = [(sw, wrd), ("sgq", tok[("xnq_ld", k)])]
                    if c == 0:
                        wts.append(("eV", tok.get(("pbfree", be_), 0)))
                    W("tensor", lambda e2, c=c, t=t, half=half, be_=be_,
                      sl=sl: e2.matmul(
                          PB[:, 512 * be_:512 * be_ + 512],
                          xnq[sl][:, c * 512 + t * 128:
                                  c * 512 + (t + 1) * 128],
                          weT[be_][:, c * D + half * 512:
                                   c * D + (half + 1) * 512],
                          start=(c == 0), stop=(c == NCH - 1)),
                      waits=wts, inc="eT")
                mm = C["eT"]
                if t == 3 and half == 1:
                    if e + 2 < NE:
                        tok[("wfree", k, e + 2)] = mm
                    else:
                        tok[("wfree", k + 1, e - 6)] = mm
                    if e == NE - 1:
                        tok[("xnq_free", sl)] = mm
                wts2 = [("eT", mm), ("eV", C["eV"])]
                if e == 0:
                    wts2.append(("eA", tok[("accinit", t, half)]))
                W("vector", lambda e2, t=t, half=half, be_=be_, gt=gt,
                  e=e: e2.scalar_tensor_tensor(
                      acc32[t][:, half * 512:(half + 1) * 512],
                      PB[:, 512 * be_:512 * be_ + 512],
                      wm32[:, gt * 8 + e:gt * 8 + e + 1],
                      acc32[t][:, half * 512:(half + 1) * 512],
                      op0=ALU.mult, op1=ALU.add),
                  waits=wts2, inc="eV")
                tok[("pbfree", be_)] = C["eV"]
                if e == NE - 1 and half == 1:
                    tok[("sttdone", t)] = C["eV"]
            for e in range(NE):
                for t in range(4):
                    for half in range(2):
                        units.append(
                            lambda e=e, t=t, half=half: mm_unit(e, t, half))
                if e + 2 < NE:
                    units.append(lambda e=e, k=k: wexp_load(k, e + 2))
                elif k + 1 < 2 * NQG:
                    units.append(lambda e=e, k=k: wexp_load(k + 1, e - 6))

            def silu_unit(t, k=k, sl=sl):
                W("scalar", lambda e, t=t, sl=sl: e.activation(
                    gsl[sl][:, t * 1024:(t + 1) * 1024], acc32[t][:, :],
                    AF.Silu),
                  waits=P.bw() + spw + [("eV", tok[("sttdone", t)]),
                                  ("eA", C["eA"]),
                                  ("eV", tok.get(("gslfree", sl), 0))],
                  inc="eA")
                tok[("accfree", t)] = C["eA"]
                if t == 3:
                    tok[("sild", k)] = C["eA"]
            for t in range(4):
                units.append(lambda t=t: silu_unit(t))
            return units

        def attn_units(s, qg):
            au = []
            nkc = 4 * qg + 4
            ktr = tok[("ktr", s)]
            vld = tok[("vld", s)]
            qrd = tok["qrd"]

            def emit_pv(h, kc):
                p2 = h % 2
                j = kc - 4 * qg
                sbk = kc % 3
                w0 = max(0, j) * 128
                mw = 65 if h == NH - 1 else 128
                pvw = [("eA", tok.get(("expC", kc), 0)), ("svld", vld)]
                if j >= 0:
                    pvw.append(("eP", tok.get(("maskdone", sbk), 0)))
                if kc == 0:
                    pvw.append(("eA", tok.get("pvrecip", 0)))
                    pvw.append(("eV", tok.get("pvevac", 0)))
                W("tensor", lambda e, kc=kc, h=h, sbk=sbk, w0=w0,
                  nkc=nkc, p2=p2, mw=mw: e.matmul(
                      PV2[0:mw, 512 * p2 + w0:512 * p2 + 512],
                      v_sb[:, kc * VF + h * VW:kc * VF + h * VW + mw],
                      expb3[sbk][:, 512 * p2 + w0:512 * p2 + 512],
                      start=(kc == 0), stop=(kc == nkc - 1)),
                  waits=pvw, inc="eT")
                tok[("exprd", sbk)] = C["eT"]

            def emit_tail_a(hp):
                pvmm = tok[("pvmm", hp)]
                W("scalar", lambda e: e.activation(
                    recL[96:97, 0:1024], PV2[64:65, 0:1024], AF.Ln),
                  waits=[("swc", wcT), ("eT", pvmm), ("eA", C["eA"]),
                         ("eA", tok.get("lnrd", 0))], inc="eA")
                W("scalar", lambda e: e.activation(
                    recE[64:65, 0:1024], recL[96:97, 0:1024],
                    AF.Exp, scale=-1.0),
                  waits=[("eA", C["eA"]),
                         ("eT", tok.get("rec16rd", 0))], inc="eA")
                tok["pvrecip"] = C["eA"]
                tok[("recv", hp)] = C["eA"]
                tok["lnrd"] = C["eA"]
                W("vector", lambda e: e.tensor_copy(
                    at2[0:64, :], PV2[0:64, 0:1024]),
                  waits=[("swc", wcT), ("eT", pvmm),
                         ("eV", tok.get("atrd", 0))], inc="eV")
                tok["pvevac"] = C["eV"]
                tok[("atv", hp)] = C["eV"]

            def emit_tail_b(hp):
                W("tensor", lambda e: e.matmul(
                    PC6[0:64, :], ones16[64:65, 0:64],
                    recE[64:65, 0:512], start=True, stop=True),
                  waits=[("eA", tok[("recv", hp)]),
                         ("eA", tok.get("pc6a", 0)),
                         ("eV", tok.get("pc6v", 0))], inc="eT")
                W("tensor", lambda e: e.matmul(
                    PC7[0:64, :], ones16[64:65, 0:64],
                    recE[64:65, 512:1024], start=True, stop=True),
                  waits=[("eV", tok.get("bcrd", 0))], inc="eT")
                tok["rec16rd"] = C["eT"]
                bcmm = C["eT"]
                W("vector", lambda e, hp=hp: e.tensor_mul(
                    attnT[0:64, hp * 512:(hp + 1) * 512],
                    at2[0:64, 0:512], PC6[0:64, :]),
                  waits=[("eV", tok[("atv", hp)]), ("eT", bcmm),
                         ("eT", tok.get("atT_rd", 0)), ("eV", C["eV"])],
                  inc="eV")
                tok["pc6v"] = C["eV"]
                W("vector", lambda e, hp=hp: e.tensor_mul(
                    attnT[64:128, hp * 512:(hp + 1) * 512],
                    at2[0:64, 512:1024], PC7[0:64, :]),
                  inc="eV")
                tok["atrd"] = C["eV"]
                tok["bcrd"] = C["eV"]

            def score_unit(hp, kc):
                ch = hp
                sbk = kc % 3
                j = kc - 4 * qg
                w0 = max(0, j) * 128
                for p2 in range(2):
                    base = p2 * 64
                    wts = [("skT", ktr), ("sqTA", qrd)]
                    if p2 == 0:
                        wts.append(("eA", tok.get("expdone", 0)))
                        if hp == 0 and kc == 0:
                            wts.append(("soA", tok.get("outsA", 0)))
                            wts.append(("soB", tok.get("outsB", 0)))
                    W("tensor", lambda e, base=base, ch=ch, kc=kc,
                      p2=p2: e.matmul(
                          PA[:, 512 * p2:512 * p2 + 512],
                          kT[base:base + 64,
                             ch * S + kc * 128:ch * S + (kc + 1) * 128],
                          qTb[base:base + 64, ch * 512:(ch + 1) * 512],
                          start=True, stop=True),
                      waits=wts, inc="eT")
                ev = (lambda a: a.rearrange("q (hh c) -> q hh c", hh=2)
                      [:, :, w0:512])
                W("scalar", lambda e, sbk=sbk, ev=ev: e.activation(
                    ev(expb3[sbk]), ev(PA), AF.Exp, scale=1.0 / 32.0),
                  waits=[("swc", wcT), ("eT", C["eT"]),
                         ("eT", tok.get(("exprd", sbk), 0))],
                  inc="eA")
                tok["expdone"] = C["eA"]
                tok[("expC", kc)] = C["eA"]
                if j >= 0:
                    tv = (lambda a: a.rearrange("q (hh c) -> q hh c", hh=2)
                          [:, :, j * 128:(j + 1) * 128])
                    W("gpsimd", lambda e, sbk=sbk, tv=tv: e.tensor_mul(
                        tv(expb3[sbk]), tv(expb3[sbk]),
                        tri[:, :].unsqueeze(1).broadcast_to([128, 2, 128])),
                      waits=[("swc", wcT), ("eA", C["eA"]), ("scst", cst),
                             ("eP", C["eP"])], inc="eP")
                    tok[("maskdone", sbk)] = C["eP"]
                if kc >= 2:
                    emit_pv(2 * hp, kc - 2)
                    emit_pv(2 * hp + 1, kc - 2)

            def hp_tail(hp):
                emit_pv(2 * hp, nkc - 2)
                emit_pv(2 * hp + 1, nkc - 2)
                emit_pv(2 * hp, nkc - 1)
                emit_pv(2 * hp + 1, nkc - 1)
                tok[("pvmm", hp)] = C["eT"]
                if hp >= 1:
                    emit_tail_b(hp - 1)
                emit_tail_a(hp)

            for hp in range(NH // 2):
                for kc in range(nkc):
                    au.append(lambda hp=hp, kc=kc: score_unit(hp, kc))
                au.append(lambda hp=hp: hp_tail(hp))
            au.append(lambda: emit_tail_b(NH // 2 - 1))
            return au

        def combine(s, qg, k):
            heads_done = C["eV"]
            r0 = s * S + qg * 512
            sild = tok[("sild", k)]
            for pc in range(2):
                dma("gpsimd", xq2[pc][:, :],
                    x_h[r0 + (pc // 2) * 128:r0 + (pc // 2 + 1) * 128,
                        (pc % 2) * 512:(pc % 2 + 1) * 512], "sxq",
                    waits=[("sxq", C["sxq"]),
                           ("eV", tok.get(("xqfree", pc % 2), 0))])
                tok[("xql", pc)] = C["sxq"]
            for jt in range(4):
                for half in range(2):
                    pc = jt * 2 + half
                    for cp in range(NCH):
                        wts = [("sow", oww), ("eV", heads_done)]
                        if cp == 0:
                            wts.append(("eA", tok["expdone"]))
                            pf = tok.get(("pafree", half))
                            if pf is not None:
                                wts.append(pf)
                        W("tensor", lambda e, jt=jt, half=half, cp=cp:
                          e.matmul(
                              PA[:, 512 * half:512 * half + 512],
                              attnT[0:128, cp * 512 + jt * 128:
                                    cp * 512 + (jt + 1) * 128],
                              ow128[:, cp * D + half * 512:
                                    cp * D + (half + 1) * 512],
                              start=(cp == 0), stop=(cp == NCH - 1)),
                          waits=wts, inc="eT")
                    omm = C["eT"]
                    so = "soA" if pc % 2 == 0 else "soB"
                    ob = acc32[0][:, (pc % 2) * 512:(pc % 2) * 512 + 512]
                    mw_ = [("eT", omm), ("eA", sild), ("eV", C["eV"]),
                           (so, tok.get(("odma", pc % 2), 0))]
                    W("vector", lambda e, jt=jt, half=half, sl=k % 2,
                      ob=ob: e.tensor_tensor(
                          ob,
                          PA[:, 512 * half:512 * half + 512],
                          gsl[sl][:, jt * 1024 + half * 512:
                                  jt * 1024 + (half + 1) * 512],
                          op=ALU.mult),
                      waits=mw_, inc="eV")
                    tok[("pafree", half)] = ("eV", C["eV"])
                    W("vector", lambda e, pc=pc, ob=ob:
                      e.tensor_tensor(ob, ob, xq2[pc % 2][:, :],
                                      op=ALU.add),
                      waits=[("sxq", tok[("xql", pc)]), ("eV", C["eV"])],
                      inc="eV")
                    tok[("xqfree", pc % 2)] = C["eV"]
                    dma("sync",
                        out_h[r0 + jt * 128:r0 + (jt + 1) * 128,
                              half * 512:(half + 1) * 512], ob, so,
                        waits=[("eV", C["eV"])])
                    tok[("odma", pc % 2)] = C[so]
                    if pc + 2 < 8:
                        npc = pc + 2
                        njt, nhalf = npc // 2, npc % 2
                        dma("gpsimd", xq2[npc % 2][:, :],
                            x_h[r0 + njt * 128:r0 + (njt + 1) * 128,
                                nhalf * 512:(nhalf + 1) * 512], "sxq",
                            waits=[("sxq", C["sxq"]),
                                   ("eV", tok.get(("xqfree", npc % 2), 0))])
                        tok[("xql", npc)] = C["sxq"]
            tok["atT_rd"] = C["eT"]
            tok["outsA"] = C["soA"]
            tok["outsB"] = C["soB"]

        wexp_load(0, 0)
        wexp_load(0, 1)
        xnq_load(0)
        for s in range(NSEQ):
            kw_ = P.bw() + spw + [("eT", tok.get("kT_free", 0))]
            for c in range(NCH):
                dma("sync", c16(AR1, c * 4096, 2048)[:, :],
                    k_sp[s, :, c * 128:(c + 1) * 128], "skT", transpose=True,
                    waits=kw_)
                kw_ = []
            tok[("ktr", s)] = C["skT"]
            dma("gpsimd", v_sb[:, :].rearrange("q (t f) -> q t f", t=NTS),
                v_sp[s].rearrange("(t q) f -> q t f", q=128), "svld",
                waits=P.bw() + spw + [("eT", tok.get("v_free", 0))])
            tok[("vld", s)] = C["svld"]
            for qg in range(NQG):
                k = s * NQG + qg
                qw_ = P.bw() + spw + [("eT", tok.get("qT_free", 0))]
                for c in range(NCH):
                    dma("sync", qTb[:, c * 512:(c + 1) * 512],
                        q_sp[s, qg * 512:(qg + 1) * 512,
                             c * 128:(c + 1) * 128], "sqTA",
                        transpose=True, waits=qw_)
                    qw_ = []
                tok["qrd"] = C["sqTA"]
                au = attn_units(s, qg)
                pu = p2_units(k)
                j = 0
                for i, u in enumerate(au):
                    share = max(min(6, len(pu)),
                                ((i + 1) * len(pu)) // len(au))
                    while j < share:
                        pu[j]()
                        j += 1
                    u()
                while j < len(pu):
                    pu[j]()
                    j += 1
                tok["qT_free"] = C["eT"]
                combine(s, qg, k)
            tok["kT_free"] = C["eT"]
            tok["v_free"] = C["eT"]

        W("scalar", None, waits=[("soA", C["soA"]), ("soB", C["soB"])])
        P.barrier()

        with nc.Block() as blk:
            P.replay(blk)
    return nc


_CACHE = {}


def _constants():
    if "c" in _CACHE:
        return _CACHE["c"]
    inv = 1.0 / (10000.0 ** (np.arange(0, HD, 2, dtype=np.float64) / HD))
    pos = np.arange(S, dtype=np.float64)
    fr = pos[:, None] * inv[None, :]
    cos = np.tile(np.cos(fr), (1, NH)).astype(np.float16)
    sin = np.tile(np.sin(fr), (1, NH)).astype(np.float16)
    tri = np.triu(np.ones((128, 128), dtype=np.float16))
    idn = np.eye(128, dtype=np.float16)
    _CACHE["c"] = (cos, sin, tri, idn)
    return _CACHE["c"]


def kernel(L0_attn_x, L0_attn_qw, L0_attn_kw, L0_attn_vw, L0_attn_ow,
           L0_ffn_Wg, L0_ffn_We, L0_ffn_be):
    x = np.ascontiguousarray(np.asarray(L0_attn_x, dtype=np.float16))
    qw = np.ascontiguousarray(np.asarray(L0_attn_qw, dtype=np.float16))
    kw = np.ascontiguousarray(np.asarray(L0_attn_kw, dtype=np.float16))
    vw = np.ascontiguousarray(np.asarray(L0_attn_vw, dtype=np.float16))
    ow = np.ascontiguousarray(np.asarray(L0_attn_ow, dtype=np.float16))
    wg = np.ascontiguousarray(np.asarray(L0_ffn_Wg, dtype=np.float32))
    we = np.ascontiguousarray(np.asarray(L0_ffn_We, dtype=np.float32))
    be = np.ascontiguousarray(np.asarray(L0_ffn_be, dtype=np.float32))
    cos, sin, tri, idn = _constants()

    nc = build_nc()
    in_maps = []
    for i in range(NCORE):
        in_maps.append({
            "x_c": np.ascontiguousarray(
                x[2 * i:2 * i + 2].reshape(T, D)),
            "qw": qw, "kw": kw, "vw": vw, "ow": ow,
            "wg": wg, "we": we, "be": be,
            "c_cos": cos, "c_sin": sin, "c_tri": tri, "c_idn": idn,
        })
    res = run_bass_kernel_spmd(nc, in_maps, core_ids=list(range(NCORE)))
    out = np.concatenate(
        [np.asarray(r["out_c"]) for r in res.results], axis=0)
    return out.reshape(16, S, D).astype(np.float32, copy=False)


# version 25
# speedup vs baseline: 1.1681x; 1.1477x over previous
import contextlib

import numpy as np

import concourse.bass as bass
import concourse.mybir as mybir
from concourse.bass_utils import run_bass_kernel_spmd

F32 = mybir.dt.float32
F16 = mybir.dt.float16
U8 = mybir.dt.uint8
ALU = mybir.AluOpType
AF = mybir.ActivationFunctionType
AX = mybir.AxisListType

NCORE = 8
D = 1024
S = 2048
NSEQ = 2
T = NSEQ * S
NT = T // 128
NTS = S // 128
NH = 16
HD = 64
NCH = 8
NE = 8
NQG = 4
EPS16 = float(np.float16(1e-5))
VW = 66
VF = NH * VW


class Prog:

    def __init__(self, nc):
        self.nc = nc
        self.ops = []
        self.sems = {}
        self.cnt = {}
        self.wm = {}

    def decl(self, ctx, *names):
        for n in names:
            self.sems[n] = ctx.enter_context(self.nc.semaphore(n))
            self.cnt[n] = 0

    def emit(self, eng, fn, waits=(), inc=None, amt=1):
        ws = []
        for (sn, v) in waits:
            if v is None or v <= 0:
                continue
            k = (eng, sn)
            if self.wm.get(k, 0) >= v:
                continue
            self.wm[k] = v
            ws.append((sn, v))
        incs = []
        if inc is not None:
            self.cnt[inc] += amt
            incs.append((inc, amt))
        self.ops.append((eng, ws, fn, incs))
        return self.cnt[inc] if inc is not None else None

    def barrier(self):
        engs = ("vector", "scalar", "tensor", "gpsimd")
        marks = {}
        for e in engs:
            sn = "bar_" + e
            self.cnt[sn] += 1
            self.ops.append((e, [], None, [(sn, 1)]))
            marks[e] = self.cnt[sn]
        for e in engs:
            self.emit(e, None,
                      waits=[("bar_" + o, marks[o]) for o in engs if o != e])
        self.last_bar = [("bar_" + o, marks[o]) for o in engs]

    def bw(self):
        return list(getattr(self, "last_bar", []))

    def replay(self, blk):
        sems, ops = self.sems, self.ops

        def runner(name):
            def body(eng):
                for (e, ws, fn, incs) in ops:
                    if e != name:
                        continue
                    for (sn, v) in ws:
                        eng.wait_ge(sems[sn], v)
                    if fn is None:
                        if incs:
                            ins = eng.drain()
                            for (sn, a) in incs:
                                ins.then_inc(sems[sn], a)
                        continue
                    ins = fn(eng)
                    for (sn, a) in incs:
                        ins.then_inc(sems[sn], a)
            return body

        blk.vector(runner("vector"))
        blk.scalar(runner("scalar"))
        blk.tensor(runner("tensor"))
        blk.gpsimd(runner("gpsimd"))
        blk.sync(runner("sync"))


def build_nc():
    nc = bass.Bass("TRN2", target_bir_lowering=False)

    x_h = nc.dram_tensor("x_c", [T, D], F16, kind="ExternalInput").ap()
    qw_h = nc.dram_tensor("qw", [D, D], F16, kind="ExternalInput").ap()
    kw_h = nc.dram_tensor("kw", [D, D], F16, kind="ExternalInput").ap()
    vw_h = nc.dram_tensor("vw", [D, D], F16, kind="ExternalInput").ap()
    ow_h = nc.dram_tensor("ow", [D, D], F16, kind="ExternalInput").ap()
    wg_h = nc.dram_tensor("wg", [NE, D], F32, kind="ExternalInput").ap()
    we_h = nc.dram_tensor("we", [NE, D, D], F32, kind="ExternalInput").ap()
    be_h = nc.dram_tensor("be", [NE, D], F32, kind="ExternalInput").ap()
    cos_h = nc.dram_tensor("c_cos", [S, 512], F16, kind="ExternalInput").ap()
    sin_h = nc.dram_tensor("c_sin", [S, 512], F16, kind="ExternalInput").ap()
    tri_h = nc.dram_tensor("c_tri", [128, 128], F16, kind="ExternalInput").ap()
    idn_h = nc.dram_tensor("c_idn", [128, 128], F16, kind="ExternalInput").ap()
    out_h = nc.dram_tensor("out_c", [T, D], F32, kind="ExternalOutput").ap()

    xn_sp = nc.dram_tensor("xn_sp", [T, D], F16, kind="Internal").ap()
    q_sp = nc.dram_tensor("q_sp", [NSEQ, S, D], F16, kind="Internal").ap()
    k_sp = nc.dram_tensor("k_sp", [NSEQ, S, D], F16, kind="Internal").ap()
    v_sp = nc.dram_tensor("v_sp", [NSEQ, S, VF], F16, kind="Internal").ap()
    g_sp = nc.dram_tensor("g_sp", [T, D], F16, kind="Internal").ap()
    we16 = nc.dram_tensor("we16", [NE, D, D], F16, kind="Internal").ap()

    ctx = contextlib.ExitStack()
    with ctx:
        sbt = lambda name, shape, dt: ctx.enter_context(
            nc.sbuf_tensor(name, shape, dt))[:]
        AR1 = sbt("AR1", [128, 65536], U8)
        AR2 = sbt("AR2", [128, 115712], U8)

        def c16(ar, off, cols):
            return ar[:, off:off + 2 * cols].bitcast(F16)

        def c32(ar, off, cols):
            return ar[:, off:off + 4 * cols].bitcast(F32)

        xnT = c16(AR1, 0, 32768)
        kT = c16(AR1, 0, 16384)
        qT = [c16(AR1, 32768, 4096), c16(AR1, 40960, 4096)]
        attnT = c16(AR1, 49152, 8192)

        xst = [c16(AR2, 92672 + 2048 * i, 1024) for i in range(4)]
        xcn16 = [c16(AR2, 100864 + 2048 * i, 1024) for i in range(4)]
        sqd16 = c16(AR2, 109056, 1024)
        lgall = c32(AR2, 111104, 256)
        l2a = c32(AR2, 112128, 256)
        qwS = c16(AR2, 0, 8192)
        kwS = c16(AR2, 16384, 8192)
        vwS = c16(AR2, 32768, 8192)
        cosb = [c16(AR2, 49152 + 2048 * i, 1024) for i in range(4)]
        qt_st = [c16(AR2, 57344 + 2048 * i, 1024) for i in range(4)]
        kt_st = [c16(AR2, 65536 + 2048 * i, 1024) for i in range(4)]
        vt_st = [c16(AR2, 73728, VF), c16(AR2, 76032, VF)]
        qr_st = [c16(AR2, 78336, 1024), c16(AR2, 80384, 1024)]
        kr_st = [c16(AR2, 82432, 1024), c16(AR2, 84480, 1024)]
        rt = [c16(AR2, 86528 + 1024 * i, 512) for i in range(4)]
        pt = [c16(AR2, 90624 + 1024 * i, 512) for i in range(2)]
        weT = [c16(AR2, 16384 * i, 8192) for i in range(4)]
        acc32 = [c32(AR2, 65536 + 4096 * i, 1024) for i in range(8)]
        g_st = [c16(AR2, 98304, 1024), c16(AR2, 100352, 1024)]
        v_sb = c16(AR2, 0, 16896)
        ow128 = c16(AR2, 34816, 8192)
        gq = c16(AR2, 67584, 4096)
        xq = c16(AR2, 75776, 4096)
        outq = c32(AR2, 83968, 4096)
        expb3 = [c16(AR2, 100352 + 2048 * i, 1024) for i in range(3)]
        at2 = c32(AR2, 106496, 1024)
        recL = c32(AR2, 51200, 1024)
        recE = c16(AR2, 55296, 1024)

        tri = sbt("tri", [128, 128], F16)
        idn = sbt("idn", [128, 128], F16)
        ones16 = sbt("ones16", [128, 64], F16)

        wm32 = sbt("wm32", [128, NT * 8], F32)
        wm16 = sbt("wm16", [128, NT * 8], F16)
        wmT = sbt("wmT", [8, NT * 128], F16)
        wgT = sbt("wgT", [128, NCH * 24], F16)
        wgs = sbt("wgs", [8, 3 * D], F16)
        wgf = sbt("wgf", [8, D], F32)
        wgr = sbt("wgr", [8, D], F32)
        be16s = sbt("be16s", [8, D], F16)
        m32 = [sbt("m32_%d" % i, [128, 1], F32) for i in range(4)]
        m16 = [sbt("m16_%d" % i, [128, 1], F16) for i in range(4)]
        mneg = [sbt("mneg_%d" % i, [128, 1], F32) for i in range(4)]
        ssum = [sbt("ssum_%d" % i, [128, 1], F32) for i in range(4)]
        v16t = [sbt("v16_%d" % i, [128, 1], F16) for i in range(4)]
        s32t = [sbt("s32_%d" % i, [128, 1], F32) for i in range(4)]
        s16t = [sbt("s16_%d" % i, [128, 1], F16) for i in range(4)]
        r32t = [sbt("r32_%d" % i, [128, 1], F32) for i in range(4)]
        r16t = [sbt("r16_%d" % i, [128, 1], F16) for i in range(4)]
        rc32 = [sbt("rc_%d" % i, [128, 1], F32) for i in range(4)]
        v1a = sbt("v1a", [128, NT], F32)
        v2a = sbt("v2a", [128, NT], F32)
        dst = sbt("dst", [128, NT], F32)
        eq16 = sbt("eq16", [128, 2 * NT * 8], F16)
        edt = sbt("edt", [128, NT], F32)
        dent = sbt("dent", [128, NT], F32)
        w1t = sbt("w1t", [128, NT], F32)
        w2t = sbt("w2t", [128, NT], F32)
        w116 = sbt("w116", [128, NT], F16)
        w216 = sbt("w216", [128, NT], F16)
        w132 = sbt("w132", [128, NT], F32)
        w232 = sbt("w232", [128, NT], F32)
        tmp8 = sbt("tmp8", [128, 8], F32)

        PA = ctx.enter_context(nc.psum_tensor("PA", [128, 1024], F32))[:]
        PB = ctx.enter_context(nc.psum_tensor("PB", [128, 1024], F32))[:]
        PV2 = ctx.enter_context(nc.psum_tensor("PV2", [128, 1024], F32))[:]
        PC6 = ctx.enter_context(nc.psum_tensor("PC6", [128, 512], F32))[:]
        PC7 = ctx.enter_context(nc.psum_tensor("PC7", [128, 512], F32))[:]
        pb7f16 = PC7.bitcast(F16)

        P = Prog(nc)
        P.decl(ctx, "eV", "eA", "eT", "eP",
               "bar_vector", "bar_scalar", "bar_tensor", "bar_gpsimd",
               "sxA", "sxB", "sxC", "sxD", "sspA", "sspB", "sspC", "sspD",
               "sxnT", "scst",
               "scsA", "scsB", "scsC", "scsD", "sw3", "sqsA", "sqsB", "sksA", "sksB",
               "svsA", "svsB", "swe", "sweA", "sweB", "sbe", "sgsA", "sgsB",
               "skT", "svld", "sqTA", "sqTB", "sgq", "sxq", "soA", "soB",
               "sow")
        W = P.emit
        C = P.cnt

        def dma(q, out, in_, sem, transpose=False, waits=()):
            def fn(e, out=out, in_=in_, tr=transpose):
                if tr:
                    return e.dma_start(out, in_, transpose=True)
                return e.dma_start(out, in_)
            return W(q, fn, waits=waits, inc=sem, amt=16)

        tok = {}

        dma("sync", qwS[:, :].rearrange("k (c n) -> k c n", c=NCH),
            qw_h.rearrange("(c k) n -> k c n", k=128), "sw3")
        dma("sync", kwS[:, :].rearrange("k (c n) -> k c n", c=NCH),
            kw_h.rearrange("(c k) n -> k c n", k=128), "sw3")
        dma("sync", vwS[:, :].rearrange("k (c n) -> k c n", c=NCH),
            vw_h.rearrange("(c k) n -> k c n", k=128), "sw3")
        dma("sync", tri[:, :], tri_h[:, :], "scst")
        dma("sync", idn[:, :], idn_h[:, :], "scst")
        dma("sync", wgf[0:8, :], wg_h[:, :], "scst")
        cst = C["scst"]
        W("vector", lambda e: e.memset(ones16[:, :], 1.0), inc="eV")

        W("vector", lambda e: e.tensor_copy(wgs[0:8, 0:D], wgf[0:8, :]),
          waits=[("scst", cst)], inc="eV")
        W("vector", lambda e: e.tensor_sub(wgr[0:8, :], wgf[0:8, :],
                                           wgs[0:8, 0:D]),
          waits=[("eV", C["eV"])], inc="eV")
        W("vector", lambda e: e.tensor_copy(wgs[0:8, D:2 * D], wgr[0:8, :]),
          waits=[("eV", C["eV"])], inc="eV")
        W("vector", lambda e: e.tensor_sub(wgr[0:8, :], wgr[0:8, :],
                                           wgs[0:8, D:2 * D]),
          waits=[("eV", C["eV"])], inc="eV")
        W("vector", lambda e: e.tensor_copy(wgs[0:8, 2 * D:3 * D], wgr[0:8, :]),
          waits=[("eV", C["eV"])], inc="eV")
        wg_split = C["eV"]
        for c in range(NCH):
            for s3 in range(3):
                W("tensor",
                  lambda e, c=c, s3=s3: e.transpose(
                      pb7f16[0:128, 0:8],
                      wgs[0:8, s3 * D + c * 128:s3 * D + (c + 1) * 128],
                      idn[0:8, 0:8]),
                  waits=[("eV", wg_split), ("scst", cst), ("eA", C["eA"])],
                  inc="eT")
                W("scalar",
                  lambda e, c=c, s3=s3: e.copy(
                      wgT[:, c * 24 + s3 * 8:c * 24 + s3 * 8 + 8],
                      pb7f16[0:128, 0:8]),
                  waits=[("eT", C["eT"])], inc="eA")
        wgT_done = C["eA"]
        wgT_tp = C["eT"]
        w3 = C["sw3"]

        SSP = ("sspA", "sspB", "sspC", "sspD")
        SX = ("sxA", "sxB", "sxC", "sxD")

        def ln_s0(t):
            q = t % 4
            dma("sync", xst[q][:, :], x_h[t * 128:(t + 1) * 128, :], SX[q],
                waits=[("eA", tok.get(("xcd", q), 0))])

        def ln_s1(t):
            q = t % 4
            W("vector", lambda e, q=q: e.reduce_sum(m32[q][:, :],
                                                    xst[q][:, :], axis=AX.X),
              waits=[(SX[q], C[SX[q]]), ("eV", C["eV"])],
              inc="eV")
            W("vector", lambda e, q=q: e.tensor_scalar(
                m16[q][:, :], m32[q][:, :], 1.0 / D, None, op0=ALU.mult),
              waits=[("eV", C["eV"])], inc="eV")
            W("vector", lambda e, q=q: e.tensor_scalar(
                mneg[q][:, :], m16[q][:, :], -1.0, None, op0=ALU.mult),
              waits=[("eV", C["eV"])], inc="eV")
            tok[("mnegd", q)] = C["eV"]

        def ln_s2(t):
            q = t % 4
            W("scalar", lambda e, q=q: e.activation(
                xcn16[q][:, :], xst[q][:, :], AF.Identity,
                bias=mneg[q][:, :], scale=1.0),
              waits=[("eV", tok[("mnegd", q)]),
                     ("eT", tok.get(("xnrd", q), 0))], inc="eA")
            tok[("xcd", q)] = C["eA"]
            W("scalar", lambda e, q=q: e.activation(
                sqd16[:, :], xcn16[q][:, :], AF.Square,
                accum_out=ssum[q][:, :]),
              waits=[("eA", C["eA"]), ("eV", tok.get(("vrd", q), 0))],
              inc="eA")
            tok[("sqdone", q)] = C["eA"]

        def ln_s3(t):
            q = t % 4
            W("vector", lambda e, q=q: e.tensor_scalar(
                v16t[q][:, :], ssum[q][:, :], 1.0 / D, None, op0=ALU.mult),
              waits=[("eA", tok[("sqdone", q)]), ("eV", C["eV"])], inc="eV")
            tok[("vrd", q)] = C["eV"]
            W("vector", lambda e, q=q: e.tensor_scalar(
                s16t[q][:, :], v16t[q][:, :], EPS16, None, op0=ALU.add),
              waits=[("eV", C["eV"])], inc="eV")
            tok[("s16d", q)] = C["eV"]

        def ln_s4(t):
            q = t % 4
            W("scalar", lambda e, q=q: e.activation(r16t[q][:, :],
                                                    s16t[q][:, :], AF.Sqrt),
              waits=[("eV", tok[("s16d", q)]), ("eA", C["eA"])], inc="eA")
            tok[("sqrtd", q)] = C["eA"]

        def ln_s5(t):
            q = t % 4
            W("vector", lambda e, q=q: e.reciprocal(rc32[q][:, :],
                                                    r16t[q][:, :]),
              waits=[("eA", tok[("sqrtd", q)]), ("eV", C["eV"])], inc="eV")
            W("vector", lambda e, q=q: e.tensor_scalar(
                xcn16[q][:, :], xcn16[q][:, :], rc32[q][:, :], None,
                op0=ALU.mult),
              waits=[("eV", C["eV"]), ("eA", tok[("sqdone", q)])], inc="eV")
            tok[("xc16", q)] = C["eV"]

        PV2f16 = PV2.bitcast(F16)
        tp16 = [PV2f16[:, 0:128], PV2f16[:, 1024:1152]]

        def ln_s6(t):
            q = t % 4
            for c in range(NCH):
                bp = c % 2
                W("tensor", lambda e, q=q, c=c, bp=bp: e.transpose(
                    tp16[bp][:, :], xcn16[q][:, c * 128:(c + 1) * 128],
                    idn[:, :]),
                  waits=[("eV", tok[("xc16", q)]), ("scst", cst),
                         ("eV", tok.get(("tpev", bp), 0)),
                         ("eA", tok.get(("tpea", bp), 0))], inc="eT")
                eng = "vector" if c % 4 < 2 else "scalar"
                def ef(e, q=q, c=c, bp=bp, t=t):
                    dst = xnT[:, c * T + t * 128:c * T + (t + 1) * 128]
                    if hasattr(e, "activation"):
                        return e.copy(dst, tp16[bp][:, :])
                    return e.tensor_copy(dst, tp16[bp][:, :])
                W(eng, ef, waits=[("eT", C["eT"])],
                  inc="eV" if eng == "vector" else "eA")
                tok[("tpev" if eng == "vector" else "tpea", bp)] = \
                    C["eV" if eng == "vector" else "eA"]
            tok[("xnrd", q)] = C["eT"]
            tok[("xnTv", t)] = C["eV"]
            tok[("xnTa", t)] = C["eA"]


        SCS = ("scsA", "scsB", "scsC", "scsD")

        def p1a(gt):
            s, t = gt // NTS, gt % NTS
            p = gt % 2
            q4 = gt % 4
            scs = SCS[q4]
            dma("sync", cosb[q4][:, 0:512], cos_h[t * 128:(t + 1) * 128, :],
                scs, waits=[("eV", tok.get(("ropeqt", gt - 4), 0)),
                            ("eP", tok.get(("ropekt", gt - 4), 0))])
            dma("sync", cosb[q4][:, 512:1024],
                sin_h[t * 128:(t + 1) * 128, :], scs)
            tok[("csr", gt)] = C[scs]
            sv = "svsA" if p == 0 else "svsB"
            vv = vt_st[p].rearrange("q (h w) -> q h w", h=NH)
            for grp in range(2):
                for c in range(NCH):
                    qv = qwS[:, c * D:(c + 1) * D].rearrange(
                        "k (h i two) -> k h two i", two=2, i=32)
                    kv = kwS[:, c * D:(c + 1) * D].rearrange(
                        "k (h i two) -> k h two i", two=2, i=32)
                    lhs = xnT[:, c * T + gt * 128:c * T + (gt + 1) * 128]
                    hs = slice(8 * grp, 8 * grp + 8)
                    PG = PA if grp == 0 else PB
                    PVg = PC6 if grp == 0 else PC7
                    rhss = [qv[:, hs, :, :], kv[:, hs, :, :],
                            vwS[:, c * D + 512 * grp:
                                c * D + 512 * grp + 512]]
                    outs_ = [PG[:, 0:512], PG[:, 512:1024], PVg[:, :]]
                    for bi3 in range(3):
                        wts = [("sw3", w3), ("eV", tok[("xnTv", gt)]),
                               ("eA", tok[("xnTa", gt)])]
                        if c == 0:
                            wts.append(("eA", tok.get(("pjev", grp), 0)))
                            if grp == 1 and bi3 == 2:
                                wts.append(("eA", wgT_done))
                        W("tensor",
                          lambda e, o_=outs_[bi3], lhs=lhs,
                          rhs=rhss[bi3], c=c:
                          e.matmul(o_, lhs, rhs,
                                   start=(c == 0), stop=(c == NCH - 1)),
                          waits=wts, inc="eT")
                mm = C["eT"]
                PG = PA if grp == 0 else PB
                PVg = PC6 if grp == 0 else PC7
                W("scalar", lambda e, q4=q4, grp=grp, PG=PG: e.copy(
                    qt_st[q4][:, 512 * grp:512 * grp + 512],
                    PG[:, 0:512]),
                  waits=[("eT", mm),
                         ("eV", tok.get(("ropeqt", gt - 4), 0)),
                         ("eP", tok.get(("ropekt", gt - 4), 0))], inc="eA")
                W("scalar", lambda e, q4=q4, grp=grp, PG=PG: e.copy(
                    kt_st[q4][:, 512 * grp:512 * grp + 512],
                    PG[:, 512:1024]), inc="eA")
                W("scalar", lambda e, vv=vv, grp=grp, PVg=PVg: e.copy(
                    vv[:, 8 * grp:8 * grp + 8, 0:64],
                    PVg[:, :].rearrange("q (h d) -> q h d", h=8)),
                  waits=[(sv, 16 * (gt // 2))], inc="eA")
                tok[("pjev", grp)] = C["eA"]
            if gt < 2:
                W("vector", lambda e, vv=vv: e.memset(vv[:, :, 64:65],
                                                      1.0), inc="eV")
                W("vector", lambda e, vv=vv: e.memset(vv[:, :, 65:66],
                                                      0.0), inc="eV")
                tok[("vones", p)] = C["eV"]
            tok[("pjevt", gt)] = C["eA"]
            dma("gpsimd", v_sp[s, t * 128:(t + 1) * 128, :],
                vt_st[p][:, :], sv,
                waits=[("eA", C["eA"]), ("eV", tok.get(("vones", p), 0))])

        def p1b(gt):
            s, t = gt // NTS, gt % NTS
            p = gt % 2
            q4 = gt % 4
            pjev = tok[("pjevt", gt)]
            csr = tok[("csr", gt)]
            scs = SCS[q4]
            qe = qr_st[p].rearrange("q (h two i) -> q h two i", two=2, i=32)
            se_ = qt_st[q4].rearrange("q (h two i) -> q h two i", two=2, i=32)
            cs_ = cosb[q4][:, 0:512].rearrange("q (h i) -> q h i", h=NH)
            sn_ = cosb[q4][:, 512:1024].rearrange("q (h i) -> q h i", h=NH)
            rh = lambda a: a.rearrange("q (h i) -> q h i", h=NH)
            sq_ = "sqsA" if p == 0 else "sqsB"
            W("vector", lambda e, se_=se_, cs_=cs_: e.tensor_mul(
                rh(rt[0][:, :]), se_[:, :, 0, :], cs_),
              waits=[("eA", pjev), (scs, csr), ("eV", wg_split),
                     ("eT", wgT_tp),
                     ("eV", tok.get(("ropeqt", gt - 1), 0))], inc="eV")
            W("vector", lambda e, se_=se_, sn_=sn_: e.tensor_mul(
                rh(rt[1][:, :]), se_[:, :, 1, :], sn_), inc="eV")
            W("vector", lambda e, se_=se_, sn_=sn_: e.tensor_mul(
                rh(rt[2][:, :]), se_[:, :, 0, :], sn_), inc="eV")
            W("vector", lambda e, se_=se_, cs_=cs_: e.tensor_mul(
                rh(rt[3][:, :]), se_[:, :, 1, :], cs_), inc="eV")
            mv = C["eV"]
            W("vector", lambda e, qe=qe: e.tensor_sub(
                qe[:, :, 0, :], rh(rt[0][:, :]), rh(rt[1][:, :])),
              waits=[("eV", mv), (sq_, 16 * (gt // 2))], inc="eV")
            W("vector", lambda e, qe=qe: e.tensor_add(
                qe[:, :, 1, :], rh(rt[2][:, :]), rh(rt[3][:, :])),
              inc="eV")
            ropeq = C["eV"]
            tok[("ropeqt", gt)] = ropeq
            ke = kr_st[p].rearrange("q (h two i) -> q h two i", two=2, i=32)
            ks_ = kt_st[q4].rearrange("q (h two i) -> q h two i", two=2, i=32)
            sk_ = "sksA" if p == 0 else "sksB"
            W("gpsimd", lambda e, ks_=ks_, cs_=cs_: e.tensor_mul(
                rh(pt[0][:, :]), ks_[:, :, 0, :], cs_),
              waits=[("eA", pjev), (scs, csr), ("eV", wg_split),
                     ("eT", wgT_tp),
                     ("eP", tok.get(("ropekt", gt - 1), 0))], inc="eP")
            W("gpsimd", lambda e, ks_=ks_, sn_=sn_: e.tensor_mul(
                rh(pt[1][:, :]), ks_[:, :, 1, :], sn_), inc="eP")
            mp = C["eP"]
            W("gpsimd", lambda e, ke=ke: e.tensor_sub(
                ke[:, :, 0, :], rh(pt[0][:, :]), rh(pt[1][:, :])),
              waits=[("eP", mp), (sk_, 16 * (gt // 2))], inc="eP")
            W("gpsimd", lambda e, ks_=ks_, sn_=sn_: e.tensor_mul(
                rh(pt[0][:, :]), ks_[:, :, 0, :], sn_),
              waits=[("eP", C["eP"])], inc="eP")
            W("gpsimd", lambda e, ks_=ks_, cs_=cs_: e.tensor_mul(
                rh(pt[1][:, :]), ks_[:, :, 1, :], cs_), inc="eP")
            mp = C["eP"]
            W("gpsimd", lambda e, ke=ke: e.tensor_add(
                ke[:, :, 1, :], rh(pt[0][:, :]), rh(pt[1][:, :])),
              waits=[("eP", mp)], inc="eP")
            ropek = C["eP"]
            tok[("ropekt", gt)] = ropek
            dma("sync", q_sp[s, t * 128:(t + 1) * 128, :],
                qr_st[p][:, :], sq_, waits=[("eV", ropeq)])
            dma("gpsimd", k_sp[s, t * 128:(t + 1) * 128, :],
                kr_st[p][:, :], sk_, waits=[("eP", ropek)])

        ln_s0(0)
        ln_s0(1)
        ln_s0(2)
        for it in range(NT + 8):
            if it < NT:
                ln_s1(it)
            if 5 <= it < NT + 5:
                ln_s6(it - 5)
            if 1 <= it < NT + 1:
                ln_s2(it - 1)
            if it + 3 < NT:
                ln_s0(it + 3)
            if 2 <= it < NT + 2:
                ln_s3(it - 2)
            if 3 <= it < NT + 3:
                ln_s4(it - 3)
            if 4 <= it < NT + 4:
                ln_s5(it - 4)
            if 12 <= it < 12 + 2 * NE and (it - 12) % 2 == 0:
                ei = (it - 12) // 2
                dma("gpsimd", we16[ei, :, :], we_h[ei, :, :], "swe")
            if it == 12 + 2 * NE:
                dma("gpsimd", be16s[0:8, :], be_h[:, :], "sbe")
            if it >= 8:
                p1b(it - 8)
            if 6 <= it < NT + 6:
                p1a(it - 6)

        for t in range(NT):
            for c in range(NCH):
                for s3 in range(3):
                    W("tensor",
                      lambda e, t=t, c=c, s3=s3: e.matmul(
                          PC6[:, t * 8:(t + 1) * 8],
                          xnT[:, c * T + t * 128:c * T + (t + 1) * 128],
                          wgT[:, c * 24 + s3 * 8:c * 24 + s3 * 8 + 8],
                          start=(c == 0 and s3 == 0),
                          stop=(c == NCH - 1 and s3 == 2)),
                      waits=[("eA", max(C["eA"] if t == 0 and c == 0 and s3 == 0 else 0, wgT_done))], inc="eT")
        W("vector", lambda e: e.tensor_copy(lgall[:, :], PC6[:, 0:256]),
          waits=[("eT", C["eT"]), ("eV", C["eV"])], inc="eV")
        d3 = lambda a: a.rearrange("q (t e) -> q t e", e=8)
        bc = lambda a: a.unsqueeze(2).broadcast_to([128, NT, 8])
        W("vector", lambda e: e.tensor_copy(lgall[:, :], PC6[:, 0:256]),
          waits=[("eT", C["eT"]), ("eV", C["eV"])], inc="eV")
        W("vector", lambda e: e.reduce_max(v1a[:, :], d3(lgall), axis=AX.X),
          waits=[("eV", C["eV"])], inc="eV")
        W("vector", lambda e: e.tensor_tensor(
            d3(eq16[:, 0:256]), d3(lgall), bc(v1a[:, :]), op=ALU.is_equal),
          waits=[("eV", C["eV"])], inc="eV")
        W("vector", lambda e: e.scalar_tensor_tensor(
            l2a[:, :], eq16[:, 0:256], -1e30, lgall[:, :],
            op0=ALU.mult, op1=ALU.add),
          waits=[("eV", C["eV"])], inc="eV")
        W("vector", lambda e: e.reduce_max(v2a[:, :], d3(l2a), axis=AX.X),
          waits=[("eV", C["eV"])], inc="eV")
        W("vector", lambda e: e.tensor_tensor(
            d3(eq16[:, 256:512]), d3(l2a), bc(v2a[:, :]), op=ALU.is_equal),
          waits=[("eV", C["eV"])], inc="eV")
        W("vector", lambda e: e.tensor_sub(dst[:, :], v2a[:, :], v1a[:, :]),
          waits=[("eV", C["eV"])], inc="eV")

        W("scalar", lambda e: e.activation(edt[:, :], dst[:, :], AF.Exp),
          waits=[("eV", C["eV"])], inc="eA")
        W("vector", lambda e: e.tensor_scalar(dent[:, :], edt[:, :], 1.0,
                                              None, op0=ALU.add),
          waits=[("eA", C["eA"])], inc="eV")
        W("vector", lambda e: e.reciprocal(w1t[:, :], dent[:, :]),
          waits=[("eV", C["eV"])], inc="eV")
        W("vector", lambda e: e.tensor_mul(w2t[:, :], edt[:, :], w1t[:, :]),
          waits=[("eV", C["eV"])], inc="eV")
        W("vector", lambda e: e.tensor_copy(w116[:, :], w1t[:, :]),
          waits=[("eV", C["eV"])], inc="eV")
        W("vector", lambda e: e.tensor_copy(w216[:, :], w2t[:, :]),
          waits=[("eV", C["eV"])], inc="eV")
        W("vector", lambda e: e.tensor_copy(w132[:, :], w116[:, :]),
          waits=[("eV", C["eV"])], inc="eV")
        W("vector", lambda e: e.tensor_copy(w232[:, :], w216[:, :]),
          waits=[("eV", C["eV"])], inc="eV")
        for t in range(NT):
            W("vector", lambda e, t=t: e.tensor_scalar(
                tmp8[:, :], eq16[:, 256 + t * 8:256 + (t + 1) * 8],
                w232[:, t:t + 1], None, op0=ALU.mult),
              waits=[("eV", C["eV"])], inc="eV")
            W("vector", lambda e, t=t: e.scalar_tensor_tensor(
                wm32[:, t * 8:(t + 1) * 8], eq16[:, t * 8:(t + 1) * 8],
                w132[:, t:t + 1], tmp8[:, :], op0=ALU.mult, op1=ALU.add),
              waits=[("eV", C["eV"])], inc="eV")
            W("vector", lambda e, t=t: e.tensor_copy(
                wm16[:, t * 8:(t + 1) * 8], wm32[:, t * 8:(t + 1) * 8]),
              waits=[("eV", C["eV"])], inc="eV")
            W("vector", lambda e, t=t: e.tensor_copy(
                wm32[:, t * 8:(t + 1) * 8], wm16[:, t * 8:(t + 1) * 8]),
              waits=[("eV", C["eV"])], inc="eV")
            W("tensor", lambda e, t=t: e.transpose(
                pb7f16[0:8, 0:128], wm16[:, t * 8:(t + 1) * 8], idn[:, :]),
              waits=[("eV", C["eV"]), ("eA", C["eA"])], inc="eT")
            W("scalar", lambda e, t=t: e.copy(
                wmT[0:8, t * 128:(t + 1) * 128], pb7f16[0:8, 0:128]),
              waits=[("eT", C["eT"])], inc="eA")

        P.barrier()

        P.barrier()
        spw = [("sqsA", C["sqsA"]), ("sqsB", C["sqsB"]),
               ("sksA", C["sksA"]), ("sksB", C["sksB"]),
               ("svsA", C["svsA"]), ("svsB", C["svsB"])]
        W("scalar", None, waits=spw)
        W("vector", None, waits=spw)

        bew = C["sbe"]
        wew = C["swe"]
        slot_free = {}
        for q4 in range(4):
            for t8 in range(8):
                gt = q4 * 8 + t8
                for half in range(2):
                    W("tensor", lambda e, gt=gt, half=half: e.matmul(
                        PC6[:, :], wmT[0:8, gt * 128:(gt + 1) * 128],
                        be16s[0:8, half * 512:(half + 1) * 512],
                        start=True, stop=True),
                      waits=[("sbe", bew), ("eA", C["eA"])], inc="eT")
                    W("scalar", lambda e, t8=t8, half=half: e.copy(
                        acc32[t8][:, half * 512:(half + 1) * 512],
                        PC6[:, :]),
                      waits=[("eT", C["eT"]),
                             ("eA", tok.get(("accfree", t8), 0))], inc="eA")
            bias_done = C["eA"]
            for eg in range(4):
                e0, e1 = 2 * eg, 2 * eg + 1
                sp = eg % 2
                slots = (weT[2 * sp], weT[2 * sp + 1])
                swx = "sweA" if sp == 0 else "sweB"
                ldw = P.bw() + [("swe", wew), ("eT", slot_free.get(sp, 0)),
                                ("scsA", C["scsA"]), ("scsB", C["scsB"])]
                for (sl, ee) in ((slots[0], e0), (slots[1], e1)):
                    for c in range(NCH):
                        dma("sync", sl[:, c * D:(c + 1) * D],
                            we16[ee, :, c * 128:(c + 1) * 128], swx,
                            transpose=True, waits=ldw)
                        ldw = []
                wrd = C[swx]
                for t8 in range(8):
                    gt = q4 * 8 + t8
                    for half in range(2):
                        bk = (t8 * 2 + half) % 2
                        PG = PA if bk == 0 else PB
                        for c in range(NCH):
                            lhs = xnT[:, c * T + gt * 128:
                                      c * T + (gt + 1) * 128]
                            for ib in range(2):
                                wts = [(swx, wrd)]
                                if c == 0:
                                    wts.append(
                                        ("eV", tok.get(("pbfree", 2 * bk + ib),
                                                       0)))
                                    if eg == 0 and ib == 0:
                                        wts.append(("eA", bias_done))
                                W("tensor",
                                  lambda e, ib=ib, lhs=lhs, c=c, PG=PG,
                                  sl=slots[ib], half=half: e.matmul(
                                      PG[:, 512 * ib:512 * ib + 512], lhs,
                                      sl[:, c * D + half * 512:
                                         c * D + (half + 1) * 512],
                                      start=(c == 0), stop=(c == NCH - 1)),
                                  waits=wts, inc="eT")
                        mm = C["eT"]
                        for ib, ee in ((0, e0), (1, e1)):
                            W("vector",
                              lambda e, ib=ib, t8=t8, half=half, PG=PG,
                              gt=gt, ee=ee: e.scalar_tensor_tensor(
                                  acc32[t8][:, half * 512:(half + 1) * 512],
                                  PG[:, 512 * ib:512 * ib + 512],
                                  wm32[:, gt * 8 + ee:gt * 8 + ee + 1],
                                  acc32[t8][:, half * 512:(half + 1) * 512],
                                  op0=ALU.mult, op1=ALU.add),
                              waits=[("eT", mm), ("eA", bias_done),
                                     ("eV", C["eV"])], inc="eV")
                            tok[("pbfree", 2 * bk + ib)] = C["eV"]
                slot_free[sp] = C["eT"]
            for t8 in range(8):
                gt = q4 * 8 + t8
                p = gt % 2
                sg_ = "sgsA" if p == 0 else "sgsB"
                W("scalar", lambda e, t8=t8, p=p: e.activation(
                    g_st[p][:, :], acc32[t8][:, :], AF.Silu),
                  waits=[("eV", C["eV"]), (sg_, 16 * (gt // 2))], inc="eA")
                tok[("accfree", t8)] = C["eA"]
                dma("sync", g_sp[gt * 128:(gt + 1) * 128, :], g_st[p][:, :],
                    sg_, waits=[("eA", C["eA"])])

        P.barrier()
        gw_ = [("sgsA", C["sgsA"]), ("sgsB", C["sgsB"])]
        W("scalar", None, waits=gw_)
        W("vector", None, waits=gw_)

        dma("sync", ow128[:, :].rearrange("k (c n) -> k c n", c=NCH),
            ow_h.rearrange("(c k) n -> k c n", k=128), "sow",
            waits=P.bw() + gw_ + spw)
        oww = C["sow"]

        for s in range(NSEQ):
            kw_ = P.bw() + [("sksA", C["sksA"]), ("sksB", C["sksB"]),
                            ("eT", tok.get("kT_free", 0))]
            for c in range(NCH):
                dma("sync", c16(AR1, c * 4096, 2048)[:, :],
                    k_sp[s, :, c * 128:(c + 1) * 128], "skT", transpose=True,
                    waits=kw_)
                kw_ = []
            ktr = C["skT"]
            dma("gpsimd", v_sb[:, :].rearrange("q (t f) -> q t f", t=NTS),
                v_sp[s].rearrange("(t q) f -> q t f", q=128), "svld",
                waits=[("svsA", C["svsA"]), ("svsB", C["svsB"]),
                       ("eT", tok.get("v_free", 0))])
            vld = C["svld"]
            qw_ = P.bw() + [("sqsA", C["sqsA"]), ("sqsB", C["sqsB"]),
                            ("eT", tok.get(("qT_free", 0), 0))]
            for c in range(NCH):
                dma("sync", qT[0][:, c * 512:(c + 1) * 512],
                    q_sp[s, 0:512, c * 128:(c + 1) * 128], "sqTA",
                    transpose=True, waits=qw_)
                qw_ = []

            for qg in range(NQG):
                pbuf = qg % 2
                sqT = "sqTA" if pbuf == 0 else "sqTB"
                qrd = C[sqT]
                if qg + 1 < NQG:
                    nb = (qg + 1) % 2
                    snq = "sqTA" if nb == 0 else "sqTB"
                    qw_ = [("eT", tok.get(("qT_free", nb), 0))]
                    for c in range(NCH):
                        dma("sync", qT[nb][:, c * 512:(c + 1) * 512],
                            q_sp[s, (qg + 1) * 512:(qg + 2) * 512,
                                 c * 128:(c + 1) * 128], snq,
                            transpose=True, waits=qw_)
                        qw_ = []
                r0 = s * S + qg * 512
                dma("gpsimd", gq[:, :].rearrange("q (t f) -> q t f", t=4),
                    g_sp[r0:r0 + 512, :].rearrange("(t q) f -> q t f", q=128),
                    "sgq", waits=[("sgsA", C["sgsA"]), ("sgsB", C["sgsB"]),
                                  ("eV", tok.get("gq_free", 0))])
                dma("gpsimd", xq[:, :].rearrange("q (t f) -> q t f", t=4),
                    x_h[r0:r0 + 512, :].rearrange("(t q) f -> q t f", q=128),
                    "sxq", waits=[("eV", tok.get("xq_free", 0))])
                gqr, xqr = C["sgq"], C["sxq"]

                nkc = 4 * qg + 4

                def emit_pv(h, kc):
                    p2 = h % 2
                    j = kc - 4 * qg
                    sbk = kc % 3
                    w0 = max(0, j) * 128
                    mw = 65 if h == NH - 1 else 128
                    pvw = [("eA", tok.get(("expC", kc), 0)),
                           ("svld", vld)]
                    if j >= 0:
                        pvw.append(("eP", tok.get(("maskdone", sbk), 0)))
                    if kc == 0:
                        pvw.append(("eA", tok.get("pvrecip", 0)))
                        pvw.append(("eV", tok.get("pvevac", 0)))
                    W("tensor", lambda e, kc=kc, h=h, sbk=sbk, w0=w0,
                      nkc=nkc, p2=p2, mw=mw: e.matmul(
                          PV2[0:mw, 512 * p2 + w0:512 * p2 + 512],
                          v_sb[:, kc * VF + h * VW:kc * VF + h * VW + mw],
                          expb3[sbk][:, 512 * p2 + w0:512 * p2 + 512],
                          start=(kc == 0), stop=(kc == nkc - 1)),
                      waits=pvw, inc="eT")
                    tok[("exprd", sbk)] = C["eT"]

                def emit_tail_a(hp):
                    pvmm = tok[("pvmm", hp)]
                    W("scalar", lambda e: e.activation(
                        recL[64:65, 0:1024], PV2[64:65, 0:1024], AF.Ln),
                      waits=[("eT", pvmm), ("eA", C["eA"]),
                             ("eA", tok.get("lnrd", 0))], inc="eA")
                    W("scalar", lambda e: e.activation(
                        recE[64:65, 0:1024], recL[64:65, 0:1024],
                        AF.Exp, scale=-1.0),
                      waits=[("eA", C["eA"]),
                             ("eT", tok.get("rec16rd", 0))], inc="eA")
                    tok["pvrecip"] = C["eA"]
                    tok[("recv", hp)] = C["eA"]
                    tok["lnrd"] = C["eA"]
                    W("vector", lambda e: e.tensor_copy(
                        at2[0:64, :], PV2[0:64, 0:1024]),
                      waits=[("eT", pvmm), ("eV", tok.get("atrd", 0))],
                      inc="eV")
                    tok["pvevac"] = C["eV"]
                    tok[("atv", hp)] = C["eV"]

                def emit_tail_b(hp):
                    h0_, h1_ = 2 * hp, 2 * hp + 1
                    W("tensor", lambda e: e.matmul(
                        PC6[0:64, :], ones16[64:65, 0:64],
                        recE[64:65, 0:512], start=True, stop=True),
                      waits=[("eA", tok[("recv", hp)]),
                             ("eV", tok.get("bcrd", 0))], inc="eT")
                    W("tensor", lambda e: e.matmul(
                        PC7[0:64, :], ones16[64:65, 0:64],
                        recE[64:65, 512:1024], start=True, stop=True),
                      inc="eT")
                    tok["rec16rd"] = C["eT"]
                    bcmm = C["eT"]
                    W("vector", lambda e, hp=hp: e.tensor_mul(
                        attnT[0:64, hp * 512:(hp + 1) * 512],
                        at2[0:64, 0:512], PC6[0:64, :]),
                      waits=[("eV", tok[("atv", hp)]), ("eT", bcmm),
                             ("eT", tok.get("atT_rd", 0)), ("eV", C["eV"])],
                      inc="eV")
                    W("vector", lambda e, hp=hp: e.tensor_mul(
                        attnT[64:128, hp * 512:(hp + 1) * 512],
                        at2[0:64, 512:1024], PC7[0:64, :]),
                      inc="eV")
                    tok["atrd"] = C["eV"]
                    tok["bcrd"] = C["eV"]

                def emit_score_pair(hp, kc):
                    ch = hp
                    bk = kc % 2
                    sbk = kc % 3
                    PS = PA if bk == 0 else PB
                    j = kc - 4 * qg
                    w0 = max(0, j) * 128
                    for p2 in range(2):
                        base = p2 * 64
                        wts = [("skT", ktr), (sqT, qrd)]
                        if p2 == 0:
                            wts.append(("eA", tok.get(("expdone", bk), 0)))
                            if bk == 0:
                                wts.append(("eV", tok.get(("ofree", 0), 0)))
                                wts.append(("eV", tok.get(("ofree", 1), 0)))
                        W("tensor", lambda e, base=base, ch=ch, kc=kc,
                          pbuf=pbuf, PS=PS, p2=p2: e.matmul(
                              PS[:, 512 * p2:512 * p2 + 512],
                              kT[base:base + 64,
                                 ch * S + kc * 128:ch * S + (kc + 1) * 128],
                              qT[pbuf][base:base + 64,
                                       ch * 512:(ch + 1) * 512],
                              start=True, stop=True),
                          waits=wts, inc="eT")
                    ev = (lambda a: a.rearrange("q (hh c) -> q hh c", hh=2)
                          [:, :, w0:512])
                    W("scalar", lambda e, PS=PS, sbk=sbk, ev=ev: e.activation(
                        ev(expb3[sbk]), ev(PS), AF.Exp, scale=1.0 / 32.0),
                      waits=[("eT", C["eT"]),
                             ("eT", tok.get(("exprd", sbk), 0))],
                      inc="eA")
                    tok[("expdone", bk)] = C["eA"]
                    tok[("expC", kc)] = C["eA"]
                    if j >= 0:
                        tv = (lambda a: a.rearrange("q (hh c) -> q hh c",
                                                    hh=2)
                              [:, :, j * 128:(j + 1) * 128])
                        W("gpsimd", lambda e, sbk=sbk, tv=tv: e.tensor_mul(
                            tv(expb3[sbk]), tv(expb3[sbk]),
                            tri[:, :].unsqueeze(1).broadcast_to(
                                [128, 2, 128])),
                          waits=[("eA", C["eA"]), ("scst", cst),
                                 ("eP", C["eP"])], inc="eP")
                        tok[("maskdone", sbk)] = C["eP"]

                for hp in range(NH // 2):
                    h0, h1 = 2 * hp, 2 * hp + 1
                    for kc in range(nkc):
                        emit_score_pair(hp, kc)
                        if kc >= 2:
                            emit_pv(h0, kc - 2)
                            emit_pv(h1, kc - 2)
                    emit_pv(h0, nkc - 2)
                    emit_pv(h1, nkc - 2)
                    emit_pv(h0, nkc - 1)
                    emit_pv(h1, nkc - 1)
                    tok[("pvmm", hp)] = C["eT"]
                    if hp >= 1:
                        emit_tail_b(hp - 1)
                    emit_tail_a(hp)
                emit_tail_b(NH // 2 - 1)
                heads_done = C["eV"]

                for jt in range(4):
                    for half in range(2):
                        for cp in range(NCH):
                            wts = [("sow", oww), ("eV", heads_done)]
                            if cp == 0:
                                wts.append(("eV",
                                            tok.get(("ofree", half), 0)))
                            W("tensor", lambda e, jt=jt, half=half, cp=cp:
                              e.matmul(
                                  PA[:, 512 * half:512 * half + 512],
                                  attnT[0:128, cp * 512 + jt * 128:
                                        cp * 512 + (jt + 1) * 128],
                                  ow128[:, cp * D + half * 512:
                                        cp * D + (half + 1) * 512],
                                  start=(cp == 0), stop=(cp == NCH - 1)),
                              waits=wts, inc="eT")
                        omm = C["eT"]
                        W("vector", lambda e, jt=jt, half=half: e.tensor_mul(
                            outq[:, jt * D + half * 512:
                                 jt * D + (half + 1) * 512],
                            PA[:, 512 * half:512 * half + 512],
                            gq[:, jt * D + half * 512:
                               jt * D + (half + 1) * 512]),
                          waits=[("eT", omm), ("sgq", gqr),
                                 ("soA", tok.get("outA", 0)),
                                 ("soB", tok.get("outB", 0)),
                                 ("eV", C["eV"])], inc="eV")
                        W("vector", lambda e, jt=jt, half=half: e.tensor_add(
                            outq[:, jt * D + half * 512:
                                 jt * D + (half + 1) * 512],
                            outq[:, jt * D + half * 512:
                                 jt * D + (half + 1) * 512],
                            xq[:, jt * D + half * 512:
                               jt * D + (half + 1) * 512]),
                          waits=[("sxq", xqr), ("eV", C["eV"])], inc="eV")
                        tok[("ofree", half)] = C["eV"]
                tok["atT_rd"] = C["eT"]
                comb = C["eV"]
                tok["gq_free"] = comb
                tok["xq_free"] = comb
                so = "soA" if (s * NQG + qg) % 2 == 0 else "soB"
                dma("sync",
                    out_h[r0:r0 + 512, :].rearrange("(t q) f -> q t f",
                                                    q=128),
                    outq[:, :].rearrange("q (t f) -> q t f", t=4), so,
                    waits=[("eV", comb)])
                tok["outA" if so == "soA" else "outB"] = C[so]
            tok["kT_free"] = C["eT"]
            tok["v_free"] = C["eT"]
            if s == 0:
                P.barrier()

        W("scalar", None, waits=[("soA", C["soA"]), ("soB", C["soB"])])
        P.barrier()

        with nc.Block() as blk:
            P.replay(blk)
    return nc


_CACHE = {}


def _constants():
    if "c" in _CACHE:
        return _CACHE["c"]
    inv = 1.0 / (10000.0 ** (np.arange(0, HD, 2, dtype=np.float64) / HD))
    pos = np.arange(S, dtype=np.float64)
    fr = pos[:, None] * inv[None, :]
    cos = np.tile(np.cos(fr), (1, NH)).astype(np.float16)
    sin = np.tile(np.sin(fr), (1, NH)).astype(np.float16)
    tri = np.triu(np.ones((128, 128), dtype=np.float16))
    idn = np.eye(128, dtype=np.float16)
    _CACHE["c"] = (cos, sin, tri, idn)
    return _CACHE["c"]


def kernel(L0_attn_x, L0_attn_qw, L0_attn_kw, L0_attn_vw, L0_attn_ow,
           L0_ffn_Wg, L0_ffn_We, L0_ffn_be):
    x = np.ascontiguousarray(np.asarray(L0_attn_x, dtype=np.float16))
    qw = np.ascontiguousarray(np.asarray(L0_attn_qw, dtype=np.float16))
    kw = np.ascontiguousarray(np.asarray(L0_attn_kw, dtype=np.float16))
    vw = np.ascontiguousarray(np.asarray(L0_attn_vw, dtype=np.float16))
    ow = np.ascontiguousarray(np.asarray(L0_attn_ow, dtype=np.float16))
    wg = np.ascontiguousarray(np.asarray(L0_ffn_Wg, dtype=np.float32))
    we = np.ascontiguousarray(np.asarray(L0_ffn_We, dtype=np.float32))
    be = np.ascontiguousarray(np.asarray(L0_ffn_be, dtype=np.float32))
    cos, sin, tri, idn = _constants()

    nc = build_nc()
    in_maps = []
    for i in range(NCORE):
        in_maps.append({
            "x_c": np.ascontiguousarray(
                x[2 * i:2 * i + 2].reshape(T, D)),
            "qw": qw, "kw": kw, "vw": vw, "ow": ow,
            "wg": wg, "we": we, "be": be,
            "c_cos": cos, "c_sin": sin, "c_tri": tri, "c_idn": idn,
        })
    res = run_bass_kernel_spmd(nc, in_maps, core_ids=list(range(NCORE)))
    out = np.concatenate(
        [np.asarray(r["out_c"]) for r in res.results], axis=0)
    return out.reshape(16, S, D).astype(np.float32, copy=False)
